# version 09
# baseline (speedup 1.0000x reference)
import numpy as np
import ml_dtypes
import concourse.bass as bass
import concourse.mybir as mybir
from concourse.bass_utils import run_bass_kernel_spmd
from contextlib import ExitStack

F32 = mybir.dt.float32
BF16 = mybir.dt.bfloat16
I32 = mybir.dt.int32
AF = mybir.ActivationFunctionType
OP = mybir.AluOpType
AX = mybir.AxisListType

B, S, E, D, NHC, II = 4, 2048, 4096, 128, 16, 2048
KC = E // 128
IKC = II // 128
ISCALE = float(1.0 / np.sqrt(128.0))
ENGINES = ("sync", "gpsimd", "scalar", "vector", "tensor")
DBG = False
FULL_WAITS = False
PSUM_EXCL = True

_IS = {F32: 4, BF16: 2, I32: 4}


class BV:

    def __init__(self, ap, key, lo, hi, dt):
        self.ap, self.key, self.lo, self.hi, self.dt = ap, key, lo, hi, dt

    def c(self, a, b):
        isz = _IS[self.dt]
        return BV(self.ap[:, a:b], self.key, self.lo + a * isz, self.lo + b * isz,
                  self.dt)

    def p(self, a, b):
        return BV(self.ap[a:b], self.key, self.lo, self.hi, self.dt)


class Tracker:
    def __init__(self, st, nc):
        self.nc = nc
        self.q = {e: [] for e in ENGINES}
        self.esem = {e: st.enter_context(nc.semaphore("es_" + e)) for e in ENGINES}
        self.NDQ = 8
        self.dsem = {e: [st.enter_context(nc.semaphore(f"ds_{e}{i}"))
                         for i in range(self.NDQ)]
                     for e in ("sync", "gpsimd", "scalar")}
        self.ccsem = st.enter_context(nc.semaphore("ccsem"))
        self.cnt = {e: 0 for e in ENGINES}
        self.dcnt = {e: [0] * self.NDQ for e in ("sync", "gpsimd", "scalar")}
        self.didx = {e: 0 for e in ("sync", "gpsimd", "scalar")}
        self.ccnt = 0
        self.reg = {}
        self.sat = {e: {} for e in ENGINES}
        self.snap = {e: {} for e in ENGINES}
        self.semobj = {}

    def _semname(self, sem):
        for e, s in self.esem.items():
            if s is sem:
                return "e_" + e
        for e, ss in self.dsem.items():
            for i, s in enumerate(ss):
                if s is sem:
                    return f"d_{e}{i}"
        return "cc"

    def _sid(self, sem):
        i = id(sem)
        self.semobj[i] = sem
        return i

    def _hazards(self, eng, reads, writes):
        need = {}

        def add(marker):
            if marker is None:
                return
            sem, val, owner = marker
            if owner == eng == "tensor" and sem is self.esem[eng]:
                return
            sid = self._sid(sem)
            if FULL_WAITS:
                if need.get(sid, (None, 0))[1] < val:
                    need[sid] = (sem, val)
                return
            if self.sat[eng].get(sid, 0) >= val:
                return
            if need.get(sid, (None, 0))[1] < val:
                need[sid] = (sem, val)
        for bv in reads:
            for iv in self.reg.get(bv.key, []):
                if iv[0] < bv.hi and bv.lo < iv[1]:
                    add(iv[2])
        for bv in writes:
            for iv in self.reg.get(bv.key, []):
                if iv[0] < bv.hi and bv.lo < iv[1]:
                    add(iv[2])
                    for r in iv[3]:
                        add(r)
        return [v for _, v in need.items()]

    def _record(self, eng, reads, writes, marker):
        for bv in reads:
            lst = self.reg.setdefault(bv.key, [])
            hit = False
            for iv in lst:
                if iv[0] == bv.lo and iv[1] == bv.hi:
                    iv[3].append(marker)
                    hit = True
                    break
            if not hit:
                lst.append([bv.lo, bv.hi, None, [marker]])
        for bv in writes:
            lst = self.reg.setdefault(bv.key, [])
            keep = []
            for iv in lst:
                if iv[0] < bv.hi and bv.lo < iv[1]:
                    if iv[0] < bv.lo:
                        keep.append([iv[0], bv.lo, iv[2], list(iv[3])])
                    if bv.hi < iv[1]:
                        keep.append([bv.hi, iv[1], iv[2], list(iv[3])])
                else:
                    keep.append(iv)
            keep.append([bv.lo, bv.hi, marker, []])
            self.reg[bv.key] = keep

    def op(self, eng, f, reads=(), writes=(), kind="compute", tag=None):
        if PSUM_EXCL:
            psr = [b for b in reads if b.key.startswith("pb")]
            if psr:
                reads = [b for b in reads if not b.key.startswith("pb")]
                writes = list(writes) + psr
        waits = self._hazards(eng, reads, writes)
        if kind == "compute":
            sem = self.esem[eng]
            self.cnt[eng] += 1
            val, k = self.cnt[eng], 1
        elif kind == "dma":
            slot = self.didx[eng] % self.NDQ
            self.didx[eng] += 1
            sem = self.dsem[eng][slot]
            prior = self.dcnt[eng][slot]
            if prior:
                sid = self._sid(sem)
                if self.sat[eng].get(sid, 0) < prior and not any(
                        s is sem and v >= prior for s, v in waits):
                    waits = list(waits) + [(sem, prior)]
            self.dcnt[eng][slot] = prior + 16
            val, k = prior + 16, 16
        else:
            sem = self.ccsem
            self.ccnt += 1
            val, k = self.ccnt, 1
        marker = (sem, val, eng)
        if not hasattr(self, "log"):
            self.log = []
        self.log.append((eng, kind, val, tag,
                         [(self._semname(s), v) for s, v in waits],
                         [(b.key, b.lo, b.hi) for b in reads],
                         [(b.key, b.lo, b.hi) for b in writes],
                         self._semname(sem)))
        self.q[eng].append((f, tuple(waits), sem, k, val, kind))
        for s, v in waits:
            sid = self._sid(s)
            self.sat[eng][sid] = max(self.sat[eng].get(sid, 0), v)
            for oe in ENGINES:
                if s is self.esem[oe]:
                    snapd = self.snap[oe].get(v)
                    if snapd:
                        for k2, v2 in snapd.items():
                            self.sat[eng][k2] = max(self.sat[eng].get(k2, 0), v2)
        if kind == "compute":
            self.snap[eng][val] = dict(self.sat[eng])
            self.snap[eng][val][self._sid(sem)] = val
        self._record(eng, reads, writes, marker)
        return marker

    def finalize(self):
        waited = {self._sid(s): set() for s in self.esem.values()}
        for eng in ENGINES:
            for (f, waits, sem, k, val, kind) in self.q[eng]:
                for s, v in waits:
                    sid = self._sid(s)
                    if sid in waited:
                        waited[sid].add(v)
        remap = {}
        newq = {e: [] for e in ENGINES}
        for eng in ENGINES:
            sid = self._sid(self.esem[eng])
            compact = 0
            for (f, waits, sem, k, val, kind) in self.q[eng]:
                inc = kind != "wait"
                if kind == "compute":
                    if val in waited[sid]:
                        compact += 1
                        remap[(sid, val)] = compact
                    else:
                        inc = False
                newq[eng].append((f, waits, sem, k, inc))
        out = {e: [] for e in ENGINES}
        for eng in ENGINES:
            for (f, waits, sem, k, inc) in newq[eng]:
                w2 = []
                for s, v in waits:
                    sid = self._sid(s)
                    if (sid, v) in remap:
                        w2.append((s, remap[(sid, v)]))
                    elif sid in waited:
                        assert False, f"wait on un-inc'd marker {v}"
                    else:
                        w2.append((s, v))

                def fn(e, f=f, w2=tuple(w2), sem=sem, k=k, inc=inc):
                    for s, v in w2:
                        e.wait_ge(s, v)
                    if f is not None:
                        inst = f(e)
                        if inc:
                            inst.then_inc(sem, k)
                out[eng].append(fn)
        self.q = out

    def wait_all(self, eng, bvs):
        need = self._hazards(eng, bvs, bvs)
        for s, v in need:
            self.q[eng].append((None, ((s, v),), None, 0, 0, "wait"))
            self.sat[eng][self._sid(s)] = max(self.sat[eng].get(self._sid(s), 0), v)


class Arena:
    def __init__(self, t, nbytes, tr):
        self.t, self.n, self.off, self.stack = t, nbytes, 0, []
        self.tr = tr
        self.nalloc = 0

    def alloc(self, cols, dtype):
        isz = _IS[dtype]
        off = (self.off + 63) // 64 * 64
        sz = cols * isz
        assert off + sz <= self.n, f"arena overflow: {off}+{sz} > {self.n}"
        self.off = off + sz
        view = self.t[:, off // 4:(off + sz + 3) // 4]
        if dtype is not F32:
            view = view.bitcast(dtype)
        self.nalloc += 1
        return BV(view[:, 0:cols], "arena", off, off + sz, dtype)

    def push(self):
        self.stack.append(self.off)

    def pop(self):
        self.off = self.stack.pop()


def build_program():
    nc = bass.Bass("TRN2", target_bir_lowering=False, debug=False,
                   detect_race_conditions=False)
    st = ExitStack()
    tr = Tracker(st, nc)

    def din(name, shape, dt=F32):
        ap = nc.dram_tensor(name, shape, dt, kind="ExternalInput").ap()
        return ap

    xT_my = din("xT_my", [E, S])
    xT_own = din("xT_own", [E, 1024])
    wqkv = din("wqkv", [NHC, 3, 128, KC * 128])
    bqkv = din("bqkv", [128, 48])
    wmod_my = din("wmod_my", [128, 24 * 4096])
    c_my = din("c_my", [4, 512])
    e_b = din("e_b", [4, 1])
    sel0 = din("sel0", [128, 1])
    sel1 = din("sel1", [128, 1])
    ow = din("ow", [KC, 128, KC * 128])
    obT = din("obT", [128, 32])
    wg = din("wg", [IKC, 128, KC * 128])
    wu = din("wu", [IKC, 128, KC * 128])
    wd = din("wd", [KC, 128, IKC * 128])
    invgT = din("invgT", [128, 32])
    invuT = din("invuT", [128, 32])
    inviT = din("inviT", [128, 16])
    swgT = din("swgT", [128, 16])
    swuT = din("swuT", [128, 16])
    swdT = din("swdT", [128, 32])
    cos2 = din("cos2", [128, S])
    sin2s = din("sin2s", [128, S])
    trimask = din("trimask", [128, 128], BF16)
    pswap = din("pswap", [128, 128], BF16)
    ident_bf = din("ident_bf", [128, 128], BF16)
    ident_f = din("ident_f", [128, 128], F32)
    ones_bf = din("ones_bf", [128, 128], BF16)
    ones_f = din("ones_f", [1, 128], F32)
    yT_my = nc.dram_tensor("yT_my", [E, 1024], F32, kind="ExternalOutput").ap()

    dbg = {}
    if DBG:
        for nm, shp in [("d_modT", [128, 192]), ("d_h0", [128, S], ),
                        ("d_q0", [128, S]), ("d_k0", [128, S]), ("d_v0", [128, S]),
                        ("d_ctx0", [128, 512]), ("d_bias0", [128, 1]),
                        ("d_ctxown", [128, 1024]), ("d_x2", [128, 1024]),
                        ("d_h2", [128, 1024]), ("d_qg0", [128, 512]),
                        ("d_inter0", [128, 512]), ("d_sgb", [128, 512]),
                        ("d_nsq", [128, 512]), ("d_norm", [128, 512]),
                        ("d_maxq", [128, 8]), ("d_tail", [4, 128]), ("d_ball", [16, 128]), ("d_sig", [16, 128]), ("d_maxq1", [128, 8]), ("d_nsq1", [128, 512]), ("d_tail1", [4, 128]), ("d_maxq2", [128, 8]), ("d_norm2", [128, 512]), ("d_nsq2", [128, 512])]:
            dbg[nm] = nc.dram_tensor(nm, shp, F32, kind="ExternalOutput").ap()

    mod_part_t = nc.dram_tensor("mod_part", [4, 6 * E], F32)
    mod_ar_t = nc.dram_tensor("mod_ar", [4, 6 * E], F32)
    inter_dram_t = nc.dram_tensor("inter_dram", [2, IKC, 128, 512], BF16)
    ctx_send_t = nc.dram_tensor("ctx_send", [2, 2048, 1024], BF16)
    ctx_ag_t = nc.dram_tensor("ctx_ag", [2, 2, 2048, 1024], BF16)
    x2_dram_t = nc.dram_tensor("x2_dram", [KC, 128, 1024], F32)
    inter_dram = inter_dram_t.ap()
    mod_part = BV(mod_part_t.ap(), "mod_part", 0, 4 * 6 * E * 4, F32)
    mod_ar = BV(mod_ar_t.ap(), "mod_ar", 0, 4 * 6 * E * 4, F32)
    ctx_send = BV(ctx_send_t.ap(), "ctx_send", 0, 2 * 2048 * 1024 * 2, BF16)
    ctx_ag = BV(ctx_ag_t.ap(), "ctx_ag", 0, 2 * 2 * 2048 * 1024 * 2, BF16)
    x2_dram = BV(x2_dram_t.ap(), "x2_dram", 0, KC * 128 * 1024 * 4, F32)
    y_out = BV(yT_my, "y_out", 0, E * 1024 * 4, F32)

    def sbv(name, shape, dt):
        t = st.enter_context(nc.sbuf_tensor(name, shape, dt))
        return BV(t[:, :] if len(shape) == 2 else t[:], name,
                  0, shape[-1] * _IS[dt], dt)

    h_all = sbv("h_all", [128, KC * S], BF16)
    tri_sb = sbv("tri_sb", [128, 128], BF16)
    psw_sb = sbv("psw_sb", [128, 128], BF16)
    idb_sb = sbv("idb_sb", [128, 128], BF16)
    idf_sb = sbv("idf_sb", [128, 128], F32)
    onb_sb = sbv("onb_sb", [128, 128], BF16)
    onf_sb = sbv("onf_sb", [1, 128], F32)
    bq_sb = sbv("bq_sb", [128, 48], F32)
    ob_sb = sbv("ob_sb", [128, 32], F32)
    modT = sbv("modT", [128, 192], F32)
    c1a = sbv("c1a", [128, 32], F32)
    c1b = sbv("c1b", [128, 32], F32)
    ga_e = sbv("ga_e", [128, 32], F32)
    gob = sbv("gob", [128, 32], F32)
    gb_e = sbv("gb_e", [128, 32], F32)
    gbswd = sbv("gbswd", [128, 32], F32)
    ivg_sb = sbv("ivg_sb", [128, 32], F32)
    ivu_sb = sbv("ivu_sb", [128, 32], F32)
    ivi_sb = sbv("ivi_sb", [128, 16], F32)
    swg_sb = sbv("swg_sb", [128, 16], F32)
    swu_sb = sbv("swu_sb", [128, 16], F32)
    swd_sb = sbv("swd_sb", [128, 32], F32)
    sel0_sb = sbv("sel0_sb", [128, 1], F32)
    sel1_sb = sbv("sel1_sb", [128, 1], F32)
    e_sb = sbv("e_sb", [4, 1], F32)
    cT_my = sbv("cT_my", [128, 16], BF16)
    eps_sb = sbv("eps_sb", [128, 1], F32)

    AR_BYTES = 77056
    arena_t = st.enter_context(nc.sbuf_tensor("arena", [128, AR_BYTES // 4], F32))
    arena = Arena(arena_t, AR_BYTES, tr)

    pb = []
    pbb = []
    for i in range(8):
        t = st.enter_context(nc.psum_tensor(f"pb{i}", [128, 512], F32))
        pb.append(BV(t[:, :], f"pb{i}", 0, 2048, F32))
        pbb.append(BV(t[:, :].bitcast(BF16), f"pb{i}", 0, 2048, BF16))

    def dma(eng, dst, src):
        rd = [src] if isinstance(src, BV) else []
        wr = [dst] if isinstance(dst, BV) else []
        sap = src.ap if isinstance(src, BV) else src
        dap = dst.ap if isinstance(dst, BV) else dst
        return tr.op(eng, lambda e, d=dap, s=sap: e.dma_start(out=d, in_=s),
                     reads=rd, writes=wr, kind="dma")

    def dmac(dst, src):
        n = dst.ap.shape[-1]
        assert n % 2048 == 0 and src.shape[-1] == n
        dap = dst.ap.rearrange("p (a b) -> p a b", b=2048)
        sap = src.rearrange("p (a b) -> p a b", b=2048)
        return tr.op("gpsimd", lambda e, d=dap, s=sap: e.dma_start(out=d, in_=s),
                     reads=[], writes=[dst], kind="dma")

    def mm(lhs, rhs, out, start, stop, transpose=False):
        if transpose:
            f = lambda e, a=lhs.ap, b=rhs.ap, o=out.ap: e.transpose(o, a, b)
        else:
            f = lambda e, a=lhs.ap, b=rhs.ap, o=out.ap, st_=start, sp=stop: \
                e.matmul(o, a, b, start=st_, stop=sp)
        return tr.op("tensor", f, reads=[lhs, rhs], writes=[out])

    def vop(f, reads, writes, eng="vector"):
        return tr.op(eng, f, reads=reads, writes=writes)

    def tt(out, a, b, op, eng="vector"):
        return vop(lambda e, o=out.ap, x=a.ap, y=b.ap, op=op:
                   e.tensor_tensor(o, x, y, op=op), [a, b], [out], eng)

    def ts(out, a, s1, s2, op0, op1=None, eng="vector"):
        r = [a] + [s for s in (s1, s2) if isinstance(s, BV)]
        a1 = s1.ap if isinstance(s1, BV) else s1
        a2 = s2.ap if isinstance(s2, BV) else s2
        def f(e, o=out.ap, x=a.ap, a1=a1, a2=a2, op0=op0, op1=op1):
            if op1 is None:
                return e.tensor_scalar(o, x, a1, None, op0=op0)
            return e.tensor_scalar(o, x, a1, a2, op0=op0, op1=op1)
        return vop(f, r, [out], eng)

    def stt(out, a, s, b, op0, op1, eng="vector"):
        r = [a, b] + ([s] if isinstance(s, BV) else [])
        sap = s.ap if isinstance(s, BV) else s
        return vop(lambda e, o=out.ap, x=a.ap, sap=sap, y=b.ap, op0=op0, op1=op1:
                   e.scalar_tensor_tensor(o, x, sap, y, op0=op0, op1=op1),
                   r, [out], eng)

    def cp(out, a, eng="vector"):
        return vop(lambda e, o=out.ap, x=a.ap: e.tensor_copy(o, x), [a], [out], eng)

    def act(out, a, func, bias=None, scale=1.0, eng="scalar"):
        r = [a] + [x for x in (bias, scale) if isinstance(x, BV)]
        bap = bias.ap if isinstance(bias, BV) else bias
        sap = scale.ap if isinstance(scale, BV) else scale

        def f(e, o=out.ap, x=a.ap, func=func, bap=bap, sap=sap):
            kw = {}
            if bap is not None:
                kw["bias"] = bap
            return e.activation(o, x, func, scale=sap, **kw)
        return tr.op(eng, f, reads=r, writes=[out])

    def red(out, a, op, absval=False):
        def f(e, o=out.ap, x=a.ap, op=op, av=absval):
            if av:
                return e.tensor_reduce(o, x, axis=AX.X, op=op,
                                       apply_absolute_value=True)
            return e.tensor_reduce(o, x, axis=AX.X, op=op)
        return vop(f, [a], [out])

    def recip(out, a):
        return vop(lambda e, o=out.ap, x=a.ap: e.reciprocal(o, x), [a], [out])

    def mkstream(ring, src_fn, total):
        state = {"i": 0}

        def ensure(upto):
            while state["i"] < min(upto, total):
                i = state["i"]
                dst = ring[i % len(ring)]
                tr.op("gpsimd",
                      lambda e, d=dst.ap, s=src_fn(i): e.dma_start(out=d, in_=s),
                      reads=[], writes=[dst], kind="dma")
                state["i"] += 1
        return ensure

    for dst, src in [(tri_sb, trimask), (psw_sb, pswap), (idb_sb, ident_bf),
                     (idf_sb, ident_f), (onb_sb, ones_bf), (onf_sb, ones_f),
                     (bq_sb, bqkv), (ob_sb, obT), (ivg_sb, invgT), (ivu_sb, invuT),
                     (ivi_sb, inviT), (swg_sb, swgT), (swu_sb, swuT),
                     (swd_sb, swdT), (sel0_sb, sel0), (sel1_sb, sel1),
                     (e_sb, e_b)]:
        dma("sync", dst, src[:, :] if not isinstance(src, BV) else src)
    tr.op("vector", lambda e: e.memset(eps_sb.ap, 1e-5), reads=[], writes=[eps_sb])

    arena.push()
    c_f = arena.alloc(512, F32)
    silu_c = arena.alloc(512, F32)
    silu_cb = arena.alloc(512, BF16)
    wring = [arena.alloc(4 * 1024, BF16) for _ in range(4)]
    pe_sb = [arena.alloc(512, F32) for _ in range(2)]

    dma("sync", c_f.p(0, 4), c_my[:, :])
    act(silu_c.p(0, 4), c_f.p(0, 4), AF.Silu)
    cp(silu_cb.p(0, 4), silu_c.p(0, 4))
    for j in range(4):
        mm(silu_cb.p(0, 4).c(128 * j, 128 * j + 128), idb_sb.p(0, 4).c(0, 4),
           pbb[7].c(4 * j, 4 * j + 4), True, True, transpose=True)
    cp(cT_my, pbb[7].c(0, 16))

    for i in range(4):
        dmac(wring[i], wmod_my[:, 4096 * i:4096 * i + 4096])
    for i in range(24):
        buf = wring[i % 4]
        for sub in range(2):
            bank = pb[(2 * i + sub) % 2]
            for kcp in range(4):
                mm(cT_my.c(4 * kcp, 4 * kcp + 4),
                   buf.c(1024 * kcp + 512 * sub, 1024 * kcp + 512 * sub + 512),
                   bank.p(0, 4), kcp == 0, kcp == 3)
            peb = pe_sb[(2 * i + sub) % 2]
            cp(peb.p(0, 4), bank.p(0, 4))
            dma("sync", mod_part.c(1024 * i + 512 * sub, 1024 * i + 512 * sub + 512),
                peb.p(0, 4))
        if i + 4 < 24:
            dmac(wring[(i + 4) % 4],
                 wmod_my[:, 4096 * (i + 4):4096 * (i + 4) + 4096])

    tr.op("gpsimd",
          lambda e: e.collective_compute(
              "AllReduce", OP.add, replica_groups=[list(range(8))],
              ins=[mod_part.ap], outs=[mod_ar.ap]),
          reads=[mod_part], writes=[mod_ar], kind="cc")
    arena.pop()

    arena.push()
    aring = [arena.alloc(1024, F32) for _ in range(2)]
    tmpa = arena.alloc(32, F32)
    tmpb = arena.alloc(32, F32)
    A_b = arena.alloc(2048, F32)
    B_b = arena.alloc(2048, F32)
    m_t = arena.alloc(512, F32)
    t1_t = arena.alloc(512, F32)
    var_t = arena.alloc(512, F32)
    sq_t = arena.alloc(512, F32)
    ht1 = [arena.alloc(2048, F32) for _ in range(2)]
    xring = [arena.alloc(2048, F32) for _ in range(2)]
    xsq = [arena.alloc(2048, BF16) for _ in range(2)]

    for kc in range(KC):
        buf, qbuf = xring[kc % 2], xsq[kc % 2]
        xbf = h_all.c(S * kc, S * kc + S)
        dma("sync", buf, xT_my[128 * kc:128 * kc + 128, :])
        cp(xbf, buf)
        act(qbuf, buf, AF.Square)
        for tb in range(4):
            mm(onb_sb, xbf.c(512 * tb, 512 * tb + 512), pb[tb],
               kc == 0, kc == KC - 1)
        for tb in range(4):
            mm(onb_sb, qbuf.c(512 * tb, 512 * tb + 512), pb[4 + tb],
               kc == 0, kc == KC - 1)
    for tb in range(4):
        Atb, Btb = A_b.c(512 * tb, 512 * tb + 512), B_b.c(512 * tb, 512 * tb + 512)
        ts(m_t, pb[tb], 1.0 / 4096.0, None, OP.mult)
        tt(t1_t, m_t, m_t, OP.mult)
        stt(var_t, pb[4 + tb], 1.0 / 4096.0, t1_t, OP.mult, OP.subtract)
        act(sq_t, var_t, AF.Sqrt, bias=eps_sb)
        recip(Atb, sq_t)
        tt(Btb, m_t, Atb, OP.mult)

    for i in range(24):
        buf = aring[i % 2]
        dma("sync", buf.p(0, 4),
            BV(mod_ar.ap[:, 1024 * i:1024 * i + 1024], "mod_ar",
               1024 * i * 4, (1024 * i + 1024) * 4, F32))
        for j in range(8):
            col = 8 * i + j
            mm(buf.p(0, 4).c(128 * j, 128 * j + 128), e_sb, pb[0].c(col, col + 1),
               True, True)
    cp(modT, pb[0].c(0, 192))
    ts(c1a, modT.c(32, 64), 1.0, None, OP.add)
    ts(c1b, modT.c(128, 160), 1.0, None, OP.add)

    def sign_max(dst, off):
        ts(tmpa, modT.c(off, off + 32), 0.0, None, OP.is_ge)
        ts(tmpa, tmpa, 2.0, -1.0, OP.mult, OP.add)
        act(tmpb, modT.c(off, off + 32), AF.Abs)
        ts(tmpb, tmpb, 0.25, None, OP.max)
        tt(dst, tmpa, tmpb, OP.mult)

    sign_max(ga_e, 64)
    sign_max(gb_e, 160)
    tt(gob, ga_e, ob_sb, OP.mult)
    tt(gbswd, gb_e, swd_sb, OP.mult)

    for kc in range(KC):
        t1 = ht1[kc % 2]
        hchunk = h_all.c(S * kc, S * kc + S)
        tt(t1, hchunk, A_b, OP.mult)
        tt(t1, t1, B_b, OP.subtract)
        act(hchunk, t1, AF.Identity, bias=modT.c(kc, kc + 1),
            scale=c1a.c(kc, kc + 1))
    arena.pop()

    arena.push()
    cos_sb = arena.alloc(2048, F32)
    sin_sb = arena.alloc(2048, F32)
    dma("sync", cos_sb, cos2[:, :])
    dma("sync", sin_sb, sin2s[:, :])
    wqr = [arena.alloc(1024, BF16) for _ in range(4)]
    qpre = arena.alloc(2048, BF16)
    kpre = arena.alloc(2048, BF16)
    qrp = qpre
    krp = kpre
    vfm = arena.alloc(2048, BF16)
    vtm = vfm
    tcos = arena.alloc(512, F32)
    usin = arena.alloc(512, F32)
    pt_ring = [arena.alloc(512, BF16) for _ in range(4)]
    nsq = arena.alloc(512, BF16)
    maxq = arena.alloc(8, F32)
    bias_v = arena.alloc(1, F32)
    prod_v = arena.alloc(1, F32)
    rsig = arena.alloc(512, F32)
    ctmp = arena.alloc(512, F32)
    cpair = [arena.alloc(1024, BF16) for _ in range(2)]

    def wq_src(i):
        hl, t, part = i // 12, (i // 4) % 3, i % 4
        return wqkv[hl, t][:, 1024 * part:1024 * part + 1024]
    ens_wq = mkstream(wqr, wq_src, NHC * 12)
    ens_wq(3)

    for hl in range(NHC):
        for t, (dst, bcol) in enumerate(((qpre, 0), (kpre, 16), (vfm, None))):
            for kc in range(KC):
                si = hl * 12 + t * 4 + kc // 8
                if kc % 8 == 0:
                    ens_wq(si + 3)
                ws = wqr[si % 4]
                col = 128 * (kc % 8)
                for tb in range(4):
                    mm(ws.c(col, col + 128),
                       h_all.c(S * kc + 512 * tb, S * kc + 512 * tb + 512),
                       pb[tb], kc == 0, kc == KC - 1)
            for tb in range(4):
                if bcol is not None:
                    ts(dst.c(512 * tb, 512 * tb + 512), pb[tb],
                       bq_sb.c(bcol + hl, bcol + hl + 1), None, OP.add)
                else:
                    cp(dst.c(512 * tb, 512 * tb + 512), pb[tb])
        for t8 in range(16):
            bank = pbb[t8 % 2]
            mm(vfm.c(128 * t8, 128 * t8 + 128), idb_sb, bank.c(0, 128),
               True, True, transpose=True)
            cp(vtm.c(128 * t8, 128 * t8 + 128), bank.c(0, 128))
        for (pre, rp) in ((qpre, qrp), (kpre, krp)):
            for tb in range(4):
                bank = pb[2 + tb % 2]
                mm(psw_sb, pre.c(512 * tb, 512 * tb + 512), bank, True, True)
                tt(tcos, pre.c(512 * tb, 512 * tb + 512),
                   cos_sb.c(512 * tb, 512 * tb + 512), OP.mult)
                tt(usin, bank, sin_sb.c(512 * tb, 512 * tb + 512), OP.mult)
                tt(rp.c(512 * tb, 512 * tb + 512), tcos, usin, OP.add)
        for (rp, mcol) in ((qrp, 0), (krp, 4)):
            for tb in range(4):
                act(nsq, rp.c(512 * tb, 512 * tb + 512), AF.Square)
                mm(onb_sb, nsq, pb[3], True, True)
                if DBG and hl == 2 and tb == 3 and rp is krp:
                    dma("gpsimd", BV(dbg["d_nsq2"], "d_nsq2", 0, 2048, F32), nsq)
                    cp(rsig, pb[3])
                    dma("sync", BV(dbg["d_norm2"], "d_norm2", 0, 2048, F32), rsig)
                if DBG and hl == 0 and tb == 0 and rp is qrp:
                    dma("gpsimd", BV(dbg["d_nsq"], "d_nsq", 0, 2048, F32), nsq)
                    cp(rsig, pb[3])
                    dma("sync", BV(dbg["d_norm"], "d_norm", 0, 2048, F32), rsig)
                red(maxq.c(mcol + tb, mcol + tb + 1), pb[3], OP.max)
        if DBG and hl == 0:
            dma("sync", BV(dbg["d_maxq"], "d_maxq", 0, 32, F32), maxq)
        if DBG and hl == 1:
            dma("sync", BV(dbg["d_maxq1"], "d_maxq1", 0, 32, F32), maxq)
        if DBG and hl == 2:
            dma("sync", BV(dbg["d_maxq2"], "d_maxq2", 0, 32, F32), maxq)
        tr.op("vector", lambda e, o=prod_v.ap, x=maxq.c(0, 4).ap: e.tensor_reduce(
            o, x, axis=AX.X, op=OP.max), reads=[maxq.c(0, 4)], writes=[prod_v],
            tag=f"redq_h{hl}")
        tr.op("vector", lambda e, o=bias_v.ap, x=maxq.c(4, 8).ap: e.tensor_reduce(
            o, x, axis=AX.X, op=OP.max), reads=[maxq.c(4, 8)], writes=[bias_v],
            tag=f"redk_h{hl}")
        if DBG and hl == 1:
            dma("sync", BV(dbg["d_tail1"][0, :], "d_tail1", 0, 4, F32), prod_v)
            dma("sync", BV(dbg["d_tail1"][1, :], "d_tail1", 4, 8, F32), bias_v)
        if DBG and hl == 0:
            dma("sync", BV(dbg["d_tail"][0, :], "d_tail", 0, 4, F32), prod_v)
            dma("sync", BV(dbg["d_tail"][1, :], "d_tail", 4, 8, F32), bias_v)
        tt(prod_v, prod_v, bias_v, OP.mult)
        if DBG and hl == 0:
            dma("sync", BV(dbg["d_tail"][2, :], "d_tail", 8, 12, F32), prod_v)
        act(prod_v, prod_v, AF.Sqrt)
        if DBG and hl == 0:
            dma("sync", BV(dbg["d_tail"][3, :], "d_tail", 12, 16, F32), prod_v)
        if DBG and hl == 1:
            dma("sync", BV(dbg["d_tail1"][2, :], "d_tail1", 8, 12, F32), prod_v)
        ts(bias_v, prod_v, -ISCALE, -ISCALE, OP.mult, OP.add)
        if DBG:
            dma("sync", BV(dbg["d_ball"][hl, :], "d_ball", 4 * hl, 4 * hl + 4, F32),
                bias_v)
        blocks = []
        for qb in range(4):
            nkc = 4 * qb + 4
            for kc in range(nkc):
                qs = max(512 * qb, 128 * kc)
                nn = 512 * (qb + 1) - qs
                blocks.append((qb, kc, qs, nn, qs - 512 * qb, nkc))

        def score_mm(i):
            qb, kc, qs, nn, off, nkc = blocks[i]
            mm(krp.c(128 * kc, 128 * kc + 128), qrp.c(qs, qs + nn),
               pb[2 + i % 2].c(0, nn), True, True)

        score_mm(0)
        for i, (qb, kc, qs, nn, off, nkc) in enumerate(blocks):
            sbank = pb[2 + i % 2]
            ptb = pt_ring[i % 4]
            if i + 1 < len(blocks):
                score_mm(i + 1)
            tr.op("scalar", lambda e, o=ptb.c(0, nn).ap, x=sbank.c(0, nn).ap:
                  e.activation(o, x, AF.Exp, bias=bias_v.ap, scale=ISCALE),
                  reads=[sbank.c(0, nn), bias_v], writes=[ptb.c(0, nn)])
            if kc >= 4 * qb:
                tt(ptb.c(0, 128), ptb.c(0, 128), tri_sb, OP.mult)
            sig_bank = pb[5 + qb % 2]
            ctx_bank = pb[4 + 3 * (qb % 2)]
            mm(onb_sb, ptb.c(0, nn), sig_bank.c(off, off + nn),
               kc == 0, kc == nkc - 1)
            mm(vtm.c(128 * kc, 128 * kc + 128), ptb.c(0, nn),
               ctx_bank.c(off, off + nn), kc == 0, kc == nkc - 1)
            if kc == nkc - 1:
                recip(rsig, sig_bank)
                tt(ctmp, ctx_bank, rsig, OP.mult)
                s_slot, t_off = qb // 2, 512 * (qb % 2)
                cb = cpair[s_slot % 2]
                ts(cb.c(t_off, t_off + 512), ctmp,
                   bq_sb.c(32 + hl, 32 + hl + 1), None, OP.add)
                if qb % 2 == 1:
                    lo = (s_slot * 2048 * 1024 + 128 * hl * 1024) * 2
                    dma("sync",
                        BV(ctx_send.ap[s_slot, 128 * hl:128 * hl + 128, :],
                           "ctx_send", lo, lo + 128 * 1024 * 2, BF16),
                        cb)
        if DBG and hl == 0:
            for nm, srcb in (("d_q0", qrp), ("d_k0", krp), ("d_v0", vtm)):
                dma("gpsimd", BV(dbg[nm], nm, 0, S * 4, F32), srcb)
            dma("sync", BV(dbg["d_bias0"], "d_bias0", 0, 4, F32), bias_v)

    arena.pop()

    mq4 = [arena.alloc(512, F32) for _ in range(4)]
    arena.push()
    agA = [arena.alloc(1024, BF16) for _ in range(2)]
    agB = [arena.alloc(1024, BF16) for _ in range(2)]
    tsel = arena.alloc(1024, BF16)
    ow_sb = [arena.alloc(4096, BF16) for _ in range(2)]
    xo_t = [arena.alloc(512, F32) for _ in range(2)]
    x2a = [arena.alloc(512, F32) for _ in range(2)]
    x2b = [arena.alloc(512, F32) for _ in range(2)]
    x2sq = [arena.alloc(512, BF16) for _ in range(2)]
    A2 = arena.alloc(1024, F32)
    B2 = arena.alloc(1024, F32)
    m2t = arena.alloc(512, F32)
    t12 = arena.alloc(512, F32)
    var2 = arena.alloc(512, F32)
    sq2 = arena.alloc(512, F32)
    h2t1 = [arena.alloc(512, F32) for _ in range(2)]
    xg_d = [arena.alloc(512, F32) for _ in range(2)]

    def ctx_own_c(kc):
        return h_all.c(1024 * kc, 1024 * kc + 1024)

    def h2c(tb, m):
        base = 32768 + 16384 * tb + 512 * m
        return h_all.c(base, base + 512)

    dmac(ow_sb[0], ow[0])
    dmac(ow_sb[1], ow[1])
    tr.op("gpsimd",
          lambda e: e.collective_compute(
              "AllGather", OP.bypass,
              replica_groups=[[0, 1], [2, 3], [4, 5], [6, 7]],
              ins=[ctx_send.ap], outs=[ctx_ag.ap]),
          reads=[ctx_send], writes=[ctx_ag], kind="cc")

    for kc in range(KC):
        r, fr = kc // 16, kc % 16
        a, b_ = agA[kc % 2], agB[kc % 2]
        base = ((r * 2 + 0) * 2048 + 128 * fr) * 1024 * 2
        dma("sync", a, BV(ctx_ag.ap[r, 0, 128 * fr:128 * fr + 128, :], "ctx_ag",
                          base, base + 128 * 1024 * 2, BF16))
        base1 = ((r * 2 + 1) * 2048 + 128 * fr) * 1024 * 2
        dma("sync", b_, BV(ctx_ag.ap[r, 1, 128 * fr:128 * fr + 128, :], "ctx_ag",
                           base1, base1 + 128 * 1024 * 2, BF16))
        ts(tsel, b_, sel1_sb, None, OP.mult)
        stt(ctx_own_c(kc), a, sel0_sb, tsel, OP.mult, OP.add)

    for m in range(KC):
        wbuf = ow_sb[m % 2]
        for tb in range(2):
            bank = pb[(2 * m + tb) % 4]
            for kc in range(KC):
                mm(wbuf.c(128 * kc, 128 * kc + 128),
                   ctx_own_c(kc).c(512 * tb, 512 * tb + 512),
                   bank, kc == 0, kc == KC - 1)
            xo = xo_t[tb]
            dma("sync", xo, xT_own[128 * m:128 * m + 128, 512 * tb:512 * tb + 512])
            i2 = (2 * m + tb) % 2
            act(x2a[i2], bank, AF.Identity, bias=gob.c(m, m + 1),
                scale=ga_e.c(m, m + 1))
            tt(x2b[i2], x2a[i2], xo, OP.add)
            lo = (m * 128 * 1024 + 512 * tb) * 4
            dma("sync", BV(x2_dram.ap[m, :, 512 * tb:512 * tb + 512], "x2_dram",
                           lo, lo + 512 * 4, F32), x2b[i2])
            cp(h2c(tb, m), x2b[i2])
            act(x2sq[i2], x2b[i2], AF.Square)
            mm(onb_sb, h2c(tb, m), pb[4 + tb], m == 0, m == KC - 1)
            mm(onb_sb, x2sq[i2], pb[6 + tb], m == 0, m == KC - 1)
        if m + 2 < KC:
            dmac(ow_sb[m % 2], ow[m + 2])
    for tb in range(2):
        A2tb, B2tb = A2.c(512 * tb, 512 * tb + 512), B2.c(512 * tb, 512 * tb + 512)
        ts(m2t, pb[4 + tb], 1.0 / 4096.0, None, OP.mult)
        tt(t12, m2t, m2t, OP.mult)
        stt(var2, pb[6 + tb], 1.0 / 4096.0, t12, OP.mult, OP.subtract)
        act(sq2, var2, AF.Sqrt, bias=eps_sb)
        recip(A2tb, sq2)
        tt(B2tb, m2t, A2tb, OP.mult)
    for tb in range(2):
        for m in range(KC):
            slot, t1 = h2c(tb, m), h2t1[m % 2]
            tt(t1, slot, A2.c(512 * tb, 512 * tb + 512), OP.mult)
            tt(t1, t1, B2.c(512 * tb, 512 * tb + 512), OP.subtract)
            act(slot, t1, AF.Identity, bias=modT.c(96 + m, 96 + m + 1),
                scale=c1b.c(m, m + 1))
            for q, ivv in ((0, ivg_sb), (1, ivu_sb)):
                mx = mq4[2 * q + tb]
                if m == 0:
                    act(mx, slot, AF.Abs, scale=ivv.c(0, 1))
                else:
                    xb = xg_d[q]
                    act(xb, slot, AF.Abs, scale=ivv.c(m, m + 1))
                    tt(mx, mx, xb, OP.max)
    arena.pop()

    arena.push()
    xsd = [arena.alloc(512, F32) for _ in range(2)]
    rmaxq = arena.alloc(8, F32)
    srow = arena.alloc(512, F32)
    rrow = arena.alloc(512, F32)
    scb_g = [arena.alloc(512, F32) for _ in range(2)]
    scb_u = [arena.alloc(512, F32) for _ in range(2)]
    rcb = [arena.alloc(512, F32) for _ in range(2)]
    qint = [arena.alloc(512, I32) for _ in range(2)]
    wgr = [arena.alloc(1024, BF16) for _ in range(4)]
    wur = [arena.alloc(1024, BF16) for _ in range(4)]
    wdr = [arena.alloc(1024, BF16) for _ in range(4)]
    g_t = arena.alloc(512, F32)
    s_t = arena.alloc(512, F32)
    u_t = arena.alloc(512, F32)
    ib_t = [arena.alloc(512, BF16) for _ in range(2)]
    d_t = arena.alloc(512, F32)
    y_tb = [arena.alloc(512, F32) for _ in range(2)]
    x2r2 = [arena.alloc(512, F32) for _ in range(2)]

    def qgu_c(q, tb, kc):
        base = (16384 * q if tb == 0 else 32768 + 16384 * q)
        return h_all.c(base + 512 * kc, base + 512 * kc + 512)

    def rb_c(tb, kc):
        base = 32768 + 8192 * tb
        return h_all.c(base + 512 * kc, base + 512 * kc + 512)

    def qi_c(tb, kc):
        base = 8192 * tb
        return h_all.c(base + 512 * kc, base + 512 * kc + 512)

    def idchunk(tb, im):
        lo = (tb * IKC + im) * 128 * 512 * 2
        return BV(inter_dram[tb, im], "inter_dram", lo, lo + 128 * 512 * 2, BF16)

    def mkscales(mx, scb, rc):
        for j in range(4):
            bank = pb[4 + j % 2]
            mm(mx.c(128 * j, 128 * j + 128), idf_sb, bank.c(0, 128),
               True, True, transpose=True)
            red(rmaxq.c(j, j + 1), bank.c(0, 128), OP.max)
        for j in range(4):
            mm(rmaxq.c(j, j + 1), idf_sb,
               pb[6].p(0, 1).c(128 * j, 128 * j + 128), True, True,
               transpose=True)
        cp(srow.p(0, 1), pb[6].p(0, 1).c(0, 512))
        ts(srow.p(0, 1), srow.p(0, 1), 1.0 / 127.0, 1e-8, OP.mult, OP.max)
        recip(rrow.p(0, 1), srow.p(0, 1))
        mm(onf_sb, srow.p(0, 1), pb[5], True, True)
        cp(scb, pb[5])
        mm(onf_sb, rrow.p(0, 1), pb[4], True, True)
        cp(rc, pb[4])

    def quant_gu(tb):
        for q, (mx, scb, ivv) in enumerate(((mq4[tb], scb_g[tb], ivg_sb),
                                            (mq4[2 + tb], scb_u[tb], ivu_sb))):
            mkscales(mx, scb, rcb[q])
            for kc in range(KC):
                qb = qint[kc % 2]
                stt(qb, h2c(tb, kc), ivv.c(kc, kc + 1), rcb[q],
                    OP.mult, OP.mult)
                cp(qgu_c(q, tb, kc), qb,
                   eng=("gpsimd" if kc % 4 == 0 else "vector"))

    ens_g = mkstream(wgr, lambda i: wg[i // 4][:, 1024 * (i % 4):
                                                1024 * (i % 4) + 1024], 64)
    ens_u = mkstream(wur, lambda i: wu[i // 4][:, 1024 * (i % 4):
                                               1024 * (i % 4) + 1024], 64)
    ens_d = mkstream(wdr, lambda i: wd[i // 2][:, 1024 * (i % 2):
                                               1024 * (i % 2) + 1024], 64)

    quant_gu(0)
    quant_gu(1)

    ens_g(3)
    ens_u(3)
    for im in range(IKC):
        bks = [pb[4 * (im % 2) + j] for j in range(4)]
        for kc in range(KC):
            gs = im * 4 + kc // 8
            if kc % 8 == 0:
                ens_g(gs + 3)
                ens_u(gs + 3)
            wgs, wus = wgr[gs % 4], wur[gs % 4]
            col = 128 * (kc % 8)
            mm(wgs.c(col, col + 128), qgu_c(0, 0, kc), bks[0], kc == 0,
               kc == KC - 1)
            mm(wgs.c(col, col + 128), qgu_c(0, 1, kc), bks[1], kc == 0,
               kc == KC - 1)
            mm(wus.c(col, col + 128), qgu_c(1, 0, kc), bks[2], kc == 0,
               kc == KC - 1)
            mm(wus.c(col, col + 128), qgu_c(1, 1, kc), bks[3], kc == 0,
               kc == KC - 1)
        for tb, mxi in ((0, mq4[0]), (1, mq4[1])):
            stt(g_t, bks[tb], swg_sb.c(im, im + 1), scb_g[tb],
                OP.mult, OP.mult)
            act(s_t, g_t, AF.Silu)
            stt(u_t, bks[2 + tb], swu_sb.c(im, im + 1), scb_u[tb],
                OP.mult, OP.mult)
            tt(u_t, s_t, u_t, OP.mult)
            ib = ib_t[tb]
            ts(ib, u_t, 10.0, -10.0, OP.min, OP.max)
            xb = xsd[tb]
            act(xb, ib, AF.Abs, scale=ivi_sb.c(im, im + 1))
            if im == 0:
                cp(mxi, xb)
            else:
                tt(mxi, mxi, xb, OP.max)
            dma("scalar", idchunk(tb, im), ib)

    scb_i = scb_g
    for tb, mxi in ((0, mq4[0]), (1, mq4[1])):
        mkscales(mxi, scb_i[tb], rcb[tb])
    for tb in range(2):
        for kc in range(IKC):
            rb = rb_c(tb, kc)
            dma("sync", rb, idchunk(tb, kc))
            qb = qint[kc % 2]
            stt(qb, rb, ivi_sb.c(kc, kc + 1), rcb[tb], OP.mult, OP.mult)
            cp(qi_c(tb, kc), qb, eng=("gpsimd" if kc % 4 == 0 else "vector"))

    ens_d(3)
    for m in range(KC):
        bd = [pb[2 * (m % 2)], pb[2 * (m % 2) + 1]]
        for kci in range(IKC):
            ds = m * 2 + kci // 8
            if kci % 8 == 0:
                ens_d(ds + 3)
            wds = wdr[ds % 4]
            col = 128 * (kci % 8)
            mm(wds.c(col, col + 128), qi_c(0, kci), bd[0], kci == 0,
               kci == IKC - 1)
            mm(wds.c(col, col + 128), qi_c(1, kci), bd[1], kci == 0,
               kci == IKC - 1)
        for tb in range(2):
            xb = x2r2[tb]
            lo = (m * 128 * 1024 + 512 * tb) * 4
            dma("sync", xb, BV(x2_dram.ap[m, :, 512 * tb:512 * tb + 512],
                               "x2_dram", lo, lo + 512 * 4, F32))
            stt(d_t, bd[tb], gbswd.c(m, m + 1), scb_i[tb], OP.mult, OP.mult)
            yb = y_tb[tb]
            tt(yb, d_t, xb, OP.add)
            dma("sync", BV(yT_my[128 * m:128 * m + 128,
                                 512 * tb:512 * tb + 512],
                           "y_out", lo, lo + 512 * 4, F32), yb)
    arena.pop()
    tr.wait_all("sync", [y_out])

    tr.finalize()
    with nc.Block() as block:
        @block.sync
        def _(e):
            for fn in tr.q["sync"]:
                fn(e)

        @block.gpsimd
        def _(e):
            for fn in tr.q["gpsimd"]:
                fn(e)

        @block.scalar
        def _(e):
            for fn in tr.q["scalar"]:
                fn(e)

        @block.vector
        def _(e):
            for fn in tr.q["vector"]:
                fn(e)

        @block.tensor
        def _(e):
            for fn in tr.q["tensor"]:
                fn(e)

    st.close()
    return nc


def make_inputs(core, inp):
    b, half = core // 2, core % 2
    lo = 1024 * half
    f32 = np.float32
    x = np.asarray(inp["L0_attn_hidden_states"], f32)
    qkv_w = np.asarray(inp["L0_attn_qkv_w"], f32)
    qkv_b = np.asarray(inp["L0_attn_qkv_b"], f32)
    out_w = np.asarray(inp["L0_attn_out_w"], f32)
    out_b = np.asarray(inp["L0_attn_out_b"], f32)
    wmod = np.asarray(inp["L0_wmod"], f32)
    cc = np.asarray(inp["L0_c"], f32)

    d = {}
    d["xT_my"] = np.ascontiguousarray(x[b].T)
    d["xT_own"] = np.ascontiguousarray(x[b, lo:lo + 1024].T)
    heads = []
    for hl in range(NHC):
        hg = 16 * half + hl
        for t in range(3):
            heads.append(qkv_w[:, E * t + 128 * hg:E * t + 128 * hg + 128]
                         .reshape(KC, 128, 128))
    d["wqkv"] = np.ascontiguousarray(
        np.stack(heads).transpose(0, 2, 1, 3)).reshape(NHC, 3, 128, KC * 128)
    bq = np.zeros((128, 48), f32)
    for hl in range(NHC):
        hg = 16 * half + hl
        bq[:, hl] = qkv_b[128 * hg:128 * hg + 128]
        bq[:, 16 + hl] = qkv_b[E + 128 * hg:E + 128 * hg + 128]
        bq[:, 32 + hl] = qkv_b[2 * E + 128 * hg:2 * E + 128 * hg + 128]
    d["bqkv"] = bq
    d["wmod_my"] = np.ascontiguousarray(
        wmod[512 * core:512 * core + 512, :].reshape(4, 128, 24, 1024)
        .transpose(1, 2, 0, 3)).reshape(128, 24 * 4096)
    d["c_my"] = np.ascontiguousarray(cc[:, 512 * core:512 * core + 512])
    e = np.zeros((4, 1), f32)
    e[b, 0] = 1.0
    d["e_b"] = e
    d["sel0"] = np.full((128, 1), 1.0 if half == 0 else 0.0, f32)
    d["sel1"] = np.full((128, 1), 1.0 if half == 1 else 0.0, f32)
    d["ow"] = np.ascontiguousarray(
        out_w.reshape(KC, 128, KC, 128).transpose(2, 1, 0, 3)).reshape(
        KC, 128, KC * 128)
    d["obT"] = np.ascontiguousarray(out_b.reshape(32, 128).T)
    wgm = np.asarray(inp["L0_ffn_w_gate"], f32)
    wum = np.asarray(inp["L0_ffn_w_up"], f32)
    wdm = np.asarray(inp["L0_ffn_w_down"], f32)
    d["wg"] = np.ascontiguousarray(
        wgm.T.reshape(KC, 128, IKC, 128).transpose(2, 1, 0, 3)).reshape(
        IKC, 128, KC * 128)
    d["wu"] = np.ascontiguousarray(
        wum.T.reshape(KC, 128, IKC, 128).transpose(2, 1, 0, 3)).reshape(
        IKC, 128, KC * 128)
    d["wd"] = np.ascontiguousarray(
        wdm.T.reshape(IKC, 128, KC, 128).transpose(2, 1, 0, 3)).reshape(
        KC, 128, IKC * 128)
    d["invgT"] = np.ascontiguousarray(
        np.asarray(inp["L0_ffn_inv_gate"], f32).reshape(KC, 128).T)
    d["invuT"] = np.ascontiguousarray(
        np.asarray(inp["L0_ffn_inv_up"], f32).reshape(KC, 128).T)
    d["inviT"] = np.ascontiguousarray(
        np.asarray(inp["L0_ffn_inv_inter"], f32).reshape(IKC, 128).T)
    d["swgT"] = np.ascontiguousarray(
        np.asarray(inp["L0_ffn_s_wgate"], f32).reshape(IKC, 128).T)
    d["swuT"] = np.ascontiguousarray(
        np.asarray(inp["L0_ffn_s_wup"], f32).reshape(IKC, 128).T)
    d["swdT"] = np.ascontiguousarray(
        np.asarray(inp["L0_ffn_s_wdown"], f32).reshape(KC, 128).T)
    return d


def make_consts():
    f32 = np.float32
    j = np.arange(64, dtype=np.float64)
    invf = 1.0 / (10000.0 ** (j / 64.0))
    t = np.arange(S, dtype=np.float64)
    ang = np.outer(invf, t)
    cos = np.cos(ang)
    sin = np.sin(ang)
    cos2 = np.concatenate([cos, cos], 0).astype(f32)
    sin2s = np.concatenate([-sin, sin], 0).astype(f32)
    tri = (np.arange(128)[:, None] <= np.arange(128)[None, :])
    psw = np.zeros((128, 128), np.float32)
    for i in range(128):
        psw[(i + 64) % 128, i] = 1.0
    return {"cos2": cos2, "sin2s": sin2s,
            "trimask": tri.astype(ml_dtypes.bfloat16),
            "pswap": psw.astype(ml_dtypes.bfloat16),
            "ident_bf": np.eye(128).astype(ml_dtypes.bfloat16),
            "ident_f": np.eye(128, dtype=f32),
            "ones_bf": np.ones((128, 128), ml_dtypes.bfloat16),
            "ones_f": np.ones((1, 128), f32)}


_CACHE = {}


def kernel(**inputs):
    if "nc" not in _CACHE:
        _CACHE["nc"] = build_program()
    nc = _CACHE["nc"]
    consts = make_consts()
    in_maps = []
    for c in range(8):
        m = make_inputs(c, inputs)
        m.update(consts)
        in_maps.append(m)
    res = run_bass_kernel_spmd(nc, in_maps, core_ids=list(range(8)))
    y = np.empty((B, S, E), np.float32)
    for c in range(8):
        b, half = c // 2, c % 2
        lo = 1024 * half
        y[b, lo:lo + 1024] = res.results[c]["yT_my"].T
    return y


# version 12
# speedup vs baseline: 1.0719x; 1.0719x over previous
import numpy as np
import ml_dtypes
import concourse.bass as bass
import concourse.mybir as mybir
from concourse.bass_utils import run_bass_kernel_spmd
from contextlib import ExitStack

F32 = mybir.dt.float32
BF16 = mybir.dt.bfloat16
I32 = mybir.dt.int32
AF = mybir.ActivationFunctionType
OP = mybir.AluOpType
AX = mybir.AxisListType

B, S, E, D, NHC, II = 4, 2048, 4096, 128, 16, 2048
KC = E // 128
IKC = II // 128
ISCALE = float(1.0 / np.sqrt(128.0))
ENGINES = ("sync", "gpsimd", "scalar", "vector", "tensor")
DBG = False
FULL_WAITS = False
PSUM_EXCL = True

_IS = {F32: 4, BF16: 2, I32: 4}


class BV:

    def __init__(self, ap, key, lo, hi, dt):
        self.ap, self.key, self.lo, self.hi, self.dt = ap, key, lo, hi, dt

    def c(self, a, b):
        isz = _IS[self.dt]
        return BV(self.ap[:, a:b], self.key, self.lo + a * isz, self.lo + b * isz,
                  self.dt)

    def p(self, a, b):
        return BV(self.ap[a:b], self.key, self.lo, self.hi, self.dt)


class Tracker:
    def __init__(self, st, nc):
        self.nc = nc
        self.q = {e: [] for e in ENGINES}
        self.esem = {e: st.enter_context(nc.semaphore("es_" + e)) for e in ENGINES}
        self.NDQ = 8
        self.dsem = {e: [st.enter_context(nc.semaphore(f"ds_{e}{i}"))
                         for i in range(self.NDQ)]
                     for e in ("sync", "gpsimd", "scalar")}
        self.ccsem = st.enter_context(nc.semaphore("ccsem"))
        self.cnt = {e: 0 for e in ENGINES}
        self.dcnt = {e: [0] * self.NDQ for e in ("sync", "gpsimd", "scalar")}
        self.didx = {e: 0 for e in ("sync", "gpsimd", "scalar")}
        self.ccnt = 0
        self.reg = {}
        self.sat = {e: {} for e in ENGINES}
        self.snap = {e: {} for e in ENGINES}
        self.semobj = {}

    def _semname(self, sem):
        for e, s in self.esem.items():
            if s is sem:
                return "e_" + e
        for e, ss in self.dsem.items():
            for i, s in enumerate(ss):
                if s is sem:
                    return f"d_{e}{i}"
        return "cc"

    def _sid(self, sem):
        i = id(sem)
        self.semobj[i] = sem
        return i

    def _hazards(self, eng, reads, writes):
        need = {}

        def add(marker):
            if marker is None:
                return
            sem, val, owner = marker
            if owner == eng == "tensor" and sem is self.esem[eng]:
                return
            sid = self._sid(sem)
            if FULL_WAITS:
                if need.get(sid, (None, 0))[1] < val:
                    need[sid] = (sem, val)
                return
            if self.sat[eng].get(sid, 0) >= val:
                return
            if need.get(sid, (None, 0))[1] < val:
                need[sid] = (sem, val)
        for bv in reads:
            for iv in self.reg.get(bv.key, []):
                if iv[0] < bv.hi and bv.lo < iv[1]:
                    add(iv[2])
        for bv in writes:
            for iv in self.reg.get(bv.key, []):
                if iv[0] < bv.hi and bv.lo < iv[1]:
                    add(iv[2])
                    for r in iv[3]:
                        add(r)
        return [v for _, v in need.items()]

    def _record(self, eng, reads, writes, marker):
        for bv in reads:
            lst = self.reg.setdefault(bv.key, [])
            hit = False
            for iv in lst:
                if iv[0] == bv.lo and iv[1] == bv.hi:
                    iv[3].append(marker)
                    hit = True
                    break
            if not hit:
                lst.append([bv.lo, bv.hi, None, [marker]])
        for bv in writes:
            lst = self.reg.setdefault(bv.key, [])
            keep = []
            for iv in lst:
                if iv[0] < bv.hi and bv.lo < iv[1]:
                    if iv[0] < bv.lo:
                        keep.append([iv[0], bv.lo, iv[2], list(iv[3])])
                    if bv.hi < iv[1]:
                        keep.append([bv.hi, iv[1], iv[2], list(iv[3])])
                else:
                    keep.append(iv)
            keep.append([bv.lo, bv.hi, marker, []])
            self.reg[bv.key] = keep

    def op(self, eng, f, reads=(), writes=(), kind="compute", tag=None):
        if PSUM_EXCL:
            psr = [b for b in reads if b.key.startswith("pb")]
            if psr:
                reads = [b for b in reads if not b.key.startswith("pb")]
                writes = list(writes) + psr
        waits = self._hazards(eng, reads, writes)
        if kind == "compute":
            sem = self.esem[eng]
            self.cnt[eng] += 1
            val, k = self.cnt[eng], 1
        elif kind == "dma":
            slot = self.didx[eng] % self.NDQ
            self.didx[eng] += 1
            sem = self.dsem[eng][slot]
            prior = self.dcnt[eng][slot]
            if prior:
                sid = self._sid(sem)
                if self.sat[eng].get(sid, 0) < prior and not any(
                        s is sem and v >= prior for s, v in waits):
                    waits = list(waits) + [(sem, prior)]
            self.dcnt[eng][slot] = prior + 16
            val, k = prior + 16, 16
        else:
            sem = self.ccsem
            self.ccnt += 1
            val, k = self.ccnt, 1
        marker = (sem, val, eng)
        if not hasattr(self, "log"):
            self.log = []
        self.log.append((eng, kind, val, tag,
                         [(self._semname(s), v) for s, v in waits],
                         [(b.key, b.lo, b.hi) for b in reads],
                         [(b.key, b.lo, b.hi) for b in writes],
                         self._semname(sem)))
        self.q[eng].append((f, tuple(waits), sem, k, val, kind))
        for s, v in waits:
            sid = self._sid(s)
            self.sat[eng][sid] = max(self.sat[eng].get(sid, 0), v)
            for oe in ENGINES:
                if s is self.esem[oe]:
                    snapd = self.snap[oe].get(v)
                    if snapd:
                        for k2, v2 in snapd.items():
                            self.sat[eng][k2] = max(self.sat[eng].get(k2, 0), v2)
        if kind == "compute":
            self.snap[eng][val] = dict(self.sat[eng])
            self.snap[eng][val][self._sid(sem)] = val
        self._record(eng, reads, writes, marker)
        return marker

    def finalize(self):
        waited = {self._sid(s): set() for s in self.esem.values()}
        for eng in ENGINES:
            for (f, waits, sem, k, val, kind) in self.q[eng]:
                for s, v in waits:
                    sid = self._sid(s)
                    if sid in waited:
                        waited[sid].add(v)
        remap = {}
        newq = {e: [] for e in ENGINES}
        for eng in ENGINES:
            sid = self._sid(self.esem[eng])
            compact = 0
            for (f, waits, sem, k, val, kind) in self.q[eng]:
                inc = kind != "wait"
                if kind == "compute":
                    if val in waited[sid]:
                        compact += 1
                        remap[(sid, val)] = compact
                    else:
                        inc = False
                newq[eng].append((f, waits, sem, k, inc))
        out = {e: [] for e in ENGINES}
        for eng in ENGINES:
            for (f, waits, sem, k, inc) in newq[eng]:
                w2 = []
                for s, v in waits:
                    sid = self._sid(s)
                    if (sid, v) in remap:
                        w2.append((s, remap[(sid, v)]))
                    elif sid in waited:
                        assert False, f"wait on un-inc'd marker {v}"
                    else:
                        w2.append((s, v))

                def fn(e, f=f, w2=tuple(w2), sem=sem, k=k, inc=inc):
                    for s, v in w2:
                        e.wait_ge(s, v)
                    if f is not None:
                        inst = f(e)
                        if inc:
                            inst.then_inc(sem, k)
                out[eng].append(fn)
        self.q = out

    def wait_all(self, eng, bvs):
        need = self._hazards(eng, bvs, bvs)
        for s, v in need:
            self.q[eng].append((None, ((s, v),), None, 0, 0, "wait"))
            self.sat[eng][self._sid(s)] = max(self.sat[eng].get(self._sid(s), 0), v)


class Arena:
    def __init__(self, t, nbytes, tr):
        self.t, self.n, self.off, self.stack = t, nbytes, 0, []
        self.tr = tr
        self.nalloc = 0

    def alloc(self, cols, dtype):
        isz = _IS[dtype]
        off = (self.off + 63) // 64 * 64
        sz = cols * isz
        assert off + sz <= self.n, f"arena overflow: {off}+{sz} > {self.n}"
        self.off = off + sz
        view = self.t[:, off // 4:(off + sz + 3) // 4]
        if dtype is not F32:
            view = view.bitcast(dtype)
        self.nalloc += 1
        return BV(view[:, 0:cols], "arena", off, off + sz, dtype)

    def push(self):
        self.stack.append(self.off)

    def pop(self):
        self.off = self.stack.pop()


def build_program():
    nc = bass.Bass("TRN2", target_bir_lowering=False, debug=False,
                   detect_race_conditions=False)
    st = ExitStack()
    tr = Tracker(st, nc)

    def din(name, shape, dt=F32):
        ap = nc.dram_tensor(name, shape, dt, kind="ExternalInput").ap()
        return ap

    xT_my = din("xT_my", [E, S])
    xT_own = din("xT_own", [E, 1024])
    wqkv = din("wqkv", [NHC, 3, 128, KC * 128])
    bqkv = din("bqkv", [128, 48])
    wmod_my = din("wmod_my", [128, 24 * 4096])
    c_my = din("c_my", [4, 512])
    e_b = din("e_b", [4, 1])
    sel0 = din("sel0", [128, 1])
    sel1 = din("sel1", [128, 1])
    ow = din("ow", [KC, 128, KC * 128])
    obT = din("obT", [128, 32])
    wg = din("wg", [IKC, 128, KC * 128])
    wu = din("wu", [IKC, 128, KC * 128])
    wd = din("wd", [KC, 128, IKC * 128])
    invgT = din("invgT", [128, 32])
    invuT = din("invuT", [128, 32])
    inviT = din("inviT", [128, 16])
    swgT = din("swgT", [128, 16])
    swuT = din("swuT", [128, 16])
    swdT = din("swdT", [128, 32])
    cos2 = din("cos2", [128, S])
    sin2s = din("sin2s", [128, S])
    trimask = din("trimask", [128, 128], BF16)
    pswap = din("pswap", [128, 128], BF16)
    ident_bf = din("ident_bf", [128, 128], BF16)
    ident_f = din("ident_f", [128, 128], F32)
    ones_bf = din("ones_bf", [128, 128], BF16)
    ones_f = din("ones_f", [1, 128], F32)
    yT_my = nc.dram_tensor("yT_my", [E, 1024], F32, kind="ExternalOutput").ap()

    dbg = {}
    if DBG:
        for nm, shp in [("d_modT", [128, 192]), ("d_h0", [128, S], ),
                        ("d_q0", [128, S]), ("d_k0", [128, S]), ("d_v0", [128, S]),
                        ("d_ctx0", [128, 512]), ("d_bias0", [128, 1]),
                        ("d_ctxown", [128, 1024]), ("d_x2", [128, 1024]),
                        ("d_h2", [128, 1024]), ("d_qg0", [128, 512]),
                        ("d_inter0", [128, 512]), ("d_sgb", [128, 512]),
                        ("d_nsq", [128, 512]), ("d_norm", [128, 512]),
                        ("d_maxq", [128, 8]), ("d_tail", [4, 128]), ("d_ball", [16, 128]), ("d_sig", [16, 128]), ("d_maxq1", [128, 8]), ("d_nsq1", [128, 512]), ("d_tail1", [4, 128]), ("d_maxq2", [128, 8]), ("d_norm2", [128, 512]), ("d_nsq2", [128, 512])]:
            dbg[nm] = nc.dram_tensor(nm, shp, F32, kind="ExternalOutput").ap()

    mod_part_t = nc.dram_tensor("mod_part", [4, 6 * E], F32)
    mod_ar_t = nc.dram_tensor("mod_ar", [4, 6 * E], F32)
    inter_dram_t = nc.dram_tensor("inter_dram", [2, IKC, 128, 512], BF16)
    ctx_send_t = nc.dram_tensor("ctx_send", [2, 2048, 1024], BF16)
    ctx_ag_t = nc.dram_tensor("ctx_ag", [2, 2, 2048, 1024], BF16)
    x2_dram_t = nc.dram_tensor("x2_dram", [KC, 128, 1024], F32)
    inter_dram = inter_dram_t.ap()
    mod_part = BV(mod_part_t.ap(), "mod_part", 0, 4 * 6 * E * 4, F32)
    mod_ar = BV(mod_ar_t.ap(), "mod_ar", 0, 4 * 6 * E * 4, F32)
    ctx_send = BV(ctx_send_t.ap(), "ctx_send", 0, 2 * 2048 * 1024 * 2, BF16)
    ctx_ag = BV(ctx_ag_t.ap(), "ctx_ag", 0, 2 * 2 * 2048 * 1024 * 2, BF16)
    x2_dram = BV(x2_dram_t.ap(), "x2_dram", 0, KC * 128 * 1024 * 4, F32)
    y_out = BV(yT_my, "y_out", 0, E * 1024 * 4, F32)

    def sbv(name, shape, dt):
        t = st.enter_context(nc.sbuf_tensor(name, shape, dt))
        return BV(t[:, :] if len(shape) == 2 else t[:], name,
                  0, shape[-1] * _IS[dt], dt)

    h_all = sbv("h_all", [128, KC * S], BF16)
    tri_sb = sbv("tri_sb", [128, 128], BF16)
    psw_sb = sbv("psw_sb", [128, 128], BF16)
    idb_sb = sbv("idb_sb", [128, 128], BF16)
    idf_sb = sbv("idf_sb", [128, 128], F32)
    onb_sb = sbv("onb_sb", [128, 128], BF16)
    onf_sb = sbv("onf_sb", [1, 128], F32)
    bq_sb = sbv("bq_sb", [128, 48], F32)
    ob_sb = sbv("ob_sb", [128, 32], F32)
    modT = sbv("modT", [128, 192], F32)
    c1a = sbv("c1a", [128, 32], F32)
    c1b = sbv("c1b", [128, 32], F32)
    ga_e = sbv("ga_e", [128, 32], F32)
    gob = sbv("gob", [128, 32], F32)
    gb_e = sbv("gb_e", [128, 32], F32)
    gbswd = sbv("gbswd", [128, 32], F32)
    ivg_sb = sbv("ivg_sb", [128, 32], F32)
    ivu_sb = sbv("ivu_sb", [128, 32], F32)
    ivi_sb = sbv("ivi_sb", [128, 16], F32)
    swg_sb = sbv("swg_sb", [128, 16], F32)
    swu_sb = sbv("swu_sb", [128, 16], F32)
    swd_sb = sbv("swd_sb", [128, 32], F32)
    sel0_sb = sbv("sel0_sb", [128, 1], F32)
    sel1_sb = sbv("sel1_sb", [128, 1], F32)
    e_sb = sbv("e_sb", [4, 1], F32)
    cT_my = sbv("cT_my", [128, 16], BF16)
    eps_sb = sbv("eps_sb", [128, 1], F32)

    AR_BYTES = 77056
    arena_t = st.enter_context(nc.sbuf_tensor("arena", [128, AR_BYTES // 4], F32))
    arena = Arena(arena_t, AR_BYTES, tr)

    pb = []
    pbb = []
    for i in range(8):
        t = st.enter_context(nc.psum_tensor(f"pb{i}", [128, 512], F32))
        pb.append(BV(t[:, :], f"pb{i}", 0, 2048, F32))
        pbb.append(BV(t[:, :].bitcast(BF16), f"pb{i}", 0, 2048, BF16))

    def dma(eng, dst, src):
        rd = [src] if isinstance(src, BV) else []
        wr = [dst] if isinstance(dst, BV) else []
        sap = src.ap if isinstance(src, BV) else src
        dap = dst.ap if isinstance(dst, BV) else dst
        return tr.op(eng, lambda e, d=dap, s=sap: e.dma_start(out=d, in_=s),
                     reads=rd, writes=wr, kind="dma")

    def dmac(dst, src, eng="gpsimd"):
        n = dst.ap.shape[-1]
        assert n % 2048 == 0 and src.shape[-1] == n
        dap = dst.ap.rearrange("p (a b) -> p a b", b=2048)
        sap = src.rearrange("p (a b) -> p a b", b=2048)
        return tr.op(eng, lambda e, d=dap, s=sap: e.dma_start(out=d, in_=s),
                     reads=[], writes=[dst], kind="dma")

    def mm(lhs, rhs, out, start, stop, transpose=False):
        if transpose:
            f = lambda e, a=lhs.ap, b=rhs.ap, o=out.ap: e.transpose(o, a, b)
        else:
            f = lambda e, a=lhs.ap, b=rhs.ap, o=out.ap, st_=start, sp=stop: \
                e.matmul(o, a, b, start=st_, stop=sp)
        return tr.op("tensor", f, reads=[lhs, rhs], writes=[out])

    def vop(f, reads, writes, eng="vector"):
        return tr.op(eng, f, reads=reads, writes=writes)

    def tt(out, a, b, op, eng="vector"):
        return vop(lambda e, o=out.ap, x=a.ap, y=b.ap, op=op:
                   e.tensor_tensor(o, x, y, op=op), [a, b], [out], eng)

    def ts(out, a, s1, s2, op0, op1=None, eng="vector"):
        r = [a] + [s for s in (s1, s2) if isinstance(s, BV)]
        a1 = s1.ap if isinstance(s1, BV) else s1
        a2 = s2.ap if isinstance(s2, BV) else s2
        def f(e, o=out.ap, x=a.ap, a1=a1, a2=a2, op0=op0, op1=op1):
            if op1 is None:
                return e.tensor_scalar(o, x, a1, None, op0=op0)
            return e.tensor_scalar(o, x, a1, a2, op0=op0, op1=op1)
        return vop(f, r, [out], eng)

    def stt(out, a, s, b, op0, op1, eng="vector"):
        r = [a, b] + ([s] if isinstance(s, BV) else [])
        sap = s.ap if isinstance(s, BV) else s
        return vop(lambda e, o=out.ap, x=a.ap, sap=sap, y=b.ap, op0=op0, op1=op1:
                   e.scalar_tensor_tensor(o, x, sap, y, op0=op0, op1=op1),
                   r, [out], eng)

    def cp(out, a, eng="vector"):
        return vop(lambda e, o=out.ap, x=a.ap: e.tensor_copy(o, x), [a], [out], eng)

    def act(out, a, func, bias=None, scale=1.0, eng="scalar"):
        r = [a] + [x for x in (bias, scale) if isinstance(x, BV)]
        bap = bias.ap if isinstance(bias, BV) else bias
        sap = scale.ap if isinstance(scale, BV) else scale

        def f(e, o=out.ap, x=a.ap, func=func, bap=bap, sap=sap):
            kw = {}
            if bap is not None:
                kw["bias"] = bap
            return e.activation(o, x, func, scale=sap, **kw)
        return tr.op(eng, f, reads=r, writes=[out])

    def red(out, a, op, absval=False):
        def f(e, o=out.ap, x=a.ap, op=op, av=absval):
            if av:
                return e.tensor_reduce(o, x, axis=AX.X, op=op,
                                       apply_absolute_value=True)
            return e.tensor_reduce(o, x, axis=AX.X, op=op)
        return vop(f, [a], [out])

    def recip(out, a):
        return vop(lambda e, o=out.ap, x=a.ap: e.reciprocal(o, x), [a], [out])

    def mkstream(ring, src_fn, total):
        state = {"i": 0}

        def ensure(upto):
            while state["i"] < min(upto, total):
                i = state["i"]
                dst = ring[i % len(ring)]
                tr.op("gpsimd",
                      lambda e, d=dst.ap, s=src_fn(i): e.dma_start(out=d, in_=s),
                      reads=[], writes=[dst], kind="dma")
                state["i"] += 1
        return ensure

    for dst, src in [(tri_sb, trimask), (psw_sb, pswap), (idb_sb, ident_bf),
                     (idf_sb, ident_f), (onb_sb, ones_bf), (onf_sb, ones_f),
                     (bq_sb, bqkv), (ob_sb, obT), (ivg_sb, invgT), (ivu_sb, invuT),
                     (ivi_sb, inviT), (swg_sb, swgT), (swu_sb, swuT),
                     (swd_sb, swdT), (sel0_sb, sel0), (sel1_sb, sel1),
                     (e_sb, e_b)]:
        dma("sync", dst, src[:, :] if not isinstance(src, BV) else src)
    tr.op("vector", lambda e: e.memset(eps_sb.ap, 1e-5), reads=[], writes=[eps_sb])

    arena.push()
    c_f = arena.alloc(512, F32)
    silu_c = arena.alloc(512, F32)
    silu_cb = arena.alloc(512, BF16)
    wring = [arena.alloc(4 * 1024, BF16) for _ in range(4)]
    pe_sb = [arena.alloc(512, F32) for _ in range(2)]

    dma("sync", c_f.p(0, 4), c_my[:, :])
    act(silu_c.p(0, 4), c_f.p(0, 4), AF.Silu)
    cp(silu_cb.p(0, 4), silu_c.p(0, 4))
    for j in range(4):
        mm(silu_cb.p(0, 4).c(128 * j, 128 * j + 128), idb_sb.p(0, 4).c(0, 4),
           pbb[7].c(4 * j, 4 * j + 4), True, True, transpose=True)
    cp(cT_my, pbb[7].c(0, 16))

    for i in range(4):
        dmac(wring[i], wmod_my[:, 4096 * i:4096 * i + 4096])
    for i in range(24):
        buf = wring[i % 4]
        for sub in range(2):
            bank = pb[(2 * i + sub) % 2]
            for kcp in range(4):
                mm(cT_my.c(4 * kcp, 4 * kcp + 4),
                   buf.c(1024 * kcp + 512 * sub, 1024 * kcp + 512 * sub + 512),
                   bank.p(0, 4), kcp == 0, kcp == 3)
            peb = pe_sb[(2 * i + sub) % 2]
            cp(peb.p(0, 4), bank.p(0, 4))
            dma("sync", mod_part.c(1024 * i + 512 * sub, 1024 * i + 512 * sub + 512),
                peb.p(0, 4))
        if i + 4 < 24:
            dmac(wring[(i + 4) % 4],
                 wmod_my[:, 4096 * (i + 4):4096 * (i + 4) + 4096])

    tr.op("gpsimd",
          lambda e: e.collective_compute(
              "AllReduce", OP.add, replica_groups=[list(range(8))],
              ins=[mod_part.ap], outs=[mod_ar.ap]),
          reads=[mod_part], writes=[mod_ar], kind="cc")
    arena.pop()

    arena.push()
    aring = [arena.alloc(1024, F32) for _ in range(2)]
    tmpa = arena.alloc(32, F32)
    tmpb = arena.alloc(32, F32)
    A_b = arena.alloc(2048, F32)
    B_b = arena.alloc(2048, F32)
    m_t = arena.alloc(512, F32)
    t1_t = arena.alloc(512, F32)
    var_t = arena.alloc(512, F32)
    sq_t = arena.alloc(512, F32)
    ht1 = [arena.alloc(2048, F32) for _ in range(2)]
    xring = [arena.alloc(2048, F32) for _ in range(2)]
    xsq = [arena.alloc(2048, BF16) for _ in range(2)]

    dma("sync", xring[0], xT_my[0:128, :])
    for kc in range(KC):
        buf, qbuf = xring[kc % 2], xsq[kc % 2]
        xbf = h_all.c(S * kc, S * kc + S)
        cp(xbf, buf)
        act(qbuf, buf, AF.Square)
        for tb in range(4):
            mm(onb_sb, xbf.c(512 * tb, 512 * tb + 512), pb[tb],
               kc == 0, kc == KC - 1)
        for tb in range(4):
            mm(onb_sb, qbuf.c(512 * tb, 512 * tb + 512), pb[4 + tb],
               kc == 0, kc == KC - 1)
        if kc + 1 < KC:
            dma("sync", xring[(kc + 1) % 2],
                xT_my[128 * (kc + 1):128 * (kc + 1) + 128, :])
    for tb in range(4):
        Atb, Btb = A_b.c(512 * tb, 512 * tb + 512), B_b.c(512 * tb, 512 * tb + 512)
        ts(m_t, pb[tb], 1.0 / 4096.0, None, OP.mult)
        tt(t1_t, m_t, m_t, OP.mult)
        stt(var_t, pb[4 + tb], 1.0 / 4096.0, t1_t, OP.mult, OP.subtract)
        act(sq_t, var_t, AF.Sqrt, bias=eps_sb)
        recip(Atb, sq_t)
        tt(Btb, m_t, Atb, OP.mult)

    for i in range(24):
        buf = aring[i % 2]
        dma("sync", buf.p(0, 4),
            BV(mod_ar.ap[:, 1024 * i:1024 * i + 1024], "mod_ar",
               1024 * i * 4, (1024 * i + 1024) * 4, F32))
        for j in range(8):
            col = 8 * i + j
            mm(buf.p(0, 4).c(128 * j, 128 * j + 128), e_sb, pb[0].c(col, col + 1),
               True, True)
    cp(modT, pb[0].c(0, 192))
    ts(c1a, modT.c(32, 64), 1.0, None, OP.add)
    ts(c1b, modT.c(128, 160), 1.0, None, OP.add)

    def sign_max(dst, off):
        ts(tmpa, modT.c(off, off + 32), 0.0, None, OP.is_ge)
        ts(tmpa, tmpa, 2.0, -1.0, OP.mult, OP.add)
        act(tmpb, modT.c(off, off + 32), AF.Abs)
        ts(tmpb, tmpb, 0.25, None, OP.max)
        tt(dst, tmpa, tmpb, OP.mult)

    sign_max(ga_e, 64)
    sign_max(gb_e, 160)
    tt(gob, ga_e, ob_sb, OP.mult)
    tt(gbswd, gb_e, swd_sb, OP.mult)

    for kc in range(KC):
        t1 = ht1[kc % 2]
        hchunk = h_all.c(S * kc, S * kc + S)
        tt(t1, hchunk, A_b, OP.mult)
        tt(t1, t1, B_b, OP.subtract)
        act(hchunk, t1, AF.Identity, bias=modT.c(kc, kc + 1),
            scale=c1a.c(kc, kc + 1))
    arena.pop()

    arena.push()
    cos_sb = arena.alloc(2048, F32)
    sin_sb = arena.alloc(2048, F32)
    dma("sync", cos_sb, cos2[:, :])
    dma("sync", sin_sb, sin2s[:, :])
    wqr = [arena.alloc(1024, BF16) for _ in range(4)]
    qpreb = [arena.alloc(2048, BF16) for _ in range(2)]
    kpreb = [arena.alloc(2048, BF16) for _ in range(2)]
    vfmb = [arena.alloc(2048, BF16) for _ in range(2)]
    tcos = arena.alloc(512, F32)
    usin = arena.alloc(512, F32)
    pt_ring = [arena.alloc(512, BF16) for _ in range(4)]
    nsqr = [arena.alloc(512, BF16) for _ in range(2)]
    maxqb = [arena.alloc(8, F32) for _ in range(2)]
    bias_vb = [arena.alloc(1, F32) for _ in range(2)]
    prod_vb = [arena.alloc(1, F32) for _ in range(2)]
    rsig = arena.alloc(512, F32)
    ctmp = arena.alloc(512, F32)
    cpair = [arena.alloc(1024, BF16) for _ in range(2)]

    def wq_src(i):
        hl, t, part = i // 12, (i // 4) % 3, i % 4
        return wqkv[hl, t][:, 1024 * part:1024 * part + 1024]
    ens_wq = mkstream(wqr, wq_src, NHC * 12)
    ens_wq(3)

    for hl in range(NHC):
        hp = hl % 2
        qp, kp, vf = qpreb[hp], kpreb[hp], vfmb[hp]
        mq, bv, pv = maxqb[hp], bias_vb[hp], prod_vb[hp]
        for t, (dst, bcol) in enumerate(((qp, 0), (kp, 16))):
            for kc in range(KC):
                si = hl * 12 + t * 4 + kc // 8
                if kc % 8 == 0:
                    ens_wq(si + 3)
                ws = wqr[si % 4]
                col = 128 * (kc % 8)
                for tb in range(4):
                    mm(ws.c(col, col + 128),
                       h_all.c(S * kc + 512 * tb, S * kc + 512 * tb + 512),
                       pb[tb], kc == 0, kc == KC - 1)
            for tb in range(4):
                ts(dst.c(512 * tb, 512 * tb + 512), pb[tb],
                   bq_sb.c(bcol + hl, bcol + hl + 1), None, OP.add)
        for gi, src in enumerate((qp, kp)):
            for tb in range(4):
                nb = nsqr[tb % 2]
                act(nb, src.c(512 * tb, 512 * tb + 512), AF.Square)
                bank = pb[4 + (4 * gi + tb) % 4]
                mm(onb_sb, nb, bank, True, True)
                red(mq.c(4 * gi + tb, 4 * gi + tb + 1), bank, OP.max)
        tr.op("vector", lambda e, o=pv.ap, x=mq.c(0, 4).ap: e.tensor_reduce(
            o, x, axis=AX.X, op=OP.max), reads=[mq.c(0, 4)], writes=[pv],
            tag=f"redq_h{hl}")
        tr.op("vector", lambda e, o=bv.ap, x=mq.c(4, 8).ap: e.tensor_reduce(
            o, x, axis=AX.X, op=OP.max), reads=[mq.c(4, 8)], writes=[bv],
            tag=f"redk_h{hl}")
        tt(pv, pv, bv, OP.mult)
        act(pv, pv, AF.Sqrt)
        ts(bv, pv, -ISCALE, -ISCALE, OP.mult, OP.add)
        for pre in (qp, kp):
            for tb in range(4):
                bank = pb[4 + tb % 4]
                mm(psw_sb, pre.c(512 * tb, 512 * tb + 512), bank, True, True)
                tt(tcos, pre.c(512 * tb, 512 * tb + 512),
                   cos_sb.c(512 * tb, 512 * tb + 512), OP.mult)
                tt(usin, bank, sin_sb.c(512 * tb, 512 * tb + 512), OP.mult)
                tt(pre.c(512 * tb, 512 * tb + 512), tcos, usin, OP.add)
        for kc in range(KC):
            si = hl * 12 + 8 + kc // 8
            if kc % 8 == 0:
                ens_wq(si + 3)
            ws = wqr[si % 4]
            col = 128 * (kc % 8)
            for tb in range(4):
                mm(ws.c(col, col + 128),
                   h_all.c(S * kc + 512 * tb, S * kc + 512 * tb + 512),
                   pb[tb], kc == 0, kc == KC - 1)
        for tb in range(4):
            cp(vf.c(512 * tb, 512 * tb + 512), pb[tb])
        for t8 in range(16):
            bank = pbb[4 + t8 % 4]
            mm(vf.c(128 * t8, 128 * t8 + 128), idb_sb, bank.c(0, 128),
               True, True, transpose=True)
            cp(vf.c(128 * t8, 128 * t8 + 128), bank.c(0, 128))
        blocks = []
        for qb in range(4):
            nkc = 4 * qb + 4
            for kc in range(nkc):
                qs = max(512 * qb, 128 * kc)
                nn = 512 * (qb + 1) - qs
                blocks.append((qb, kc, qs, nn, qs - 512 * qb, nkc))

        def score_mm(i):
            qb, kc, qs, nn, off, nkc = blocks[i]
            mm(kp.c(128 * kc, 128 * kc + 128), qp.c(qs, qs + nn),
               pb[2 + i % 2].c(0, nn), True, True)

        score_mm(0)
        for i, (qb, kc, qs, nn, off, nkc) in enumerate(blocks):
            sbank = pb[2 + i % 2]
            ptb = pt_ring[i % 4]
            if i + 1 < len(blocks):
                score_mm(i + 1)
            tr.op("scalar", lambda e, o=ptb.c(0, nn).ap, x=sbank.c(0, nn).ap,
                  bvp=bv.ap:
                  e.activation(o, x, AF.Exp, bias=bvp, scale=ISCALE),
                  reads=[sbank.c(0, nn), bv], writes=[ptb.c(0, nn)])
            if kc >= 4 * qb:
                tt(ptb.c(0, 128), ptb.c(0, 128), tri_sb, OP.mult)
            sig_bank = pb[5 + qb % 2]
            ctx_bank = pb[4 + 3 * (qb % 2)]
            mm(onb_sb, ptb.c(0, nn), sig_bank.c(off, off + nn),
               kc == 0, kc == nkc - 1)
            mm(vf.c(128 * kc, 128 * kc + 128), ptb.c(0, nn),
               ctx_bank.c(off, off + nn), kc == 0, kc == nkc - 1)
            if kc == nkc - 1:
                recip(rsig, sig_bank)
                tt(ctmp, ctx_bank, rsig, OP.mult)
                s_slot, t_off = qb // 2, 512 * (qb % 2)
                cb = cpair[s_slot % 2]
                ts(cb.c(t_off, t_off + 512), ctmp,
                   bq_sb.c(32 + hl, 32 + hl + 1), None, OP.add)
                if qb % 2 == 1:
                    lo = (s_slot * 2048 * 1024 + 128 * hl * 1024) * 2
                    dma("sync",
                        BV(ctx_send.ap[s_slot, 128 * hl:128 * hl + 128, :],
                           "ctx_send", lo, lo + 128 * 1024 * 2, BF16),
                        cb)

    arena.pop()

    mq4 = [arena.alloc(512, F32) for _ in range(4)]
    for _mx in mq4:
        tr.op("vector", lambda e, o=_mx.ap: e.memset(o, 0.0), reads=[],
              writes=[_mx])
    arena.push()
    agA = [arena.alloc(1024, BF16) for _ in range(2)]
    agB = [arena.alloc(1024, BF16) for _ in range(2)]
    tsel = arena.alloc(1024, BF16)
    ow_sb = [arena.alloc(4096, BF16) for _ in range(2)]
    xo_t = [arena.alloc(512, F32) for _ in range(4)]
    x2a = [arena.alloc(512, F32) for _ in range(2)]
    x2b = [arena.alloc(512, F32) for _ in range(2)]
    x2sq = [arena.alloc(512, BF16) for _ in range(2)]
    A2 = arena.alloc(1024, F32)
    B2 = arena.alloc(1024, F32)
    m2t = arena.alloc(512, F32)
    t12 = arena.alloc(512, F32)
    var2 = arena.alloc(512, F32)
    sq2 = var2
    h2t1 = [arena.alloc(512, F32) for _ in range(2)]
    xg_d = x2a

    def ctx_own_c(kc):
        return h_all.c(1024 * kc, 1024 * kc + 1024)

    def h2c(tb, m):
        base = 32768 + 16384 * tb + 512 * m
        return h_all.c(base, base + 512)

    dmac(ow_sb[0], ow[0])
    dmac(ow_sb[1], ow[1])
    tr.op("gpsimd",
          lambda e: e.collective_compute(
              "AllGather", OP.bypass,
              replica_groups=[[0, 1], [2, 3], [4, 5], [6, 7]],
              ins=[ctx_send.ap], outs=[ctx_ag.ap]),
          reads=[ctx_send], writes=[ctx_ag], kind="cc")

    for kc in range(KC):
        r, fr = kc // 16, kc % 16
        a, b_ = agA[kc % 2], agB[kc % 2]
        base = ((r * 2 + 0) * 2048 + 128 * fr) * 1024 * 2
        dma("sync", a, BV(ctx_ag.ap[r, 0, 128 * fr:128 * fr + 128, :], "ctx_ag",
                          base, base + 128 * 1024 * 2, BF16))
        base1 = ((r * 2 + 1) * 2048 + 128 * fr) * 1024 * 2
        dma("sync", b_, BV(ctx_ag.ap[r, 1, 128 * fr:128 * fr + 128, :], "ctx_ag",
                           base1, base1 + 128 * 1024 * 2, BF16))
        ts(tsel, b_, sel1_sb, None, OP.mult)
        stt(ctx_own_c(kc), a, sel0_sb, tsel, OP.mult, OP.add)

    for m0 in range(2):
        for tb in range(2):
            dma("sync", xo_t[(2 * m0 + tb) % 4],
                xT_own[128 * m0:128 * m0 + 128, 512 * tb:512 * tb + 512])
    for m in range(KC):
        wbuf = ow_sb[m % 2]
        for tb in range(2):
            bank = pb[(2 * m + tb) % 4]
            for kc in range(KC):
                mm(wbuf.c(128 * kc, 128 * kc + 128),
                   ctx_own_c(kc).c(512 * tb, 512 * tb + 512),
                   bank, kc == 0, kc == KC - 1)
            xo = xo_t[(2 * m + tb) % 4]
            i2 = (2 * m + tb) % 2
            act(x2a[i2], bank, AF.Identity, bias=gob.c(m, m + 1),
                scale=ga_e.c(m, m + 1))
            tt(x2b[i2], x2a[i2], xo, OP.add)
            lo = (m * 128 * 1024 + 512 * tb) * 4
            dma("sync", BV(x2_dram.ap[m, :, 512 * tb:512 * tb + 512], "x2_dram",
                           lo, lo + 512 * 4, F32), x2b[i2])
            cp(h2c(tb, m), x2b[i2])
            act(x2sq[i2], x2b[i2], AF.Square)
            mm(onb_sb, h2c(tb, m), pb[4 + tb], m == 0, m == KC - 1)
            mm(onb_sb, x2sq[i2], pb[6 + tb], m == 0, m == KC - 1)
            if m + 2 < KC:
                dma("sync", xo_t[(2 * m + tb) % 4],
                    xT_own[128 * (m + 2):128 * (m + 2) + 128,
                           512 * tb:512 * tb + 512])
        if m + 2 < KC:
            dmac(ow_sb[m % 2], ow[m + 2])
    for tb in range(2):
        A2tb, B2tb = A2.c(512 * tb, 512 * tb + 512), B2.c(512 * tb, 512 * tb + 512)
        ts(m2t, pb[4 + tb], 1.0 / 4096.0, None, OP.mult)
        tt(t12, m2t, m2t, OP.mult)
        stt(var2, pb[6 + tb], 1.0 / 4096.0, t12, OP.mult, OP.subtract)
        act(sq2, var2, AF.Sqrt, bias=eps_sb)
        recip(A2tb, sq2)
        tt(B2tb, m2t, A2tb, OP.mult)
    for tb in range(2):
        for m in range(KC):
            slot, t1 = h2c(tb, m), h2t1[m % 2]
            tt(t1, slot, A2.c(512 * tb, 512 * tb + 512), OP.mult)
            tt(t1, t1, B2.c(512 * tb, 512 * tb + 512), OP.subtract)
            act(slot, t1, AF.Identity, bias=modT.c(96 + m, 96 + m + 1),
                scale=c1b.c(m, m + 1))
            for q, ivv in ((0, ivg_sb), (1, ivu_sb)):
                mx = mq4[2 * q + tb]
                xb = xg_d[q]
                act(xb, slot, AF.Abs, scale=ivv.c(m, m + 1))
                tt(mx, mx, xb, OP.max)
    arena.pop()

    arena.push()
    xsd = [arena.alloc(512, F32) for _ in range(2)]
    rmaxq = arena.alloc(8, F32)
    srow = arena.alloc(512, F32)
    scb_g = [arena.alloc(512, F32) for _ in range(2)]
    scb_u = [arena.alloc(512, F32) for _ in range(2)]
    rcb = [arena.alloc(512, F32) for _ in range(2)]
    qint = [arena.alloc(512, I32) for _ in range(3)]
    wgr = [arena.alloc(1024, BF16) for _ in range(4)]
    wur = [arena.alloc(1024, BF16) for _ in range(4)]
    wdr = [arena.alloc(1024, BF16) for _ in range(4)]
    g_t = arena.alloc(512, F32)
    s_t = arena.alloc(512, F32)
    u_t = arena.alloc(512, F32)
    ib_t = [arena.alloc(512, BF16) for _ in range(2)]
    d_t = g_t
    y_tb = [s_t, u_t]
    x2r2 = xsd + [arena.alloc(512, F32) for _ in range(2)]

    def qgu_c(q, tb, kc):
        base = (16384 * q if tb == 0 else 32768 + 16384 * q)
        return h_all.c(base + 512 * kc, base + 512 * kc + 512)

    def rb_c(tb, kc):
        base = 32768 + 8192 * tb
        return h_all.c(base + 512 * kc, base + 512 * kc + 512)

    def qi_c(tb, kc):
        base = 8192 * tb
        return h_all.c(base + 512 * kc, base + 512 * kc + 512)

    def idchunk(tb, im):
        lo = (tb * IKC + im) * 128 * 512 * 2
        return BV(inter_dram[tb, im], "inter_dram", lo, lo + 128 * 512 * 2, BF16)

    def mkscales(mx, scb, rc):
        for j in range(4):
            bank = pb[4 + j % 2]
            mm(mx.c(128 * j, 128 * j + 128), idf_sb, bank.c(0, 128),
               True, True, transpose=True)
            red(rmaxq.c(j, j + 1), bank.c(0, 128), OP.max)
        for j in range(4):
            mm(rmaxq.c(j, j + 1), idf_sb,
               pb[6].p(0, 1).c(128 * j, 128 * j + 128), True, True,
               transpose=True)
        cp(srow.p(0, 1), pb[6].p(0, 1).c(0, 512))
        ts(srow.p(0, 1), srow.p(0, 1), 1.0 / 127.0, 1e-8, OP.mult, OP.max)
        mm(onf_sb, srow.p(0, 1), pb[5], True, True)
        cp(scb, pb[5])
        recip(rc, scb)

    def quant_gu(tb):
        for q, (mx, scb, ivv) in enumerate(((mq4[tb], scb_g[tb], ivg_sb),
                                            (mq4[2 + tb], scb_u[tb], ivu_sb))):
            mkscales(mx, scb, rcb[q])
            for kc in range(KC):
                qb = qint[kc % 3]
                stt(qb, h2c(tb, kc), ivv.c(kc, kc + 1), rcb[q],
                    OP.mult, OP.mult)
                dma("gpsimd", qgu_c(q, tb, kc), qb)

    ens_g = mkstream(wgr, lambda i: wg[i // 4][:, 1024 * (i % 4):
                                                1024 * (i % 4) + 1024], 64)
    ens_u = mkstream(wur, lambda i: wu[i // 4][:, 1024 * (i % 4):
                                               1024 * (i % 4) + 1024], 64)
    ens_d = mkstream(wdr, lambda i: wd[i // 2][:, 1024 * (i % 2):
                                               1024 * (i % 2) + 1024], 64)

    quant_gu(0)
    quant_gu(1)

    ens_g(3)
    ens_u(3)
    for im in range(IKC):
        bks = [pb[4 * (im % 2) + j] for j in range(4)]
        for kc in range(KC):
            gs = im * 4 + kc // 8
            if kc % 8 == 0:
                ens_g(gs + 3)
                ens_u(gs + 3)
            wgs, wus = wgr[gs % 4], wur[gs % 4]
            col = 128 * (kc % 8)
            mm(wgs.c(col, col + 128), qgu_c(0, 0, kc), bks[0], kc == 0,
               kc == KC - 1)
            mm(wgs.c(col, col + 128), qgu_c(0, 1, kc), bks[1], kc == 0,
               kc == KC - 1)
            mm(wus.c(col, col + 128), qgu_c(1, 0, kc), bks[2], kc == 0,
               kc == KC - 1)
            mm(wus.c(col, col + 128), qgu_c(1, 1, kc), bks[3], kc == 0,
               kc == KC - 1)
        for tb, mxi in ((0, mq4[0]), (1, mq4[1])):
            stt(g_t, bks[tb], swg_sb.c(im, im + 1), scb_g[tb],
                OP.mult, OP.mult)
            act(s_t, g_t, AF.Silu)
            stt(u_t, bks[2 + tb], swu_sb.c(im, im + 1), scb_u[tb],
                OP.mult, OP.mult)
            tt(u_t, s_t, u_t, OP.mult)
            ib = ib_t[tb]
            ts(ib, u_t, 10.0, -10.0, OP.min, OP.max)
            xb = xsd[tb]
            act(xb, ib, AF.Abs, scale=ivi_sb.c(im, im + 1))
            if im == 0:
                cp(mxi, xb)
            else:
                tt(mxi, mxi, xb, OP.max)
            dma("scalar", idchunk(tb, im), ib)

    scb_i = scb_g
    for tb, mxi in ((0, mq4[0]), (1, mq4[1])):
        mkscales(mxi, scb_i[tb], rcb[tb])
    for tb in range(2):
        for kc in range(IKC):
            rb = rb_c(tb, kc)
            dma("sync", rb, idchunk(tb, kc))
            qb = qint[kc % 3]
            stt(qb, rb, ivi_sb.c(kc, kc + 1), rcb[tb], OP.mult, OP.mult)
            dma("gpsimd", qi_c(tb, kc), qb)

    ens_d(3)
    for m0 in range(2):
        for tb in range(2):
            lo = (m0 * 128 * 1024 + 512 * tb) * 4
            dma("sync", x2r2[(2 * m0 + tb) % 4],
                BV(x2_dram.ap[m0, :, 512 * tb:512 * tb + 512],
                   "x2_dram", lo, lo + 512 * 4, F32))
    for m in range(KC):
        bd = [pb[2 * (m % 2)], pb[2 * (m % 2) + 1]]
        for kci in range(IKC):
            ds = m * 2 + kci // 8
            if kci % 8 == 0:
                ens_d(ds + 3)
            wds = wdr[ds % 4]
            col = 128 * (kci % 8)
            mm(wds.c(col, col + 128), qi_c(0, kci), bd[0], kci == 0,
               kci == IKC - 1)
            mm(wds.c(col, col + 128), qi_c(1, kci), bd[1], kci == 0,
               kci == IKC - 1)
        for tb in range(2):
            xb = x2r2[(2 * m + tb) % 4]
            lo = (m * 128 * 1024 + 512 * tb) * 4
            stt(d_t, bd[tb], gbswd.c(m, m + 1), scb_i[tb], OP.mult, OP.mult)
            yb = y_tb[tb]
            tt(yb, d_t, xb, OP.add)
            dma("sync", BV(yT_my[128 * m:128 * m + 128,
                                 512 * tb:512 * tb + 512],
                           "y_out", lo, lo + 512 * 4, F32), yb)
            if m + 2 < KC:
                lo2 = ((m + 2) * 128 * 1024 + 512 * tb) * 4
                dma("sync", x2r2[(2 * m + tb) % 4],
                    BV(x2_dram.ap[m + 2, :, 512 * tb:512 * tb + 512],
                       "x2_dram", lo2, lo2 + 512 * 4, F32))
    arena.pop()
    tr.wait_all("sync", [y_out])

    tr.finalize()
    with nc.Block() as block:
        @block.sync
        def _(e):
            for fn in tr.q["sync"]:
                fn(e)

        @block.gpsimd
        def _(e):
            for fn in tr.q["gpsimd"]:
                fn(e)

        @block.scalar
        def _(e):
            for fn in tr.q["scalar"]:
                fn(e)

        @block.vector
        def _(e):
            for fn in tr.q["vector"]:
                fn(e)

        @block.tensor
        def _(e):
            for fn in tr.q["tensor"]:
                fn(e)

    st.close()
    return nc


def make_inputs(core, inp):
    b, half = core // 2, core % 2
    lo = 1024 * half
    f32 = np.float32
    x = np.asarray(inp["L0_attn_hidden_states"], f32)
    qkv_w = np.asarray(inp["L0_attn_qkv_w"], f32)
    qkv_b = np.asarray(inp["L0_attn_qkv_b"], f32)
    out_w = np.asarray(inp["L0_attn_out_w"], f32)
    out_b = np.asarray(inp["L0_attn_out_b"], f32)
    wmod = np.asarray(inp["L0_wmod"], f32)
    cc = np.asarray(inp["L0_c"], f32)

    d = {}
    d["xT_my"] = np.ascontiguousarray(x[b].T)
    d["xT_own"] = np.ascontiguousarray(x[b, lo:lo + 1024].T)
    heads = []
    for hl in range(NHC):
        hg = 16 * half + hl
        for t in range(3):
            heads.append(qkv_w[:, E * t + 128 * hg:E * t + 128 * hg + 128]
                         .reshape(KC, 128, 128))
    d["wqkv"] = np.ascontiguousarray(
        np.stack(heads).transpose(0, 2, 1, 3)).reshape(NHC, 3, 128, KC * 128)
    bq = np.zeros((128, 48), f32)
    for hl in range(NHC):
        hg = 16 * half + hl
        bq[:, hl] = qkv_b[128 * hg:128 * hg + 128]
        bq[:, 16 + hl] = qkv_b[E + 128 * hg:E + 128 * hg + 128]
        bq[:, 32 + hl] = qkv_b[2 * E + 128 * hg:2 * E + 128 * hg + 128]
    d["bqkv"] = bq
    d["wmod_my"] = np.ascontiguousarray(
        wmod[512 * core:512 * core + 512, :].reshape(4, 128, 24, 1024)
        .transpose(1, 2, 0, 3)).reshape(128, 24 * 4096)
    d["c_my"] = np.ascontiguousarray(cc[:, 512 * core:512 * core + 512])
    e = np.zeros((4, 1), f32)
    e[b, 0] = 1.0
    d["e_b"] = e
    d["sel0"] = np.full((128, 1), 1.0 if half == 0 else 0.0, f32)
    d["sel1"] = np.full((128, 1), 1.0 if half == 1 else 0.0, f32)
    d["ow"] = np.ascontiguousarray(
        out_w.reshape(KC, 128, KC, 128).transpose(2, 1, 0, 3)).reshape(
        KC, 128, KC * 128)
    d["obT"] = np.ascontiguousarray(out_b.reshape(32, 128).T)
    wgm = np.asarray(inp["L0_ffn_w_gate"], f32)
    wum = np.asarray(inp["L0_ffn_w_up"], f32)
    wdm = np.asarray(inp["L0_ffn_w_down"], f32)
    d["wg"] = np.ascontiguousarray(
        wgm.T.reshape(KC, 128, IKC, 128).transpose(2, 1, 0, 3)).reshape(
        IKC, 128, KC * 128)
    d["wu"] = np.ascontiguousarray(
        wum.T.reshape(KC, 128, IKC, 128).transpose(2, 1, 0, 3)).reshape(
        IKC, 128, KC * 128)
    d["wd"] = np.ascontiguousarray(
        wdm.T.reshape(IKC, 128, KC, 128).transpose(2, 1, 0, 3)).reshape(
        KC, 128, IKC * 128)
    d["invgT"] = np.ascontiguousarray(
        np.asarray(inp["L0_ffn_inv_gate"], f32).reshape(KC, 128).T)
    d["invuT"] = np.ascontiguousarray(
        np.asarray(inp["L0_ffn_inv_up"], f32).reshape(KC, 128).T)
    d["inviT"] = np.ascontiguousarray(
        np.asarray(inp["L0_ffn_inv_inter"], f32).reshape(IKC, 128).T)
    d["swgT"] = np.ascontiguousarray(
        np.asarray(inp["L0_ffn_s_wgate"], f32).reshape(IKC, 128).T)
    d["swuT"] = np.ascontiguousarray(
        np.asarray(inp["L0_ffn_s_wup"], f32).reshape(IKC, 128).T)
    d["swdT"] = np.ascontiguousarray(
        np.asarray(inp["L0_ffn_s_wdown"], f32).reshape(KC, 128).T)
    return d


def make_consts():
    f32 = np.float32
    j = np.arange(64, dtype=np.float64)
    invf = 1.0 / (10000.0 ** (j / 64.0))
    t = np.arange(S, dtype=np.float64)
    ang = np.outer(invf, t)
    cos = np.cos(ang)
    sin = np.sin(ang)
    cos2 = np.concatenate([cos, cos], 0).astype(f32)
    sin2s = np.concatenate([-sin, sin], 0).astype(f32)
    tri = (np.arange(128)[:, None] <= np.arange(128)[None, :])
    psw = np.zeros((128, 128), np.float32)
    for i in range(128):
        psw[(i + 64) % 128, i] = 1.0
    return {"cos2": cos2, "sin2s": sin2s,
            "trimask": tri.astype(ml_dtypes.bfloat16),
            "pswap": psw.astype(ml_dtypes.bfloat16),
            "ident_bf": np.eye(128).astype(ml_dtypes.bfloat16),
            "ident_f": np.eye(128, dtype=f32),
            "ones_bf": np.ones((128, 128), ml_dtypes.bfloat16),
            "ones_f": np.ones((1, 128), f32)}


_CACHE = {}


def kernel(**inputs):
    if "nc" not in _CACHE:
        _CACHE["nc"] = build_program()
    nc = _CACHE["nc"]
    consts = make_consts()
    in_maps = []
    for c in range(8):
        m = make_inputs(c, inputs)
        m.update(consts)
        in_maps.append(m)
    res = run_bass_kernel_spmd(nc, in_maps, core_ids=list(range(8)))
    y = np.empty((B, S, E), np.float32)
    for c in range(8):
        b, half = c // 2, c % 2
        lo = 1024 * half
        y[b, lo:lo + 1024] = res.results[c]["yT_my"].T
    return y


# version 13
# speedup vs baseline: 1.1137x; 1.0390x over previous
import numpy as np
import ml_dtypes
import concourse.bass as bass
import concourse.mybir as mybir
from concourse.bass_utils import run_bass_kernel_spmd
from contextlib import ExitStack

F32 = mybir.dt.float32
BF16 = mybir.dt.bfloat16
I32 = mybir.dt.int32
AF = mybir.ActivationFunctionType
OP = mybir.AluOpType
AX = mybir.AxisListType

B, S, E, D, NHC, II = 4, 2048, 4096, 128, 16, 2048
KC = E // 128
IKC = II // 128
ISCALE = float(1.0 / np.sqrt(128.0))
ENGINES = ("sync", "gpsimd", "scalar", "vector", "tensor")
DBG = False
FULL_WAITS = False
PSUM_EXCL = True

_IS = {F32: 4, BF16: 2, I32: 4}


class BV:

    def __init__(self, ap, key, lo, hi, dt):
        self.ap, self.key, self.lo, self.hi, self.dt = ap, key, lo, hi, dt

    def c(self, a, b):
        isz = _IS[self.dt]
        return BV(self.ap[:, a:b], self.key, self.lo + a * isz, self.lo + b * isz,
                  self.dt)

    def p(self, a, b):
        return BV(self.ap[a:b], self.key, self.lo, self.hi, self.dt)


class Tracker:
    def __init__(self, st, nc):
        self.nc = nc
        self.q = {e: [] for e in ENGINES}
        self.esem = {e: st.enter_context(nc.semaphore("es_" + e)) for e in ENGINES}
        self.NDQ = 8
        self.dsem = {e: [st.enter_context(nc.semaphore(f"ds_{e}{i}"))
                         for i in range(self.NDQ)]
                     for e in ("sync", "gpsimd", "scalar")}
        self.ccsem = st.enter_context(nc.semaphore("ccsem"))
        self.cnt = {e: 0 for e in ENGINES}
        self.dcnt = {e: [0] * self.NDQ for e in ("sync", "gpsimd", "scalar")}
        self.didx = {e: 0 for e in ("sync", "gpsimd", "scalar")}
        self.ccnt = 0
        self.reg = {}
        self.sat = {e: {} for e in ENGINES}
        self.snap = {e: {} for e in ENGINES}
        self.semobj = {}

    def _semname(self, sem):
        for e, s in self.esem.items():
            if s is sem:
                return "e_" + e
        for e, ss in self.dsem.items():
            for i, s in enumerate(ss):
                if s is sem:
                    return f"d_{e}{i}"
        return "cc"

    def _sid(self, sem):
        i = id(sem)
        self.semobj[i] = sem
        return i

    def _hazards(self, eng, reads, writes):
        need = {}

        def add(marker):
            if marker is None:
                return
            sem, val, owner = marker
            if owner == eng == "tensor" and sem is self.esem[eng]:
                return
            sid = self._sid(sem)
            if FULL_WAITS:
                if need.get(sid, (None, 0))[1] < val:
                    need[sid] = (sem, val)
                return
            if self.sat[eng].get(sid, 0) >= val:
                return
            if need.get(sid, (None, 0))[1] < val:
                need[sid] = (sem, val)
        for bv in reads:
            for iv in self.reg.get(bv.key, []):
                if iv[0] < bv.hi and bv.lo < iv[1]:
                    add(iv[2])
        for bv in writes:
            for iv in self.reg.get(bv.key, []):
                if iv[0] < bv.hi and bv.lo < iv[1]:
                    add(iv[2])
                    for r in iv[3]:
                        add(r)
        return [v for _, v in need.items()]

    def _record(self, eng, reads, writes, marker):
        for bv in reads:
            lst = self.reg.setdefault(bv.key, [])
            hit = False
            for iv in lst:
                if iv[0] == bv.lo and iv[1] == bv.hi:
                    iv[3].append(marker)
                    hit = True
                    break
            if not hit:
                lst.append([bv.lo, bv.hi, None, [marker]])
        for bv in writes:
            lst = self.reg.setdefault(bv.key, [])
            keep = []
            for iv in lst:
                if iv[0] < bv.hi and bv.lo < iv[1]:
                    if iv[0] < bv.lo:
                        keep.append([iv[0], bv.lo, iv[2], list(iv[3])])
                    if bv.hi < iv[1]:
                        keep.append([bv.hi, iv[1], iv[2], list(iv[3])])
                else:
                    keep.append(iv)
            keep.append([bv.lo, bv.hi, marker, []])
            self.reg[bv.key] = keep

    def op(self, eng, f, reads=(), writes=(), kind="compute", tag=None):
        if PSUM_EXCL:
            psr = [b for b in reads if b.key.startswith("pb")]
            if psr:
                reads = [b for b in reads if not b.key.startswith("pb")]
                writes = list(writes) + psr
        waits = self._hazards(eng, reads, writes)
        if kind == "compute":
            sem = self.esem[eng]
            self.cnt[eng] += 1
            val, k = self.cnt[eng], 1
        elif kind == "dma":
            slot = self.didx[eng] % self.NDQ
            self.didx[eng] += 1
            sem = self.dsem[eng][slot]
            prior = self.dcnt[eng][slot]
            if prior:
                sid = self._sid(sem)
                if self.sat[eng].get(sid, 0) < prior and not any(
                        s is sem and v >= prior for s, v in waits):
                    waits = list(waits) + [(sem, prior)]
            self.dcnt[eng][slot] = prior + 16
            val, k = prior + 16, 16
        else:
            sem = self.ccsem
            self.ccnt += 1
            val, k = self.ccnt, 1
        marker = (sem, val, eng)
        if not hasattr(self, "log"):
            self.log = []
        self.log.append((eng, kind, val, tag,
                         [(self._semname(s), v) for s, v in waits],
                         [(b.key, b.lo, b.hi) for b in reads],
                         [(b.key, b.lo, b.hi) for b in writes],
                         self._semname(sem)))
        self.q[eng].append((f, tuple(waits), sem, k, val, kind))
        for s, v in waits:
            sid = self._sid(s)
            self.sat[eng][sid] = max(self.sat[eng].get(sid, 0), v)
            for oe in ENGINES:
                if s is self.esem[oe]:
                    snapd = self.snap[oe].get(v)
                    if snapd:
                        for k2, v2 in snapd.items():
                            self.sat[eng][k2] = max(self.sat[eng].get(k2, 0), v2)
        if kind == "compute":
            self.snap[eng][val] = dict(self.sat[eng])
            self.snap[eng][val][self._sid(sem)] = val
        self._record(eng, reads, writes, marker)
        return marker

    def finalize(self):
        waited = {self._sid(s): set() for s in self.esem.values()}
        for eng in ENGINES:
            for (f, waits, sem, k, val, kind) in self.q[eng]:
                for s, v in waits:
                    sid = self._sid(s)
                    if sid in waited:
                        waited[sid].add(v)
        remap = {}
        newq = {e: [] for e in ENGINES}
        for eng in ENGINES:
            sid = self._sid(self.esem[eng])
            compact = 0
            for (f, waits, sem, k, val, kind) in self.q[eng]:
                inc = kind != "wait"
                if kind == "compute":
                    if val in waited[sid]:
                        compact += 1
                        remap[(sid, val)] = compact
                    else:
                        inc = False
                newq[eng].append((f, waits, sem, k, inc))
        out = {e: [] for e in ENGINES}
        for eng in ENGINES:
            for (f, waits, sem, k, inc) in newq[eng]:
                w2 = []
                for s, v in waits:
                    sid = self._sid(s)
                    if (sid, v) in remap:
                        w2.append((s, remap[(sid, v)]))
                    elif sid in waited:
                        assert False, f"wait on un-inc'd marker {v}"
                    else:
                        w2.append((s, v))

                def fn(e, f=f, w2=tuple(w2), sem=sem, k=k, inc=inc):
                    for s, v in w2:
                        e.wait_ge(s, v)
                    if f is not None:
                        inst = f(e)
                        if inc:
                            inst.then_inc(sem, k)
                out[eng].append(fn)
        self.q = out

    def wait_all(self, eng, bvs):
        need = self._hazards(eng, bvs, bvs)
        for s, v in need:
            self.q[eng].append((None, ((s, v),), None, 0, 0, "wait"))
            self.sat[eng][self._sid(s)] = max(self.sat[eng].get(self._sid(s), 0), v)


class Arena:
    def __init__(self, t, nbytes, tr):
        self.t, self.n, self.off, self.stack = t, nbytes, 0, []
        self.tr = tr
        self.nalloc = 0

    def alloc(self, cols, dtype):
        isz = _IS[dtype]
        off = (self.off + 63) // 64 * 64
        sz = cols * isz
        assert off + sz <= self.n, f"arena overflow: {off}+{sz} > {self.n}"
        self.off = off + sz
        view = self.t[:, off // 4:(off + sz + 3) // 4]
        if dtype is not F32:
            view = view.bitcast(dtype)
        self.nalloc += 1
        return BV(view[:, 0:cols], "arena", off, off + sz, dtype)

    def push(self):
        self.stack.append(self.off)

    def pop(self):
        self.off = self.stack.pop()


def build_program():
    nc = bass.Bass("TRN2", target_bir_lowering=False, debug=False,
                   detect_race_conditions=False)
    st = ExitStack()
    tr = Tracker(st, nc)

    def din(name, shape, dt=F32):
        ap = nc.dram_tensor(name, shape, dt, kind="ExternalInput").ap()
        return ap

    xT_my = din("xT_my", [E, S])
    xT_own = din("xT_own", [E, 1024])
    wqkv = din("wqkv", [NHC, 3, 128, KC * 128])
    bqkv = din("bqkv", [128, 48])
    wmod_my = din("wmod_my", [128, 24 * 4096])
    c_my = din("c_my", [4, 512])
    e_b = din("e_b", [4, 1])
    sel0 = din("sel0", [128, 1])
    sel1 = din("sel1", [128, 1])
    ow = din("ow", [KC, 128, KC * 128])
    obT = din("obT", [128, 32])
    wg = din("wg", [IKC, 128, KC * 128])
    wu = din("wu", [IKC, 128, KC * 128])
    wd = din("wd", [KC, 128, IKC * 128])
    invgT = din("invgT", [128, 32])
    invuT = din("invuT", [128, 32])
    inviT = din("inviT", [128, 16])
    swgT = din("swgT", [128, 16])
    swuT = din("swuT", [128, 16])
    swdT = din("swdT", [128, 32])
    cos2 = din("cos2", [128, S])
    sin2s = din("sin2s", [128, S])
    trimask = din("trimask", [128, 128], BF16)
    pswap = din("pswap", [128, 128], BF16)
    ident_bf = din("ident_bf", [128, 128], BF16)
    ident_f = din("ident_f", [128, 128], F32)
    ones_bf = din("ones_bf", [128, 128], BF16)
    ones_f = din("ones_f", [1, 128], F32)
    yT_my = nc.dram_tensor("yT_my", [E, 1024], F32, kind="ExternalOutput").ap()

    dbg = {}
    if DBG:
        for nm, shp in [("d_modT", [128, 192]), ("d_h0", [128, S], ),
                        ("d_q0", [128, S]), ("d_k0", [128, S]), ("d_v0", [128, S]),
                        ("d_ctx0", [128, 512]), ("d_bias0", [128, 1]),
                        ("d_ctxown", [128, 1024]), ("d_x2", [128, 1024]),
                        ("d_h2", [128, 1024]), ("d_qg0", [128, 512]),
                        ("d_inter0", [128, 512]), ("d_sgb", [128, 512]),
                        ("d_nsq", [128, 512]), ("d_norm", [128, 512]),
                        ("d_maxq", [128, 8]), ("d_tail", [4, 128]), ("d_ball", [16, 128]), ("d_sig", [16, 128]), ("d_maxq1", [128, 8]), ("d_nsq1", [128, 512]), ("d_tail1", [4, 128]), ("d_maxq2", [128, 8]), ("d_norm2", [128, 512]), ("d_nsq2", [128, 512])]:
            dbg[nm] = nc.dram_tensor(nm, shp, F32, kind="ExternalOutput").ap()

    mod_part_t = nc.dram_tensor("mod_part", [4, 6 * E], F32)
    mod_ar_t = nc.dram_tensor("mod_ar", [4, 6 * E], F32)
    inter_dram_t = nc.dram_tensor("inter_dram", [2, IKC, 128, 512], BF16)
    ctx_send_t = nc.dram_tensor("ctx_send", [4, 2, 512, 1024], BF16)
    ctx_ag_t = nc.dram_tensor("ctx_ag", [4, 2, 2, 512, 1024], BF16)
    x2_dram_t = nc.dram_tensor("x2_dram", [KC, 128, 1024], F32)
    inter_dram = inter_dram_t.ap()
    mod_part = BV(mod_part_t.ap(), "mod_part", 0, 4 * 6 * E * 4, F32)
    mod_ar = BV(mod_ar_t.ap(), "mod_ar", 0, 4 * 6 * E * 4, F32)
    ctx_send = BV(ctx_send_t.ap(), "ctx_send", 0, 2 * 2048 * 1024 * 2, BF16)
    ctx_ag = BV(ctx_ag_t.ap(), "ctx_ag", 0, 2 * 2 * 2048 * 1024 * 2, BF16)
    x2_dram = BV(x2_dram_t.ap(), "x2_dram", 0, KC * 128 * 1024 * 4, F32)
    y_out = BV(yT_my, "y_out", 0, E * 1024 * 4, F32)

    def sbv(name, shape, dt):
        t = st.enter_context(nc.sbuf_tensor(name, shape, dt))
        return BV(t[:, :] if len(shape) == 2 else t[:], name,
                  0, shape[-1] * _IS[dt], dt)

    h_all = sbv("h_all", [128, KC * S], BF16)
    tri_sb = sbv("tri_sb", [128, 128], BF16)
    psw_sb = sbv("psw_sb", [128, 128], BF16)
    idb_sb = sbv("idb_sb", [128, 128], BF16)
    idf_sb = sbv("idf_sb", [128, 128], F32)
    onb_sb = sbv("onb_sb", [128, 128], BF16)
    onf_sb = sbv("onf_sb", [1, 128], F32)
    bq_sb = sbv("bq_sb", [128, 48], F32)
    ob_sb = sbv("ob_sb", [128, 32], F32)
    modT = sbv("modT", [128, 192], F32)
    c1a = sbv("c1a", [128, 32], F32)
    c1b = sbv("c1b", [128, 32], F32)
    ga_e = sbv("ga_e", [128, 32], F32)
    gob = sbv("gob", [128, 32], F32)
    gb_e = sbv("gb_e", [128, 32], F32)
    gbswd = sbv("gbswd", [128, 32], F32)
    ivg_sb = sbv("ivg_sb", [128, 32], F32)
    ivu_sb = sbv("ivu_sb", [128, 32], F32)
    ivi_sb = sbv("ivi_sb", [128, 16], F32)
    swg_sb = sbv("swg_sb", [128, 16], F32)
    swu_sb = sbv("swu_sb", [128, 16], F32)
    swd_sb = sbv("swd_sb", [128, 32], F32)
    sel0_sb = sbv("sel0_sb", [128, 1], F32)
    sel1_sb = sbv("sel1_sb", [128, 1], F32)
    e_sb = sbv("e_sb", [4, 1], F32)
    cT_my = sbv("cT_my", [128, 16], BF16)
    eps_sb = sbv("eps_sb", [128, 1], F32)

    AR_BYTES = 77056
    arena_t = st.enter_context(nc.sbuf_tensor("arena", [128, AR_BYTES // 4], F32))
    arena = Arena(arena_t, AR_BYTES, tr)

    pb = []
    pbb = []
    for i in range(8):
        t = st.enter_context(nc.psum_tensor(f"pb{i}", [128, 512], F32))
        pb.append(BV(t[:, :], f"pb{i}", 0, 2048, F32))
        pbb.append(BV(t[:, :].bitcast(BF16), f"pb{i}", 0, 2048, BF16))

    def dma(eng, dst, src):
        rd = [src] if isinstance(src, BV) else []
        wr = [dst] if isinstance(dst, BV) else []
        sap = src.ap if isinstance(src, BV) else src
        dap = dst.ap if isinstance(dst, BV) else dst
        return tr.op(eng, lambda e, d=dap, s=sap: e.dma_start(out=d, in_=s),
                     reads=rd, writes=wr, kind="dma")

    def dmac(dst, src, eng="gpsimd"):
        n = dst.ap.shape[-1]
        assert n % 2048 == 0 and src.shape[-1] == n
        dap = dst.ap.rearrange("p (a b) -> p a b", b=2048)
        sap = src.rearrange("p (a b) -> p a b", b=2048)
        return tr.op(eng, lambda e, d=dap, s=sap: e.dma_start(out=d, in_=s),
                     reads=[], writes=[dst], kind="dma")

    def mm(lhs, rhs, out, start, stop, transpose=False):
        if transpose:
            f = lambda e, a=lhs.ap, b=rhs.ap, o=out.ap: e.transpose(o, a, b)
        else:
            f = lambda e, a=lhs.ap, b=rhs.ap, o=out.ap, st_=start, sp=stop: \
                e.matmul(o, a, b, start=st_, stop=sp)
        return tr.op("tensor", f, reads=[lhs, rhs], writes=[out])

    def vop(f, reads, writes, eng="vector"):
        return tr.op(eng, f, reads=reads, writes=writes)

    def tt(out, a, b, op, eng="vector"):
        return vop(lambda e, o=out.ap, x=a.ap, y=b.ap, op=op:
                   e.tensor_tensor(o, x, y, op=op), [a, b], [out], eng)

    def ts(out, a, s1, s2, op0, op1=None, eng="vector"):
        r = [a] + [s for s in (s1, s2) if isinstance(s, BV)]
        a1 = s1.ap if isinstance(s1, BV) else s1
        a2 = s2.ap if isinstance(s2, BV) else s2
        def f(e, o=out.ap, x=a.ap, a1=a1, a2=a2, op0=op0, op1=op1):
            if op1 is None:
                return e.tensor_scalar(o, x, a1, None, op0=op0)
            return e.tensor_scalar(o, x, a1, a2, op0=op0, op1=op1)
        return vop(f, r, [out], eng)

    def stt(out, a, s, b, op0, op1, eng="vector"):
        r = [a, b] + ([s] if isinstance(s, BV) else [])
        sap = s.ap if isinstance(s, BV) else s
        return vop(lambda e, o=out.ap, x=a.ap, sap=sap, y=b.ap, op0=op0, op1=op1:
                   e.scalar_tensor_tensor(o, x, sap, y, op0=op0, op1=op1),
                   r, [out], eng)

    def cp(out, a, eng="vector"):
        return vop(lambda e, o=out.ap, x=a.ap: e.tensor_copy(o, x), [a], [out], eng)

    def act(out, a, func, bias=None, scale=1.0, eng="scalar"):
        r = [a] + [x for x in (bias, scale) if isinstance(x, BV)]
        bap = bias.ap if isinstance(bias, BV) else bias
        sap = scale.ap if isinstance(scale, BV) else scale

        def f(e, o=out.ap, x=a.ap, func=func, bap=bap, sap=sap):
            kw = {}
            if bap is not None:
                kw["bias"] = bap
            return e.activation(o, x, func, scale=sap, **kw)
        return tr.op(eng, f, reads=r, writes=[out])

    def red(out, a, op, absval=False):
        def f(e, o=out.ap, x=a.ap, op=op, av=absval):
            if av:
                return e.tensor_reduce(o, x, axis=AX.X, op=op,
                                       apply_absolute_value=True)
            return e.tensor_reduce(o, x, axis=AX.X, op=op)
        return vop(f, [a], [out])

    def recip(out, a):
        return vop(lambda e, o=out.ap, x=a.ap: e.reciprocal(o, x), [a], [out])

    def mkstream(ring, src_fn, total):
        state = {"i": 0}

        def ensure(upto):
            while state["i"] < min(upto, total):
                i = state["i"]
                dst = ring[i % len(ring)]
                tr.op("gpsimd",
                      lambda e, d=dst.ap, s=src_fn(i): e.dma_start(out=d, in_=s),
                      reads=[], writes=[dst], kind="dma")
                state["i"] += 1
        return ensure

    for dst, src in [(tri_sb, trimask), (psw_sb, pswap), (idb_sb, ident_bf),
                     (idf_sb, ident_f), (onb_sb, ones_bf), (onf_sb, ones_f),
                     (bq_sb, bqkv), (ob_sb, obT), (ivg_sb, invgT), (ivu_sb, invuT),
                     (ivi_sb, inviT), (swg_sb, swgT), (swu_sb, swuT),
                     (swd_sb, swdT), (sel0_sb, sel0), (sel1_sb, sel1),
                     (e_sb, e_b)]:
        dma("sync", dst, src[:, :] if not isinstance(src, BV) else src)
    tr.op("vector", lambda e: e.memset(eps_sb.ap, 1e-5), reads=[], writes=[eps_sb])

    arena.push()
    c_f = arena.alloc(512, F32)
    silu_c = arena.alloc(512, F32)
    silu_cb = arena.alloc(512, BF16)
    wring = [arena.alloc(4 * 1024, BF16) for _ in range(4)]
    pe_sb = [arena.alloc(512, F32) for _ in range(2)]

    dma("sync", c_f.p(0, 4), c_my[:, :])
    act(silu_c.p(0, 4), c_f.p(0, 4), AF.Silu)
    cp(silu_cb.p(0, 4), silu_c.p(0, 4))
    for j in range(4):
        mm(silu_cb.p(0, 4).c(128 * j, 128 * j + 128), idb_sb.p(0, 4).c(0, 4),
           pbb[7].c(4 * j, 4 * j + 4), True, True, transpose=True)
    cp(cT_my, pbb[7].c(0, 16))

    for i in range(4):
        dmac(wring[i], wmod_my[:, 4096 * i:4096 * i + 4096])
    for i in range(24):
        buf = wring[i % 4]
        for sub in range(2):
            bank = pb[(2 * i + sub) % 2]
            for kcp in range(4):
                mm(cT_my.c(4 * kcp, 4 * kcp + 4),
                   buf.c(1024 * kcp + 512 * sub, 1024 * kcp + 512 * sub + 512),
                   bank.p(0, 4), kcp == 0, kcp == 3)
            peb = pe_sb[(2 * i + sub) % 2]
            cp(peb.p(0, 4), bank.p(0, 4))
            dma("sync", mod_part.c(1024 * i + 512 * sub, 1024 * i + 512 * sub + 512),
                peb.p(0, 4))
        if i + 4 < 24:
            dmac(wring[(i + 4) % 4],
                 wmod_my[:, 4096 * (i + 4):4096 * (i + 4) + 4096])

    tr.op("gpsimd",
          lambda e: e.collective_compute(
              "AllReduce", OP.add, replica_groups=[list(range(8))],
              ins=[mod_part.ap], outs=[mod_ar.ap]),
          reads=[mod_part], writes=[mod_ar], kind="cc")
    arena.pop()

    arena.push()
    aring = [arena.alloc(1024, F32) for _ in range(2)]
    tmpa = arena.alloc(32, F32)
    tmpb = arena.alloc(32, F32)
    A_b = arena.alloc(2048, F32)
    B_b = arena.alloc(2048, F32)
    m_t = arena.alloc(512, F32)
    t1_t = arena.alloc(512, F32)
    var_t = arena.alloc(512, F32)
    sq_t = arena.alloc(512, F32)
    ht1 = [arena.alloc(2048, F32) for _ in range(2)]
    xring = [arena.alloc(2048, F32) for _ in range(2)]
    xsq = [arena.alloc(2048, BF16) for _ in range(2)]

    dma("sync", xring[0], xT_my[0:128, :])
    for kc in range(KC):
        buf, qbuf = xring[kc % 2], xsq[kc % 2]
        xbf = h_all.c(S * kc, S * kc + S)
        cp(xbf, buf)
        act(qbuf, buf, AF.Square)
        for tb in range(4):
            mm(onb_sb, xbf.c(512 * tb, 512 * tb + 512), pb[tb],
               kc == 0, kc == KC - 1)
        for tb in range(4):
            mm(onb_sb, qbuf.c(512 * tb, 512 * tb + 512), pb[4 + tb],
               kc == 0, kc == KC - 1)
        if kc + 1 < KC:
            dma("sync", xring[(kc + 1) % 2],
                xT_my[128 * (kc + 1):128 * (kc + 1) + 128, :])
    for tb in range(4):
        Atb, Btb = A_b.c(512 * tb, 512 * tb + 512), B_b.c(512 * tb, 512 * tb + 512)
        ts(m_t, pb[tb], 1.0 / 4096.0, None, OP.mult)
        tt(t1_t, m_t, m_t, OP.mult)
        stt(var_t, pb[4 + tb], 1.0 / 4096.0, t1_t, OP.mult, OP.subtract)
        act(sq_t, var_t, AF.Sqrt, bias=eps_sb)
        recip(Atb, sq_t)
        tt(Btb, m_t, Atb, OP.mult)

    for i in range(24):
        buf = aring[i % 2]
        dma("sync", buf.p(0, 4),
            BV(mod_ar.ap[:, 1024 * i:1024 * i + 1024], "mod_ar",
               1024 * i * 4, (1024 * i + 1024) * 4, F32))
        for j in range(8):
            col = 8 * i + j
            mm(buf.p(0, 4).c(128 * j, 128 * j + 128), e_sb, pb[0].c(col, col + 1),
               True, True)
    cp(modT, pb[0].c(0, 192))
    ts(c1a, modT.c(32, 64), 1.0, None, OP.add)
    ts(c1b, modT.c(128, 160), 1.0, None, OP.add)

    def sign_max(dst, off):
        ts(tmpa, modT.c(off, off + 32), 0.0, None, OP.is_ge)
        ts(tmpa, tmpa, 2.0, -1.0, OP.mult, OP.add)
        act(tmpb, modT.c(off, off + 32), AF.Abs)
        ts(tmpb, tmpb, 0.25, None, OP.max)
        tt(dst, tmpa, tmpb, OP.mult)

    sign_max(ga_e, 64)
    sign_max(gb_e, 160)
    tt(gob, ga_e, ob_sb, OP.mult)
    tt(gbswd, gb_e, swd_sb, OP.mult)

    for kc in range(KC):
        t1 = ht1[kc % 2]
        hchunk = h_all.c(S * kc, S * kc + S)
        tt(t1, hchunk, A_b, OP.mult)
        tt(t1, t1, B_b, OP.subtract)
        act(hchunk, t1, AF.Identity, bias=modT.c(kc, kc + 1),
            scale=c1a.c(kc, kc + 1))
    arena.pop()

    arena.push()
    cos_sb = arena.alloc(2048, F32)
    sin_sb = arena.alloc(2048, F32)
    dma("sync", cos_sb, cos2[:, :])
    dma("sync", sin_sb, sin2s[:, :])
    wqr = [arena.alloc(1024, BF16) for _ in range(4)]
    qpreb = [arena.alloc(2048, BF16) for _ in range(2)]
    kpreb = [arena.alloc(2048, BF16) for _ in range(2)]
    vfmb = [arena.alloc(2048, BF16) for _ in range(2)]
    tcos = arena.alloc(512, F32)
    usin = arena.alloc(512, F32)
    pt_ring = [arena.alloc(512, BF16) for _ in range(4)]
    nsqr = [arena.alloc(512, BF16) for _ in range(2)]
    maxqb = [arena.alloc(8, F32) for _ in range(2)]
    bias_vb = [arena.alloc(1, F32) for _ in range(2)]
    prod_vb = [arena.alloc(1, F32) for _ in range(2)]
    rsig = arena.alloc(512, F32)
    ctmp = arena.alloc(512, F32)
    cpair = [arena.alloc(1024, BF16) for _ in range(2)]

    def wq_src(i):
        hl, t, part = i // 12, (i // 4) % 3, i % 4
        return wqkv[hl, t][:, 1024 * part:1024 * part + 1024]
    ens_wq = mkstream(wqr, wq_src, NHC * 12)
    ens_wq(3)

    for hl in range(NHC):
        hp = hl % 2
        qp, kp, vf = qpreb[hp], kpreb[hp], vfmb[hp]
        mq, bv, pv = maxqb[hp], bias_vb[hp], prod_vb[hp]
        for t, (dst, bcol) in enumerate(((qp, 0), (kp, 16))):
            for kc in range(KC):
                si = hl * 12 + t * 4 + kc // 8
                if kc % 8 == 0:
                    ens_wq(si + 3)
                ws = wqr[si % 4]
                col = 128 * (kc % 8)
                for tb in range(4):
                    mm(ws.c(col, col + 128),
                       h_all.c(S * kc + 512 * tb, S * kc + 512 * tb + 512),
                       pb[tb], kc == 0, kc == KC - 1)
            for tb in range(4):
                ts(dst.c(512 * tb, 512 * tb + 512), pb[tb],
                   bq_sb.c(bcol + hl, bcol + hl + 1), None, OP.add)
        for gi, src in enumerate((qp, kp)):
            for tb in range(4):
                nb = nsqr[tb % 2]
                act(nb, src.c(512 * tb, 512 * tb + 512), AF.Square)
                bank = pb[4 + (4 * gi + tb) % 4]
                mm(onb_sb, nb, bank, True, True)
                red(mq.c(4 * gi + tb, 4 * gi + tb + 1), bank, OP.max)
        tr.op("vector", lambda e, o=pv.ap, x=mq.c(0, 4).ap: e.tensor_reduce(
            o, x, axis=AX.X, op=OP.max), reads=[mq.c(0, 4)], writes=[pv],
            tag=f"redq_h{hl}")
        tr.op("vector", lambda e, o=bv.ap, x=mq.c(4, 8).ap: e.tensor_reduce(
            o, x, axis=AX.X, op=OP.max), reads=[mq.c(4, 8)], writes=[bv],
            tag=f"redk_h{hl}")
        tt(pv, pv, bv, OP.mult)
        act(pv, pv, AF.Sqrt)
        ts(bv, pv, -ISCALE, -ISCALE, OP.mult, OP.add)
        for pre in (qp, kp):
            for tb in range(4):
                bank = pb[4 + tb % 4]
                mm(psw_sb, pre.c(512 * tb, 512 * tb + 512), bank, True, True)
                tt(tcos, pre.c(512 * tb, 512 * tb + 512),
                   cos_sb.c(512 * tb, 512 * tb + 512), OP.mult)
                tt(usin, bank, sin_sb.c(512 * tb, 512 * tb + 512), OP.mult)
                tt(pre.c(512 * tb, 512 * tb + 512), tcos, usin, OP.add)
        for kc in range(KC):
            si = hl * 12 + 8 + kc // 8
            if kc % 8 == 0:
                ens_wq(si + 3)
            ws = wqr[si % 4]
            col = 128 * (kc % 8)
            for tb in range(4):
                mm(ws.c(col, col + 128),
                   h_all.c(S * kc + 512 * tb, S * kc + 512 * tb + 512),
                   pb[tb], kc == 0, kc == KC - 1)
        for tb in range(4):
            cp(vf.c(512 * tb, 512 * tb + 512), pb[tb])
        for t8 in range(16):
            bank = pbb[4 + t8 % 4]
            mm(vf.c(128 * t8, 128 * t8 + 128), idb_sb, bank.c(0, 128),
               True, True, transpose=True)
            cp(vf.c(128 * t8, 128 * t8 + 128), bank.c(0, 128))
        blocks = []
        for qb in range(4):
            nkc = 4 * qb + 4
            for kc in range(nkc):
                qs = max(512 * qb, 128 * kc)
                nn = 512 * (qb + 1) - qs
                blocks.append((qb, kc, qs, nn, qs - 512 * qb, nkc))

        def score_mm(i):
            qb, kc, qs, nn, off, nkc = blocks[i]
            mm(kp.c(128 * kc, 128 * kc + 128), qp.c(qs, qs + nn),
               pb[2 + i % 2].c(0, nn), True, True)

        score_mm(0)
        for i, (qb, kc, qs, nn, off, nkc) in enumerate(blocks):
            sbank = pb[2 + i % 2]
            ptb = pt_ring[i % 4]
            if i + 1 < len(blocks):
                score_mm(i + 1)
            tr.op("scalar", lambda e, o=ptb.c(0, nn).ap, x=sbank.c(0, nn).ap,
                  bvp=bv.ap:
                  e.activation(o, x, AF.Exp, bias=bvp, scale=ISCALE),
                  reads=[sbank.c(0, nn), bv], writes=[ptb.c(0, nn)])
            if kc >= 4 * qb:
                tt(ptb.c(0, 128), ptb.c(0, 128), tri_sb, OP.mult)
            sig_bank = pb[5 + qb % 2]
            ctx_bank = pb[4 + 3 * (qb % 2)]
            mm(onb_sb, ptb.c(0, nn), sig_bank.c(off, off + nn),
               kc == 0, kc == nkc - 1)
            mm(vf.c(128 * kc, 128 * kc + 128), ptb.c(0, nn),
               ctx_bank.c(off, off + nn), kc == 0, kc == nkc - 1)
            if kc == nkc - 1:
                recip(rsig, sig_bank)
                tt(ctmp, ctx_bank, rsig, OP.mult)
                s_slot, t_off = qb // 2, 512 * (qb % 2)
                cb = cpair[s_slot % 2]
                ts(cb.c(t_off, t_off + 512), ctmp,
                   bq_sb.c(32 + hl, 32 + hl + 1), None, OP.add)
                if qb % 2 == 1:
                    gg, rr = hl // 4, hl % 4
                    lo = (((gg * 2 + s_slot) * 512) + 128 * rr) * 1024 * 2
                    dma("sync",
                        BV(ctx_send.ap[gg, s_slot, 128 * rr:128 * rr + 128, :],
                           "ctx_send", lo, lo + 128 * 1024 * 2, BF16),
                        cb)
        if hl % 4 == 3:
            g = hl // 4
            snd = BV(ctx_send.ap[g], "ctx_send", g * 2 * 512 * 1024 * 2,
                     (g + 1) * 2 * 512 * 1024 * 2, BF16)
            agv = BV(ctx_ag.ap[g], "ctx_ag", g * 4 * 512 * 1024 * 2,
                     (g + 1) * 4 * 512 * 1024 * 2, BF16)
            tr.op("gpsimd",
                  lambda e, s=snd.ap, o=agv.ap: e.collective_compute(
                      "AllGather", OP.bypass,
                      replica_groups=[[0, 1], [2, 3], [4, 5], [6, 7]],
                      ins=[s], outs=[o]),
                  reads=[snd], writes=[agv], kind="cc")

    arena.pop()

    mq4 = [arena.alloc(512, F32) for _ in range(4)]
    for _mx in mq4:
        tr.op("vector", lambda e, o=_mx.ap: e.memset(o, 0.0), reads=[],
              writes=[_mx])
    arena.push()
    agA = [arena.alloc(1024, BF16) for _ in range(2)]
    agB = [arena.alloc(1024, BF16) for _ in range(2)]
    tsel = arena.alloc(1024, BF16)
    ow_sb = [arena.alloc(4096, BF16) for _ in range(2)]
    xo_t = [arena.alloc(512, F32) for _ in range(4)]
    x2a = [arena.alloc(512, F32) for _ in range(2)]
    x2b = [arena.alloc(512, F32) for _ in range(2)]
    x2sq = [arena.alloc(512, BF16) for _ in range(2)]
    A2 = arena.alloc(1024, F32)
    B2 = arena.alloc(1024, F32)
    m2t = arena.alloc(512, F32)
    t12 = arena.alloc(512, F32)
    var2 = arena.alloc(512, F32)
    sq2 = var2
    h2t1 = [arena.alloc(512, F32) for _ in range(2)]
    xg_d = x2a

    def ctx_own_c(kc):
        return h_all.c(1024 * kc, 1024 * kc + 1024)

    def h2c(tb, m):
        base = 32768 + 16384 * tb + 512 * m
        return h_all.c(base, base + 512)

    dmac(ow_sb[0], ow[0])
    dmac(ow_sb[1], ow[1])

    for kc in range(KC):
        r, fr = kc // 16, kc % 16
        gg, rr = fr // 4, fr % 4
        a, b_ = agA[kc % 2], agB[kc % 2]
        base = ((((gg * 2 + r) * 2 + 0) * 512) + 128 * rr) * 1024 * 2
        dma("sync", a, BV(ctx_ag.ap[gg, r, 0, 128 * rr:128 * rr + 128, :],
                          "ctx_ag", base, base + 128 * 1024 * 2, BF16))
        base1 = ((((gg * 2 + r) * 2 + 1) * 512) + 128 * rr) * 1024 * 2
        dma("sync", b_, BV(ctx_ag.ap[gg, r, 1, 128 * rr:128 * rr + 128, :],
                           "ctx_ag", base1, base1 + 128 * 1024 * 2, BF16))
        ts(tsel, b_, sel1_sb, None, OP.mult)
        stt(ctx_own_c(kc), a, sel0_sb, tsel, OP.mult, OP.add)

    for m0 in range(2):
        for tb in range(2):
            dma("sync", xo_t[(2 * m0 + tb) % 4],
                xT_own[128 * m0:128 * m0 + 128, 512 * tb:512 * tb + 512])
    for m in range(KC):
        wbuf = ow_sb[m % 2]
        for tb in range(2):
            bank = pb[(2 * m + tb) % 4]
            for kc in range(KC):
                mm(wbuf.c(128 * kc, 128 * kc + 128),
                   ctx_own_c(kc).c(512 * tb, 512 * tb + 512),
                   bank, kc == 0, kc == KC - 1)
            xo = xo_t[(2 * m + tb) % 4]
            i2 = (2 * m + tb) % 2
            act(x2a[i2], bank, AF.Identity, bias=gob.c(m, m + 1),
                scale=ga_e.c(m, m + 1))
            tt(x2b[i2], x2a[i2], xo, OP.add)
            lo = (m * 128 * 1024 + 512 * tb) * 4
            dma("sync", BV(x2_dram.ap[m, :, 512 * tb:512 * tb + 512], "x2_dram",
                           lo, lo + 512 * 4, F32), x2b[i2])
            cp(h2c(tb, m), x2b[i2])
            act(x2sq[i2], x2b[i2], AF.Square)
            mm(onb_sb, h2c(tb, m), pb[4 + tb], m == 0, m == KC - 1)
            mm(onb_sb, x2sq[i2], pb[6 + tb], m == 0, m == KC - 1)
            if m + 2 < KC:
                dma("sync", xo_t[(2 * m + tb) % 4],
                    xT_own[128 * (m + 2):128 * (m + 2) + 128,
                           512 * tb:512 * tb + 512])
        if m + 2 < KC:
            dmac(ow_sb[m % 2], ow[m + 2])
    for tb in range(2):
        A2tb, B2tb = A2.c(512 * tb, 512 * tb + 512), B2.c(512 * tb, 512 * tb + 512)
        ts(m2t, pb[4 + tb], 1.0 / 4096.0, None, OP.mult)
        tt(t12, m2t, m2t, OP.mult)
        stt(var2, pb[6 + tb], 1.0 / 4096.0, t12, OP.mult, OP.subtract)
        act(sq2, var2, AF.Sqrt, bias=eps_sb)
        recip(A2tb, sq2)
        tt(B2tb, m2t, A2tb, OP.mult)
    for tb in range(2):
        for m in range(KC):
            slot, t1 = h2c(tb, m), h2t1[m % 2]
            tt(t1, slot, A2.c(512 * tb, 512 * tb + 512), OP.mult)
            tt(t1, t1, B2.c(512 * tb, 512 * tb + 512), OP.subtract)
            act(slot, t1, AF.Identity, bias=modT.c(96 + m, 96 + m + 1),
                scale=c1b.c(m, m + 1))
            for q, ivv in ((0, ivg_sb), (1, ivu_sb)):
                mx = mq4[2 * q + tb]
                xb = xg_d[q]
                act(xb, slot, AF.Abs, scale=ivv.c(m, m + 1))
                tt(mx, mx, xb, OP.max)
    arena.pop()

    arena.push()
    xsd = [arena.alloc(512, F32) for _ in range(2)]
    rmaxq = arena.alloc(8, F32)
    srow = arena.alloc(512, F32)
    scb_g = [arena.alloc(512, F32) for _ in range(2)]
    scb_u = [arena.alloc(512, F32) for _ in range(2)]
    rcb = [arena.alloc(512, F32) for _ in range(2)]
    qint = [arena.alloc(512, I32) for _ in range(3)]
    wgr = [arena.alloc(1024, BF16) for _ in range(4)]
    wur = [arena.alloc(1024, BF16) for _ in range(4)]
    wdr = [arena.alloc(1024, BF16) for _ in range(4)]
    g_t = arena.alloc(512, F32)
    s_t = arena.alloc(512, F32)
    u_t = arena.alloc(512, F32)
    ib_t = [arena.alloc(512, BF16) for _ in range(2)]
    d_t = g_t
    y_tb = [s_t, u_t]
    x2r2 = xsd + [arena.alloc(512, F32) for _ in range(2)]

    def qgu_c(q, tb, kc):
        base = (16384 * q if tb == 0 else 32768 + 16384 * q)
        return h_all.c(base + 512 * kc, base + 512 * kc + 512)

    def rb_c(tb, kc):
        base = 32768 + 8192 * tb
        return h_all.c(base + 512 * kc, base + 512 * kc + 512)

    def qi_c(tb, kc):
        base = 8192 * tb
        return h_all.c(base + 512 * kc, base + 512 * kc + 512)

    def idchunk(tb, im):
        lo = (tb * IKC + im) * 128 * 512 * 2
        return BV(inter_dram[tb, im], "inter_dram", lo, lo + 128 * 512 * 2, BF16)

    def mkscales(mx, scb, rc):
        for j in range(4):
            bank = pb[4 + j % 2]
            mm(mx.c(128 * j, 128 * j + 128), idf_sb, bank.c(0, 128),
               True, True, transpose=True)
            red(rmaxq.c(j, j + 1), bank.c(0, 128), OP.max)
        for j in range(4):
            mm(rmaxq.c(j, j + 1), idf_sb,
               pb[6].p(0, 1).c(128 * j, 128 * j + 128), True, True,
               transpose=True)
        cp(srow.p(0, 1), pb[6].p(0, 1).c(0, 512))
        ts(srow.p(0, 1), srow.p(0, 1), 1.0 / 127.0, 1e-8, OP.mult, OP.max)
        mm(onf_sb, srow.p(0, 1), pb[5], True, True)
        cp(scb, pb[5])
        recip(rc, scb)

    def quant_gu(tb):
        for q, (mx, scb, ivv) in enumerate(((mq4[tb], scb_g[tb], ivg_sb),
                                            (mq4[2 + tb], scb_u[tb], ivu_sb))):
            mkscales(mx, scb, rcb[q])
            for kc in range(KC):
                qb = qint[kc % 3]
                stt(qb, h2c(tb, kc), ivv.c(kc, kc + 1), rcb[q],
                    OP.mult, OP.mult)
                dma("gpsimd", qgu_c(q, tb, kc), qb)

    ens_g = mkstream(wgr, lambda i: wg[i // 4][:, 1024 * (i % 4):
                                                1024 * (i % 4) + 1024], 64)
    ens_u = mkstream(wur, lambda i: wu[i // 4][:, 1024 * (i % 4):
                                               1024 * (i % 4) + 1024], 64)
    ens_d = mkstream(wdr, lambda i: wd[i // 2][:, 1024 * (i % 2):
                                               1024 * (i % 2) + 1024], 64)

    quant_gu(0)
    quant_gu(1)

    ens_g(3)
    ens_u(3)
    for im in range(IKC):
        bks = [pb[4 * (im % 2) + j] for j in range(4)]
        for kc in range(KC):
            gs = im * 4 + kc // 8
            if kc % 8 == 0:
                ens_g(gs + 3)
                ens_u(gs + 3)
            wgs, wus = wgr[gs % 4], wur[gs % 4]
            col = 128 * (kc % 8)
            mm(wgs.c(col, col + 128), qgu_c(0, 0, kc), bks[0], kc == 0,
               kc == KC - 1)
            mm(wgs.c(col, col + 128), qgu_c(0, 1, kc), bks[1], kc == 0,
               kc == KC - 1)
            mm(wus.c(col, col + 128), qgu_c(1, 0, kc), bks[2], kc == 0,
               kc == KC - 1)
            mm(wus.c(col, col + 128), qgu_c(1, 1, kc), bks[3], kc == 0,
               kc == KC - 1)
        for tb, mxi in ((0, mq4[0]), (1, mq4[1])):
            stt(g_t, bks[tb], swg_sb.c(im, im + 1), scb_g[tb],
                OP.mult, OP.mult)
            act(s_t, g_t, AF.Silu)
            stt(u_t, bks[2 + tb], swu_sb.c(im, im + 1), scb_u[tb],
                OP.mult, OP.mult)
            tt(u_t, s_t, u_t, OP.mult)
            ib = ib_t[tb]
            ts(ib, u_t, 10.0, -10.0, OP.min, OP.max)
            xb = xsd[tb]
            act(xb, ib, AF.Abs, scale=ivi_sb.c(im, im + 1))
            if im == 0:
                cp(mxi, xb)
            else:
                tt(mxi, mxi, xb, OP.max)
            dma("scalar", idchunk(tb, im), ib)

    scb_i = scb_g
    for tb, mxi in ((0, mq4[0]), (1, mq4[1])):
        mkscales(mxi, scb_i[tb], rcb[tb])
    for tb in range(2):
        for kc in range(IKC):
            rb = rb_c(tb, kc)
            dma("sync", rb, idchunk(tb, kc))
            qb = qint[kc % 3]
            stt(qb, rb, ivi_sb.c(kc, kc + 1), rcb[tb], OP.mult, OP.mult)
            dma("gpsimd", qi_c(tb, kc), qb)

    ens_d(3)
    for m0 in range(2):
        for tb in range(2):
            lo = (m0 * 128 * 1024 + 512 * tb) * 4
            dma("sync", x2r2[(2 * m0 + tb) % 4],
                BV(x2_dram.ap[m0, :, 512 * tb:512 * tb + 512],
                   "x2_dram", lo, lo + 512 * 4, F32))
    for m in range(KC):
        bd = [pb[2 * (m % 2)], pb[2 * (m % 2) + 1]]
        for kci in range(IKC):
            ds = m * 2 + kci // 8
            if kci % 8 == 0:
                ens_d(ds + 3)
            wds = wdr[ds % 4]
            col = 128 * (kci % 8)
            mm(wds.c(col, col + 128), qi_c(0, kci), bd[0], kci == 0,
               kci == IKC - 1)
            mm(wds.c(col, col + 128), qi_c(1, kci), bd[1], kci == 0,
               kci == IKC - 1)
        for tb in range(2):
            xb = x2r2[(2 * m + tb) % 4]
            lo = (m * 128 * 1024 + 512 * tb) * 4
            stt(d_t, bd[tb], gbswd.c(m, m + 1), scb_i[tb], OP.mult, OP.mult)
            yb = y_tb[tb]
            tt(yb, d_t, xb, OP.add)
            dma("sync", BV(yT_my[128 * m:128 * m + 128,
                                 512 * tb:512 * tb + 512],
                           "y_out", lo, lo + 512 * 4, F32), yb)
            if m + 2 < KC:
                lo2 = ((m + 2) * 128 * 1024 + 512 * tb) * 4
                dma("sync", x2r2[(2 * m + tb) % 4],
                    BV(x2_dram.ap[m + 2, :, 512 * tb:512 * tb + 512],
                       "x2_dram", lo2, lo2 + 512 * 4, F32))
    arena.pop()
    tr.wait_all("sync", [y_out])

    tr.finalize()
    with nc.Block() as block:
        @block.sync
        def _(e):
            for fn in tr.q["sync"]:
                fn(e)

        @block.gpsimd
        def _(e):
            for fn in tr.q["gpsimd"]:
                fn(e)

        @block.scalar
        def _(e):
            for fn in tr.q["scalar"]:
                fn(e)

        @block.vector
        def _(e):
            for fn in tr.q["vector"]:
                fn(e)

        @block.tensor
        def _(e):
            for fn in tr.q["tensor"]:
                fn(e)

    st.close()
    return nc


def make_inputs(core, inp):
    b, half = core // 2, core % 2
    lo = 1024 * half
    f32 = np.float32
    x = np.asarray(inp["L0_attn_hidden_states"], f32)
    qkv_w = np.asarray(inp["L0_attn_qkv_w"], f32)
    qkv_b = np.asarray(inp["L0_attn_qkv_b"], f32)
    out_w = np.asarray(inp["L0_attn_out_w"], f32)
    out_b = np.asarray(inp["L0_attn_out_b"], f32)
    wmod = np.asarray(inp["L0_wmod"], f32)
    cc = np.asarray(inp["L0_c"], f32)

    d = {}
    d["xT_my"] = np.ascontiguousarray(x[b].T)
    d["xT_own"] = np.ascontiguousarray(x[b, lo:lo + 1024].T)
    heads = []
    for hl in range(NHC):
        hg = 16 * half + hl
        for t in range(3):
            heads.append(qkv_w[:, E * t + 128 * hg:E * t + 128 * hg + 128]
                         .reshape(KC, 128, 128))
    d["wqkv"] = np.ascontiguousarray(
        np.stack(heads).transpose(0, 2, 1, 3)).reshape(NHC, 3, 128, KC * 128)
    bq = np.zeros((128, 48), f32)
    for hl in range(NHC):
        hg = 16 * half + hl
        bq[:, hl] = qkv_b[128 * hg:128 * hg + 128]
        bq[:, 16 + hl] = qkv_b[E + 128 * hg:E + 128 * hg + 128]
        bq[:, 32 + hl] = qkv_b[2 * E + 128 * hg:2 * E + 128 * hg + 128]
    d["bqkv"] = bq
    d["wmod_my"] = np.ascontiguousarray(
        wmod[512 * core:512 * core + 512, :].reshape(4, 128, 24, 1024)
        .transpose(1, 2, 0, 3)).reshape(128, 24 * 4096)
    d["c_my"] = np.ascontiguousarray(cc[:, 512 * core:512 * core + 512])
    e = np.zeros((4, 1), f32)
    e[b, 0] = 1.0
    d["e_b"] = e
    d["sel0"] = np.full((128, 1), 1.0 if half == 0 else 0.0, f32)
    d["sel1"] = np.full((128, 1), 1.0 if half == 1 else 0.0, f32)
    d["ow"] = np.ascontiguousarray(
        out_w.reshape(KC, 128, KC, 128).transpose(2, 1, 0, 3)).reshape(
        KC, 128, KC * 128)
    d["obT"] = np.ascontiguousarray(out_b.reshape(32, 128).T)
    wgm = np.asarray(inp["L0_ffn_w_gate"], f32)
    wum = np.asarray(inp["L0_ffn_w_up"], f32)
    wdm = np.asarray(inp["L0_ffn_w_down"], f32)
    d["wg"] = np.ascontiguousarray(
        wgm.T.reshape(KC, 128, IKC, 128).transpose(2, 1, 0, 3)).reshape(
        IKC, 128, KC * 128)
    d["wu"] = np.ascontiguousarray(
        wum.T.reshape(KC, 128, IKC, 128).transpose(2, 1, 0, 3)).reshape(
        IKC, 128, KC * 128)
    d["wd"] = np.ascontiguousarray(
        wdm.T.reshape(IKC, 128, KC, 128).transpose(2, 1, 0, 3)).reshape(
        KC, 128, IKC * 128)
    d["invgT"] = np.ascontiguousarray(
        np.asarray(inp["L0_ffn_inv_gate"], f32).reshape(KC, 128).T)
    d["invuT"] = np.ascontiguousarray(
        np.asarray(inp["L0_ffn_inv_up"], f32).reshape(KC, 128).T)
    d["inviT"] = np.ascontiguousarray(
        np.asarray(inp["L0_ffn_inv_inter"], f32).reshape(IKC, 128).T)
    d["swgT"] = np.ascontiguousarray(
        np.asarray(inp["L0_ffn_s_wgate"], f32).reshape(IKC, 128).T)
    d["swuT"] = np.ascontiguousarray(
        np.asarray(inp["L0_ffn_s_wup"], f32).reshape(IKC, 128).T)
    d["swdT"] = np.ascontiguousarray(
        np.asarray(inp["L0_ffn_s_wdown"], f32).reshape(KC, 128).T)
    return d


def make_consts():
    f32 = np.float32
    j = np.arange(64, dtype=np.float64)
    invf = 1.0 / (10000.0 ** (j / 64.0))
    t = np.arange(S, dtype=np.float64)
    ang = np.outer(invf, t)
    cos = np.cos(ang)
    sin = np.sin(ang)
    cos2 = np.concatenate([cos, cos], 0).astype(f32)
    sin2s = np.concatenate([-sin, sin], 0).astype(f32)
    tri = (np.arange(128)[:, None] <= np.arange(128)[None, :])
    psw = np.zeros((128, 128), np.float32)
    for i in range(128):
        psw[(i + 64) % 128, i] = 1.0
    return {"cos2": cos2, "sin2s": sin2s,
            "trimask": tri.astype(ml_dtypes.bfloat16),
            "pswap": psw.astype(ml_dtypes.bfloat16),
            "ident_bf": np.eye(128).astype(ml_dtypes.bfloat16),
            "ident_f": np.eye(128, dtype=f32),
            "ones_bf": np.ones((128, 128), ml_dtypes.bfloat16),
            "ones_f": np.ones((1, 128), f32)}


_CACHE = {}


def kernel(**inputs):
    if "nc" not in _CACHE:
        _CACHE["nc"] = build_program()
    nc = _CACHE["nc"]
    consts = make_consts()
    in_maps = []
    for c in range(8):
        m = make_inputs(c, inputs)
        m.update(consts)
        in_maps.append(m)
    res = run_bass_kernel_spmd(nc, in_maps, core_ids=list(range(8)))
    y = np.empty((B, S, E), np.float32)
    for c in range(8):
        b, half = c // 2, c % 2
        lo = 1024 * half
        y[b, lo:lo + 1024] = res.results[c]["yT_my"].T
    return y


# version 14 (final)
# speedup vs baseline: 1.1187x; 1.0045x over previous
import numpy as np
import ml_dtypes
import concourse.bass as bass
import concourse.mybir as mybir
from concourse.bass_utils import run_bass_kernel_spmd
from contextlib import ExitStack

F32 = mybir.dt.float32
BF16 = mybir.dt.bfloat16
I32 = mybir.dt.int32
AF = mybir.ActivationFunctionType
OP = mybir.AluOpType
AX = mybir.AxisListType

B, S, E, D, NHC, II = 4, 2048, 4096, 128, 16, 2048
KC = E // 128
IKC = II // 128
ISCALE = float(1.0 / np.sqrt(128.0))
ENGINES = ("sync", "gpsimd", "scalar", "vector", "tensor")
DBG = False
FULL_WAITS = False
PSUM_EXCL = True

_IS = {F32: 4, BF16: 2, I32: 4}


class BV:

    def __init__(self, ap, key, lo, hi, dt):
        self.ap, self.key, self.lo, self.hi, self.dt = ap, key, lo, hi, dt

    def c(self, a, b):
        isz = _IS[self.dt]
        return BV(self.ap[:, a:b], self.key, self.lo + a * isz, self.lo + b * isz,
                  self.dt)

    def p(self, a, b):
        return BV(self.ap[a:b], self.key, self.lo, self.hi, self.dt)


class Tracker:
    def __init__(self, st, nc):
        self.nc = nc
        self.q = {e: [] for e in ENGINES}
        self.esem = {e: st.enter_context(nc.semaphore("es_" + e)) for e in ENGINES}
        self.NDQ = 8
        self.dsem = {e: [st.enter_context(nc.semaphore(f"ds_{e}{i}"))
                         for i in range(self.NDQ)]
                     for e in ("sync", "gpsimd", "scalar")}
        self.ccsem = st.enter_context(nc.semaphore("ccsem"))
        self.cnt = {e: 0 for e in ENGINES}
        self.dcnt = {e: [0] * self.NDQ for e in ("sync", "gpsimd", "scalar")}
        self.didx = {e: 0 for e in ("sync", "gpsimd", "scalar")}
        self.ccnt = 0
        self.reg = {}
        self.sat = {e: {} for e in ENGINES}
        self.snap = {e: {} for e in ENGINES}
        self.semobj = {}

    def _semname(self, sem):
        for e, s in self.esem.items():
            if s is sem:
                return "e_" + e
        for e, ss in self.dsem.items():
            for i, s in enumerate(ss):
                if s is sem:
                    return f"d_{e}{i}"
        return "cc"

    def _sid(self, sem):
        i = id(sem)
        self.semobj[i] = sem
        return i

    def _hazards(self, eng, reads, writes):
        need = {}

        def add(marker):
            if marker is None:
                return
            sem, val, owner = marker
            if owner == eng == "tensor" and sem is self.esem[eng]:
                return
            sid = self._sid(sem)
            if FULL_WAITS:
                if need.get(sid, (None, 0))[1] < val:
                    need[sid] = (sem, val)
                return
            if self.sat[eng].get(sid, 0) >= val:
                return
            if need.get(sid, (None, 0))[1] < val:
                need[sid] = (sem, val)
        for bv in reads:
            for iv in self.reg.get(bv.key, []):
                if iv[0] < bv.hi and bv.lo < iv[1]:
                    add(iv[2])
        for bv in writes:
            for iv in self.reg.get(bv.key, []):
                if iv[0] < bv.hi and bv.lo < iv[1]:
                    add(iv[2])
                    for r in iv[3]:
                        add(r)
        return [v for _, v in need.items()]

    def _record(self, eng, reads, writes, marker):
        for bv in reads:
            lst = self.reg.setdefault(bv.key, [])
            hit = False
            for iv in lst:
                if iv[0] == bv.lo and iv[1] == bv.hi:
                    iv[3].append(marker)
                    hit = True
                    break
            if not hit:
                lst.append([bv.lo, bv.hi, None, [marker]])
        for bv in writes:
            lst = self.reg.setdefault(bv.key, [])
            keep = []
            for iv in lst:
                if iv[0] < bv.hi and bv.lo < iv[1]:
                    if iv[0] < bv.lo:
                        keep.append([iv[0], bv.lo, iv[2], list(iv[3])])
                    if bv.hi < iv[1]:
                        keep.append([bv.hi, iv[1], iv[2], list(iv[3])])
                else:
                    keep.append(iv)
            keep.append([bv.lo, bv.hi, marker, []])
            self.reg[bv.key] = keep

    def op(self, eng, f, reads=(), writes=(), kind="compute", tag=None):
        if PSUM_EXCL:
            psr = [b for b in reads if b.key.startswith("pb")]
            if psr:
                reads = [b for b in reads if not b.key.startswith("pb")]
                writes = list(writes) + psr
        waits = self._hazards(eng, reads, writes)
        if kind == "compute":
            sem = self.esem[eng]
            self.cnt[eng] += 1
            val, k = self.cnt[eng], 1
        elif kind == "dma":
            slot = self.didx[eng] % self.NDQ
            self.didx[eng] += 1
            sem = self.dsem[eng][slot]
            prior = self.dcnt[eng][slot]
            if prior:
                sid = self._sid(sem)
                if self.sat[eng].get(sid, 0) < prior and not any(
                        s is sem and v >= prior for s, v in waits):
                    waits = list(waits) + [(sem, prior)]
            self.dcnt[eng][slot] = prior + 16
            val, k = prior + 16, 16
        else:
            sem = self.ccsem
            self.ccnt += 1
            val, k = self.ccnt, 1
        marker = (sem, val, eng)
        if not hasattr(self, "log"):
            self.log = []
        self.log.append((eng, kind, val, tag,
                         [(self._semname(s), v) for s, v in waits],
                         [(b.key, b.lo, b.hi) for b in reads],
                         [(b.key, b.lo, b.hi) for b in writes],
                         self._semname(sem)))
        self.q[eng].append((f, tuple(waits), sem, k, val, kind))
        for s, v in waits:
            sid = self._sid(s)
            self.sat[eng][sid] = max(self.sat[eng].get(sid, 0), v)
            for oe in ENGINES:
                if s is self.esem[oe]:
                    snapd = self.snap[oe].get(v)
                    if snapd:
                        for k2, v2 in snapd.items():
                            self.sat[eng][k2] = max(self.sat[eng].get(k2, 0), v2)
        if kind == "compute":
            self.snap[eng][val] = dict(self.sat[eng])
            self.snap[eng][val][self._sid(sem)] = val
        self._record(eng, reads, writes, marker)
        return marker

    def finalize(self):
        waited = {self._sid(s): set() for s in self.esem.values()}
        for eng in ENGINES:
            for (f, waits, sem, k, val, kind) in self.q[eng]:
                for s, v in waits:
                    sid = self._sid(s)
                    if sid in waited:
                        waited[sid].add(v)
        remap = {}
        newq = {e: [] for e in ENGINES}
        for eng in ENGINES:
            sid = self._sid(self.esem[eng])
            compact = 0
            for (f, waits, sem, k, val, kind) in self.q[eng]:
                inc = kind != "wait"
                if kind == "compute":
                    if val in waited[sid]:
                        compact += 1
                        remap[(sid, val)] = compact
                    else:
                        inc = False
                newq[eng].append((f, waits, sem, k, inc))
        out = {e: [] for e in ENGINES}
        for eng in ENGINES:
            for (f, waits, sem, k, inc) in newq[eng]:
                w2 = []
                for s, v in waits:
                    sid = self._sid(s)
                    if (sid, v) in remap:
                        w2.append((s, remap[(sid, v)]))
                    elif sid in waited:
                        assert False, f"wait on un-inc'd marker {v}"
                    else:
                        w2.append((s, v))

                def fn(e, f=f, w2=tuple(w2), sem=sem, k=k, inc=inc):
                    for s, v in w2:
                        e.wait_ge(s, v)
                    if f is not None:
                        inst = f(e)
                        if inc:
                            inst.then_inc(sem, k)
                out[eng].append(fn)
        self.q = out

    def wait_all(self, eng, bvs):
        need = self._hazards(eng, bvs, bvs)
        for s, v in need:
            self.q[eng].append((None, ((s, v),), None, 0, 0, "wait"))
            self.sat[eng][self._sid(s)] = max(self.sat[eng].get(self._sid(s), 0), v)


class Arena:
    def __init__(self, t, nbytes, tr):
        self.t, self.n, self.off, self.stack = t, nbytes, 0, []
        self.tr = tr
        self.nalloc = 0

    def alloc(self, cols, dtype):
        isz = _IS[dtype]
        off = (self.off + 63) // 64 * 64
        sz = cols * isz
        assert off + sz <= self.n, f"arena overflow: {off}+{sz} > {self.n}"
        self.off = off + sz
        view = self.t[:, off // 4:(off + sz + 3) // 4]
        if dtype is not F32:
            view = view.bitcast(dtype)
        self.nalloc += 1
        return BV(view[:, 0:cols], "arena", off, off + sz, dtype)

    def push(self):
        self.stack.append(self.off)

    def pop(self):
        self.off = self.stack.pop()


def build_program():
    nc = bass.Bass("TRN2", target_bir_lowering=False, debug=False,
                   detect_race_conditions=False)
    st = ExitStack()
    tr = Tracker(st, nc)

    def din(name, shape, dt=F32):
        ap = nc.dram_tensor(name, shape, dt, kind="ExternalInput").ap()
        return ap

    xT_my = din("xT_my", [E, S])
    xT_own = din("xT_own", [E, 1024])
    wqkv = din("wqkv", [NHC, 3, 128, KC * 128])
    bqkv = din("bqkv", [128, 48])
    wmod_my = din("wmod_my", [128, 24 * 4096])
    c_my = din("c_my", [4, 512])
    e_b = din("e_b", [4, 1])
    sel0 = din("sel0", [128, 1])
    sel1 = din("sel1", [128, 1])
    ow = din("ow", [KC, 128, KC * 128])
    obT = din("obT", [128, 32])
    wg = din("wg", [IKC, 128, KC * 128])
    wu = din("wu", [IKC, 128, KC * 128])
    wd = din("wd", [KC, 128, IKC * 128])
    invgT = din("invgT", [128, 32])
    invuT = din("invuT", [128, 32])
    inviT = din("inviT", [128, 16])
    swgT = din("swgT", [128, 16])
    swuT = din("swuT", [128, 16])
    swdT = din("swdT", [128, 32])
    cos2 = din("cos2", [128, S])
    sin2s = din("sin2s", [128, S])
    trimask = din("trimask", [128, 128], BF16)
    pswap = din("pswap", [128, 128], BF16)
    ident_bf = din("ident_bf", [128, 128], BF16)
    ident_f = din("ident_f", [128, 128], F32)
    ones_bf = din("ones_bf", [128, 128], BF16)
    ones_f = din("ones_f", [1, 128], F32)
    yT_my = nc.dram_tensor("yT_my", [E, 1024], F32, kind="ExternalOutput").ap()

    dbg = {}
    if DBG:
        for nm, shp in [("d_modT", [128, 192]), ("d_h0", [128, S], ),
                        ("d_q0", [128, S]), ("d_k0", [128, S]), ("d_v0", [128, S]),
                        ("d_ctx0", [128, 512]), ("d_bias0", [128, 1]),
                        ("d_ctxown", [128, 1024]), ("d_x2", [128, 1024]),
                        ("d_h2", [128, 1024]), ("d_qg0", [128, 512]),
                        ("d_inter0", [128, 512]), ("d_sgb", [128, 512]),
                        ("d_nsq", [128, 512]), ("d_norm", [128, 512]),
                        ("d_maxq", [128, 8]), ("d_tail", [4, 128]), ("d_ball", [16, 128]), ("d_sig", [16, 128]), ("d_maxq1", [128, 8]), ("d_nsq1", [128, 512]), ("d_tail1", [4, 128]), ("d_maxq2", [128, 8]), ("d_norm2", [128, 512]), ("d_nsq2", [128, 512])]:
            dbg[nm] = nc.dram_tensor(nm, shp, F32, kind="ExternalOutput").ap()

    mod_part_t = nc.dram_tensor("mod_part", [4, 6 * E], F32)
    mod_ar_t = nc.dram_tensor("mod_ar", [4, 6 * E], F32)
    inter_dram_t = nc.dram_tensor("inter_dram", [2, IKC, 128, 512], BF16)
    ctx_send_t = nc.dram_tensor("ctx_send", [4, 2, 512, 1024], BF16)
    ctx_ag_t = nc.dram_tensor("ctx_ag", [4, 2, 2, 512, 1024], BF16)
    x2_dram_t = nc.dram_tensor("x2_dram", [KC, 128, 1024], F32)
    inter_dram = inter_dram_t.ap()
    mod_part = BV(mod_part_t.ap(), "mod_part", 0, 4 * 6 * E * 4, F32)
    mod_ar = BV(mod_ar_t.ap(), "mod_ar", 0, 4 * 6 * E * 4, F32)
    ctx_send = BV(ctx_send_t.ap(), "ctx_send", 0, 2 * 2048 * 1024 * 2, BF16)
    ctx_ag = BV(ctx_ag_t.ap(), "ctx_ag", 0, 2 * 2 * 2048 * 1024 * 2, BF16)
    x2_dram = BV(x2_dram_t.ap(), "x2_dram", 0, KC * 128 * 1024 * 4, F32)
    y_out = BV(yT_my, "y_out", 0, E * 1024 * 4, F32)

    def sbv(name, shape, dt):
        t = st.enter_context(nc.sbuf_tensor(name, shape, dt))
        return BV(t[:, :] if len(shape) == 2 else t[:], name,
                  0, shape[-1] * _IS[dt], dt)

    h_all = sbv("h_all", [128, KC * S], BF16)
    tri_sb = sbv("tri_sb", [128, 128], BF16)
    psw_sb = sbv("psw_sb", [128, 128], BF16)
    idb_sb = sbv("idb_sb", [128, 128], BF16)
    idf_sb = sbv("idf_sb", [128, 128], F32)
    onb_sb = sbv("onb_sb", [128, 128], BF16)
    onf_sb = sbv("onf_sb", [1, 128], F32)
    bq_sb = sbv("bq_sb", [128, 48], F32)
    ob_sb = sbv("ob_sb", [128, 32], F32)
    modT = sbv("modT", [128, 192], F32)
    c1a = sbv("c1a", [128, 32], F32)
    c1b = sbv("c1b", [128, 32], F32)
    ga_e = sbv("ga_e", [128, 32], F32)
    gob = sbv("gob", [128, 32], F32)
    gb_e = sbv("gb_e", [128, 32], F32)
    gbswd = sbv("gbswd", [128, 32], F32)
    ivg_sb = sbv("ivg_sb", [128, 32], F32)
    ivu_sb = sbv("ivu_sb", [128, 32], F32)
    ivi_sb = sbv("ivi_sb", [128, 16], F32)
    swg_sb = sbv("swg_sb", [128, 16], F32)
    swu_sb = sbv("swu_sb", [128, 16], F32)
    swd_sb = sbv("swd_sb", [128, 32], F32)
    sel0_sb = sbv("sel0_sb", [128, 1], F32)
    sel1_sb = sbv("sel1_sb", [128, 1], F32)
    e_sb = sbv("e_sb", [4, 1], F32)
    cT_my = sbv("cT_my", [128, 16], BF16)
    eps_sb = sbv("eps_sb", [128, 1], F32)

    AR_BYTES = 77056
    arena_t = st.enter_context(nc.sbuf_tensor("arena", [128, AR_BYTES // 4], F32))
    arena = Arena(arena_t, AR_BYTES, tr)

    pb = []
    pbb = []
    for i in range(8):
        t = st.enter_context(nc.psum_tensor(f"pb{i}", [128, 512], F32))
        pb.append(BV(t[:, :], f"pb{i}", 0, 2048, F32))
        pbb.append(BV(t[:, :].bitcast(BF16), f"pb{i}", 0, 2048, BF16))

    def dma(eng, dst, src):
        rd = [src] if isinstance(src, BV) else []
        wr = [dst] if isinstance(dst, BV) else []
        sap = src.ap if isinstance(src, BV) else src
        dap = dst.ap if isinstance(dst, BV) else dst
        return tr.op(eng, lambda e, d=dap, s=sap: e.dma_start(out=d, in_=s),
                     reads=rd, writes=wr, kind="dma")

    def dmac(dst, src, eng="gpsimd"):
        n = dst.ap.shape[-1]
        assert n % 2048 == 0 and src.shape[-1] == n
        dap = dst.ap.rearrange("p (a b) -> p a b", b=2048)
        sap = src.rearrange("p (a b) -> p a b", b=2048)
        return tr.op(eng, lambda e, d=dap, s=sap: e.dma_start(out=d, in_=s),
                     reads=[], writes=[dst], kind="dma")

    def mm(lhs, rhs, out, start, stop, transpose=False):
        if transpose:
            f = lambda e, a=lhs.ap, b=rhs.ap, o=out.ap: e.transpose(o, a, b)
        else:
            f = lambda e, a=lhs.ap, b=rhs.ap, o=out.ap, st_=start, sp=stop: \
                e.matmul(o, a, b, start=st_, stop=sp)
        return tr.op("tensor", f, reads=[lhs, rhs], writes=[out])

    def vop(f, reads, writes, eng="vector"):
        return tr.op(eng, f, reads=reads, writes=writes)

    def tt(out, a, b, op, eng="vector"):
        return vop(lambda e, o=out.ap, x=a.ap, y=b.ap, op=op:
                   e.tensor_tensor(o, x, y, op=op), [a, b], [out], eng)

    def ts(out, a, s1, s2, op0, op1=None, eng="vector"):
        r = [a] + [s for s in (s1, s2) if isinstance(s, BV)]
        a1 = s1.ap if isinstance(s1, BV) else s1
        a2 = s2.ap if isinstance(s2, BV) else s2
        def f(e, o=out.ap, x=a.ap, a1=a1, a2=a2, op0=op0, op1=op1):
            if op1 is None:
                return e.tensor_scalar(o, x, a1, None, op0=op0)
            return e.tensor_scalar(o, x, a1, a2, op0=op0, op1=op1)
        return vop(f, r, [out], eng)

    def stt(out, a, s, b, op0, op1, eng="vector"):
        r = [a, b] + ([s] if isinstance(s, BV) else [])
        sap = s.ap if isinstance(s, BV) else s
        return vop(lambda e, o=out.ap, x=a.ap, sap=sap, y=b.ap, op0=op0, op1=op1:
                   e.scalar_tensor_tensor(o, x, sap, y, op0=op0, op1=op1),
                   r, [out], eng)

    def cp(out, a, eng="vector"):
        return vop(lambda e, o=out.ap, x=a.ap: e.tensor_copy(o, x), [a], [out], eng)

    def act(out, a, func, bias=None, scale=1.0, eng="scalar"):
        r = [a] + [x for x in (bias, scale) if isinstance(x, BV)]
        bap = bias.ap if isinstance(bias, BV) else bias
        sap = scale.ap if isinstance(scale, BV) else scale

        def f(e, o=out.ap, x=a.ap, func=func, bap=bap, sap=sap):
            kw = {}
            if bap is not None:
                kw["bias"] = bap
            return e.activation(o, x, func, scale=sap, **kw)
        return tr.op(eng, f, reads=r, writes=[out])

    def red(out, a, op, absval=False):
        def f(e, o=out.ap, x=a.ap, op=op, av=absval):
            if av:
                return e.tensor_reduce(o, x, axis=AX.X, op=op,
                                       apply_absolute_value=True)
            return e.tensor_reduce(o, x, axis=AX.X, op=op)
        return vop(f, [a], [out])

    def recip(out, a):
        return vop(lambda e, o=out.ap, x=a.ap: e.reciprocal(o, x), [a], [out])

    def mkstream(ring, src_fn, total):
        state = {"i": 0}

        def ensure(upto):
            while state["i"] < min(upto, total):
                i = state["i"]
                dst = ring[i % len(ring)]
                tr.op("gpsimd",
                      lambda e, d=dst.ap, s=src_fn(i): e.dma_start(out=d, in_=s),
                      reads=[], writes=[dst], kind="dma")
                state["i"] += 1
        return ensure

    for dst, src in [(tri_sb, trimask), (psw_sb, pswap), (idb_sb, ident_bf),
                     (idf_sb, ident_f), (onb_sb, ones_bf), (onf_sb, ones_f),
                     (bq_sb, bqkv), (ob_sb, obT), (ivg_sb, invgT), (ivu_sb, invuT),
                     (ivi_sb, inviT), (swg_sb, swgT), (swu_sb, swuT),
                     (swd_sb, swdT), (sel0_sb, sel0), (sel1_sb, sel1),
                     (e_sb, e_b)]:
        dma("sync", dst, src[:, :] if not isinstance(src, BV) else src)
    tr.op("vector", lambda e: e.memset(eps_sb.ap, 1e-5), reads=[], writes=[eps_sb])

    arena.push()
    c_f = arena.alloc(512, F32)
    silu_c = arena.alloc(512, F32)
    silu_cb = arena.alloc(512, BF16)
    wring = [arena.alloc(4 * 1024, BF16) for _ in range(4)]
    pe_sb = [arena.alloc(512, F32) for _ in range(2)]

    dma("sync", c_f.p(0, 4), c_my[:, :])
    act(silu_c.p(0, 4), c_f.p(0, 4), AF.Silu)
    cp(silu_cb.p(0, 4), silu_c.p(0, 4))
    for j in range(4):
        mm(silu_cb.p(0, 4).c(128 * j, 128 * j + 128), idb_sb.p(0, 4).c(0, 4),
           pbb[7].c(4 * j, 4 * j + 4), True, True, transpose=True)
    cp(cT_my, pbb[7].c(0, 16))

    for i in range(4):
        dmac(wring[i], wmod_my[:, 4096 * i:4096 * i + 4096])
    for i in range(24):
        buf = wring[i % 4]
        for sub in range(2):
            bank = pb[(2 * i + sub) % 2]
            for kcp in range(4):
                mm(cT_my.c(4 * kcp, 4 * kcp + 4),
                   buf.c(1024 * kcp + 512 * sub, 1024 * kcp + 512 * sub + 512),
                   bank.p(0, 4), kcp == 0, kcp == 3)
            peb = pe_sb[(2 * i + sub) % 2]
            cp(peb.p(0, 4), bank.p(0, 4))
            dma("sync", mod_part.c(1024 * i + 512 * sub, 1024 * i + 512 * sub + 512),
                peb.p(0, 4))
        if i + 4 < 24:
            dmac(wring[(i + 4) % 4],
                 wmod_my[:, 4096 * (i + 4):4096 * (i + 4) + 4096])

    tr.op("gpsimd",
          lambda e: e.collective_compute(
              "AllReduce", OP.add, replica_groups=[list(range(8))],
              ins=[mod_part.ap], outs=[mod_ar.ap]),
          reads=[mod_part], writes=[mod_ar], kind="cc")
    arena.pop()

    arena.push()
    aring = [arena.alloc(1024, F32) for _ in range(2)]
    tmpa = arena.alloc(32, F32)
    tmpb = arena.alloc(32, F32)
    A_b = arena.alloc(2048, F32)
    B_b = arena.alloc(2048, F32)
    m_t = arena.alloc(512, F32)
    t1_t = arena.alloc(512, F32)
    var_t = arena.alloc(512, F32)
    sq_t = arena.alloc(512, F32)
    ht1 = [arena.alloc(2048, F32) for _ in range(2)]
    xring = [arena.alloc(2048, F32) for _ in range(2)]
    xsq = [arena.alloc(2048, BF16) for _ in range(2)]

    dma("sync", xring[0], xT_my[0:128, :])
    for kc in range(KC):
        buf, qbuf = xring[kc % 2], xsq[kc % 2]
        xbf = h_all.c(S * kc, S * kc + S)
        cp(xbf, buf)
        act(qbuf, buf, AF.Square)
        for tb in range(4):
            mm(onb_sb, xbf.c(512 * tb, 512 * tb + 512), pb[tb],
               kc == 0, kc == KC - 1)
        for tb in range(4):
            mm(onb_sb, qbuf.c(512 * tb, 512 * tb + 512), pb[4 + tb],
               kc == 0, kc == KC - 1)
        if kc + 1 < KC:
            dma("sync", xring[(kc + 1) % 2],
                xT_my[128 * (kc + 1):128 * (kc + 1) + 128, :])
    for tb in range(4):
        Atb, Btb = A_b.c(512 * tb, 512 * tb + 512), B_b.c(512 * tb, 512 * tb + 512)
        ts(m_t, pb[tb], 1.0 / 4096.0, None, OP.mult)
        tt(t1_t, m_t, m_t, OP.mult)
        stt(var_t, pb[4 + tb], 1.0 / 4096.0, t1_t, OP.mult, OP.subtract)
        act(sq_t, var_t, AF.Sqrt, bias=eps_sb)
        recip(Atb, sq_t)
        tt(Btb, m_t, Atb, OP.mult)

    for i in range(24):
        buf = aring[i % 2]
        dma("sync", buf.p(0, 4),
            BV(mod_ar.ap[:, 1024 * i:1024 * i + 1024], "mod_ar",
               1024 * i * 4, (1024 * i + 1024) * 4, F32))
        for j in range(8):
            col = 8 * i + j
            mm(buf.p(0, 4).c(128 * j, 128 * j + 128), e_sb, pb[0].c(col, col + 1),
               True, True)
    cp(modT, pb[0].c(0, 192))
    ts(c1a, modT.c(32, 64), 1.0, None, OP.add)
    ts(c1b, modT.c(128, 160), 1.0, None, OP.add)

    def sign_max(dst, off):
        ts(tmpa, modT.c(off, off + 32), 0.0, None, OP.is_ge)
        ts(tmpa, tmpa, 2.0, -1.0, OP.mult, OP.add)
        act(tmpb, modT.c(off, off + 32), AF.Abs)
        ts(tmpb, tmpb, 0.25, None, OP.max)
        tt(dst, tmpa, tmpb, OP.mult)

    sign_max(ga_e, 64)
    sign_max(gb_e, 160)
    tt(gob, ga_e, ob_sb, OP.mult)
    tt(gbswd, gb_e, swd_sb, OP.mult)

    for kc in range(KC):
        t1 = ht1[kc % 2]
        hchunk = h_all.c(S * kc, S * kc + S)
        tt(t1, hchunk, A_b, OP.mult)
        tt(t1, t1, B_b, OP.subtract)
        act(hchunk, t1, AF.Identity, bias=modT.c(kc, kc + 1),
            scale=c1a.c(kc, kc + 1))
    arena.pop()

    arena.push()
    cos_sb = arena.alloc(2048, F32)
    sin_sb = arena.alloc(2048, F32)
    dma("sync", cos_sb, cos2[:, :])
    dma("sync", sin_sb, sin2s[:, :])
    wqr = [arena.alloc(1024, BF16) for _ in range(4)]
    qpreb = [arena.alloc(2048, BF16) for _ in range(2)]
    kpreb = [arena.alloc(2048, BF16) for _ in range(2)]
    vfmb = [arena.alloc(2048, BF16) for _ in range(2)]
    tcos = arena.alloc(512, F32)
    usin = arena.alloc(512, F32)
    pt_ring = [arena.alloc(512, BF16) for _ in range(4)]
    nsqr = [arena.alloc(512, BF16) for _ in range(2)]
    maxqb = [arena.alloc(8, F32) for _ in range(2)]
    bias_vb = [arena.alloc(1, F32) for _ in range(2)]
    prod_vb = [arena.alloc(1, F32) for _ in range(2)]
    rsig = arena.alloc(512, F32)
    ctmp = arena.alloc(512, F32)
    cpair = [arena.alloc(1024, BF16) for _ in range(2)]

    def wq_src(i):
        hl, t, part = i // 12, (i // 4) % 3, i % 4
        return wqkv[hl, t][:, 1024 * part:1024 * part + 1024]
    ens_wq = mkstream(wqr, wq_src, NHC * 12)
    ens_wq(3)

    for hl in range(NHC):
        hp = hl % 2
        qp, kp, vf = qpreb[hp], kpreb[hp], vfmb[hp]
        mq, bv, pv = maxqb[hp], bias_vb[hp], prod_vb[hp]
        for t, (dst, bcol) in enumerate(((qp, 0), (kp, 16))):
            for kc in range(KC):
                si = hl * 12 + t * 4 + kc // 8
                if kc % 8 == 0:
                    ens_wq(si + 3)
                ws = wqr[si % 4]
                col = 128 * (kc % 8)
                for tb in range(4):
                    mm(ws.c(col, col + 128),
                       h_all.c(S * kc + 512 * tb, S * kc + 512 * tb + 512),
                       pb[tb], kc == 0, kc == KC - 1)
            for tb in range(4):
                ts(dst.c(512 * tb, 512 * tb + 512), pb[tb],
                   bq_sb.c(bcol + hl, bcol + hl + 1), None, OP.add)
        for gi, src in enumerate((qp, kp)):
            for tb in range(4):
                nb = nsqr[tb % 2]
                act(nb, src.c(512 * tb, 512 * tb + 512), AF.Square)
                bank = pb[4 + (4 * gi + tb) % 4]
                mm(onb_sb, nb, bank, True, True)
                red(mq.c(4 * gi + tb, 4 * gi + tb + 1), bank, OP.max)
        tr.op("vector", lambda e, o=pv.ap, x=mq.c(0, 4).ap: e.tensor_reduce(
            o, x, axis=AX.X, op=OP.max), reads=[mq.c(0, 4)], writes=[pv],
            tag=f"redq_h{hl}")
        tr.op("vector", lambda e, o=bv.ap, x=mq.c(4, 8).ap: e.tensor_reduce(
            o, x, axis=AX.X, op=OP.max), reads=[mq.c(4, 8)], writes=[bv],
            tag=f"redk_h{hl}")
        tt(pv, pv, bv, OP.mult)
        act(pv, pv, AF.Sqrt)
        ts(bv, pv, -ISCALE, -ISCALE, OP.mult, OP.add)
        for pre in (qp, kp):
            for tb in range(4):
                bank = pb[4 + tb % 4]
                mm(psw_sb, pre.c(512 * tb, 512 * tb + 512), bank, True, True)
                tt(tcos, pre.c(512 * tb, 512 * tb + 512),
                   cos_sb.c(512 * tb, 512 * tb + 512), OP.mult)
                tt(usin, bank, sin_sb.c(512 * tb, 512 * tb + 512), OP.mult)
                tt(pre.c(512 * tb, 512 * tb + 512), tcos, usin, OP.add)
        for kc in range(KC):
            si = hl * 12 + 8 + kc // 8
            if kc % 8 == 0:
                ens_wq(si + 3)
            ws = wqr[si % 4]
            col = 128 * (kc % 8)
            for tb in range(4):
                mm(ws.c(col, col + 128),
                   h_all.c(S * kc + 512 * tb, S * kc + 512 * tb + 512),
                   pb[tb], kc == 0, kc == KC - 1)
        for tb in range(4):
            cp(vf.c(512 * tb, 512 * tb + 512), pb[tb])
        for t8 in range(16):
            bank = pbb[4 + t8 % 4]
            mm(vf.c(128 * t8, 128 * t8 + 128), idb_sb, bank.c(0, 128),
               True, True, transpose=True)
            cp(vf.c(128 * t8, 128 * t8 + 128), bank.c(0, 128))
        blocks = []
        for qb in range(4):
            nkc = 4 * qb + 4
            for kc in range(nkc):
                qs = max(512 * qb, 128 * kc)
                nn = 512 * (qb + 1) - qs
                blocks.append((qb, kc, qs, nn, qs - 512 * qb, nkc))

        def score_mm(i):
            qb, kc, qs, nn, off, nkc = blocks[i]
            mm(kp.c(128 * kc, 128 * kc + 128), qp.c(qs, qs + nn),
               pb[2 + i % 2].c(0, nn), True, True)

        score_mm(0)
        for i, (qb, kc, qs, nn, off, nkc) in enumerate(blocks):
            sbank = pb[2 + i % 2]
            ptb = pt_ring[i % 4]
            if i + 1 < len(blocks):
                score_mm(i + 1)
            tr.op("scalar", lambda e, o=ptb.c(0, nn).ap, x=sbank.c(0, nn).ap,
                  bvp=bv.ap:
                  e.activation(o, x, AF.Exp, bias=bvp, scale=ISCALE),
                  reads=[sbank.c(0, nn), bv], writes=[ptb.c(0, nn)])
            if kc >= 4 * qb:
                tt(ptb.c(0, 128), ptb.c(0, 128), tri_sb, OP.mult)
            sig_bank = pb[5 + qb % 2]
            ctx_bank = pb[4 + 3 * (qb % 2)]
            mm(onb_sb, ptb.c(0, nn), sig_bank.c(off, off + nn),
               kc == 0, kc == nkc - 1)
            mm(vf.c(128 * kc, 128 * kc + 128), ptb.c(0, nn),
               ctx_bank.c(off, off + nn), kc == 0, kc == nkc - 1)
            if kc == nkc - 1:
                recip(rsig, sig_bank)
                tt(ctmp, ctx_bank, rsig, OP.mult)
                s_slot, t_off = qb // 2, 512 * (qb % 2)
                cb = cpair[s_slot % 2]
                ts(cb.c(t_off, t_off + 512), ctmp,
                   bq_sb.c(32 + hl, 32 + hl + 1), None, OP.add)
                if qb % 2 == 1:
                    gg, rr = hl // 4, hl % 4
                    lo = (((gg * 2 + s_slot) * 512) + 128 * rr) * 1024 * 2
                    dma("sync",
                        BV(ctx_send.ap[gg, s_slot, 128 * rr:128 * rr + 128, :],
                           "ctx_send", lo, lo + 128 * 1024 * 2, BF16),
                        cb)
        if hl % 4 == 3:
            g = hl // 4
            snd = BV(ctx_send.ap[g], "ctx_send", g * 2 * 512 * 1024 * 2,
                     (g + 1) * 2 * 512 * 1024 * 2, BF16)
            agv = BV(ctx_ag.ap[g], "ctx_ag", g * 4 * 512 * 1024 * 2,
                     (g + 1) * 4 * 512 * 1024 * 2, BF16)
            tr.op("gpsimd",
                  lambda e, s=snd.ap, o=agv.ap: e.collective_compute(
                      "AllGather", OP.bypass,
                      replica_groups=[[0, 1], [2, 3], [4, 5], [6, 7]],
                      ins=[s], outs=[o]),
                  reads=[snd], writes=[agv], kind="cc")

    arena.pop()

    mq4 = [arena.alloc(512, F32) for _ in range(4)]
    for _mx in mq4:
        tr.op("vector", lambda e, o=_mx.ap: e.memset(o, 0.0), reads=[],
              writes=[_mx])
    arena.push()
    agA = [arena.alloc(1024, BF16) for _ in range(2)]
    agB = [arena.alloc(1024, BF16) for _ in range(2)]
    tsel = arena.alloc(1024, BF16)
    ow_sb = [arena.alloc(4096, BF16) for _ in range(2)]
    xo_t = [arena.alloc(512, F32) for _ in range(4)]
    x2a = [arena.alloc(512, F32) for _ in range(2)]
    x2b = [arena.alloc(512, F32) for _ in range(2)]
    x2sq = [arena.alloc(512, BF16) for _ in range(2)]
    A2 = arena.alloc(1024, F32)
    B2 = arena.alloc(1024, F32)
    m2t = arena.alloc(512, F32)
    t12 = arena.alloc(512, F32)
    var2 = arena.alloc(512, F32)
    sq2 = var2
    h2t1 = [arena.alloc(512, F32) for _ in range(2)]
    xg_d = x2a + x2b

    def ctx_own_c(kc):
        return h_all.c(1024 * kc, 1024 * kc + 1024)

    def h2c(tb, m):
        base = 32768 + 16384 * tb + 512 * m
        return h_all.c(base, base + 512)

    dmac(ow_sb[0], ow[0])
    dmac(ow_sb[1], ow[1])

    for kc in range(KC):
        r, fr = kc // 16, kc % 16
        gg, rr = fr // 4, fr % 4
        a, b_ = agA[kc % 2], agB[kc % 2]
        base = ((((gg * 2 + r) * 2 + 0) * 512) + 128 * rr) * 1024 * 2
        dma("sync", a, BV(ctx_ag.ap[gg, r, 0, 128 * rr:128 * rr + 128, :],
                          "ctx_ag", base, base + 128 * 1024 * 2, BF16))
        base1 = ((((gg * 2 + r) * 2 + 1) * 512) + 128 * rr) * 1024 * 2
        dma("sync", b_, BV(ctx_ag.ap[gg, r, 1, 128 * rr:128 * rr + 128, :],
                           "ctx_ag", base1, base1 + 128 * 1024 * 2, BF16))
        ts(tsel, b_, sel1_sb, None, OP.mult)
        stt(ctx_own_c(kc), a, sel0_sb, tsel, OP.mult, OP.add)

    for m0 in range(2):
        for tb in range(2):
            dma("sync", xo_t[(2 * m0 + tb) % 4],
                xT_own[128 * m0:128 * m0 + 128, 512 * tb:512 * tb + 512])
    for m in range(KC):
        wbuf = ow_sb[m % 2]
        for tb in range(2):
            bank = pb[(2 * m + tb) % 4]
            for kc in range(KC):
                mm(wbuf.c(128 * kc, 128 * kc + 128),
                   ctx_own_c(kc).c(512 * tb, 512 * tb + 512),
                   bank, kc == 0, kc == KC - 1)
            xo = xo_t[(2 * m + tb) % 4]
            i2 = (2 * m + tb) % 2
            act(x2a[i2], bank, AF.Identity, bias=gob.c(m, m + 1),
                scale=ga_e.c(m, m + 1))
            tt(x2b[i2], x2a[i2], xo, OP.add)
            lo = (m * 128 * 1024 + 512 * tb) * 4
            dma("sync", BV(x2_dram.ap[m, :, 512 * tb:512 * tb + 512], "x2_dram",
                           lo, lo + 512 * 4, F32), x2b[i2])
            cp(h2c(tb, m), x2b[i2])
            act(x2sq[i2], x2b[i2], AF.Square)
            mm(onb_sb, h2c(tb, m), pb[4 + tb], m == 0, m == KC - 1)
            mm(onb_sb, x2sq[i2], pb[6 + tb], m == 0, m == KC - 1)
            if m + 2 < KC:
                dma("sync", xo_t[(2 * m + tb) % 4],
                    xT_own[128 * (m + 2):128 * (m + 2) + 128,
                           512 * tb:512 * tb + 512])
        if m + 2 < KC:
            dmac(ow_sb[m % 2], ow[m + 2])
    for tb in range(2):
        A2tb, B2tb = A2.c(512 * tb, 512 * tb + 512), B2.c(512 * tb, 512 * tb + 512)
        ts(m2t, pb[4 + tb], 1.0 / 4096.0, None, OP.mult)
        tt(t12, m2t, m2t, OP.mult)
        stt(var2, pb[6 + tb], 1.0 / 4096.0, t12, OP.mult, OP.subtract)
        act(sq2, var2, AF.Sqrt, bias=eps_sb)
        recip(A2tb, sq2)
        tt(B2tb, m2t, A2tb, OP.mult)
    for tb in range(2):
        for m in range(KC):
            slot, t1 = h2c(tb, m), h2t1[m % 2]
            te = "gpsimd" if tb else "vector"
            tt(t1, slot, A2.c(512 * tb, 512 * tb + 512), OP.mult, eng=te)
            tt(t1, t1, B2.c(512 * tb, 512 * tb + 512), OP.subtract, eng=te)
            act(slot, t1, AF.Identity, bias=modT.c(96 + m, 96 + m + 1),
                scale=c1b.c(m, m + 1))
            for q, ivv in ((0, ivg_sb), (1, ivu_sb)):
                mx = mq4[2 * q + tb]
                xb = xg_d[(2 * m + q) % 4]
                act(xb, slot, AF.Abs, scale=ivv.c(m, m + 1))
                tt(mx, mx, xb, OP.max)
    arena.pop()

    arena.push()
    xsd = [arena.alloc(512, F32) for _ in range(2)]
    rmaxq = arena.alloc(8, F32)
    srow = arena.alloc(512, F32)
    scb_g = [arena.alloc(512, F32) for _ in range(2)]
    scb_u = [arena.alloc(512, F32) for _ in range(2)]
    rcb = [arena.alloc(512, F32) for _ in range(2)]
    qint = [arena.alloc(512, I32) for _ in range(3)]
    wgr = [arena.alloc(1024, BF16) for _ in range(4)]
    wur = [arena.alloc(1024, BF16) for _ in range(4)]
    wdr = [arena.alloc(1024, BF16) for _ in range(4)]
    g_t = arena.alloc(512, F32)
    s_t = arena.alloc(512, F32)
    u_t = arena.alloc(512, F32)
    ib_t = [arena.alloc(512, BF16) for _ in range(2)]
    d_t = g_t
    y_tb = [s_t, u_t]
    x2r2 = xsd + [arena.alloc(512, F32) for _ in range(2)]

    def qgu_c(q, tb, kc):
        base = (16384 * q if tb == 0 else 32768 + 16384 * q)
        return h_all.c(base + 512 * kc, base + 512 * kc + 512)

    def rb_c(tb, kc):
        base = 32768 + 8192 * tb
        return h_all.c(base + 512 * kc, base + 512 * kc + 512)

    def qi_c(tb, kc):
        base = 8192 * tb
        return h_all.c(base + 512 * kc, base + 512 * kc + 512)

    def idchunk(tb, im):
        lo = (tb * IKC + im) * 128 * 512 * 2
        return BV(inter_dram[tb, im], "inter_dram", lo, lo + 128 * 512 * 2, BF16)

    def mkscales(mx, scb, rc):
        for j in range(4):
            bank = pb[4 + j % 2]
            mm(mx.c(128 * j, 128 * j + 128), idf_sb, bank.c(0, 128),
               True, True, transpose=True)
            red(rmaxq.c(j, j + 1), bank.c(0, 128), OP.max)
        for j in range(4):
            mm(rmaxq.c(j, j + 1), idf_sb,
               pb[6].p(0, 1).c(128 * j, 128 * j + 128), True, True,
               transpose=True)
        cp(srow.p(0, 1), pb[6].p(0, 1).c(0, 512))
        ts(srow.p(0, 1), srow.p(0, 1), 1.0 / 127.0, 1e-8, OP.mult, OP.max)
        mm(onf_sb, srow.p(0, 1), pb[5], True, True)
        cp(scb, pb[5])
        recip(rc, scb)

    def quant_gu(tb):
        for q, (mx, scb, ivv) in enumerate(((mq4[tb], scb_g[tb], ivg_sb),
                                            (mq4[2 + tb], scb_u[tb], ivu_sb))):
            mkscales(mx, scb, rcb[q])
            for kc in range(KC):
                qb = qint[kc % 3]
                stt(qb, h2c(tb, kc), ivv.c(kc, kc + 1), rcb[q],
                    OP.mult, OP.mult)
                dma("gpsimd", qgu_c(q, tb, kc), qb)

    ens_g = mkstream(wgr, lambda i: wg[i // 4][:, 1024 * (i % 4):
                                                1024 * (i % 4) + 1024], 64)
    ens_u = mkstream(wur, lambda i: wu[i // 4][:, 1024 * (i % 4):
                                               1024 * (i % 4) + 1024], 64)
    ens_d = mkstream(wdr, lambda i: wd[i // 2][:, 1024 * (i % 2):
                                               1024 * (i % 2) + 1024], 64)

    quant_gu(0)
    quant_gu(1)

    ens_g(3)
    ens_u(3)
    for im in range(IKC):
        bks = [pb[4 * (im % 2) + j] for j in range(4)]
        for kc in range(KC):
            gs = im * 4 + kc // 8
            if kc % 8 == 0:
                ens_g(gs + 3)
                ens_u(gs + 3)
            wgs, wus = wgr[gs % 4], wur[gs % 4]
            col = 128 * (kc % 8)
            mm(wgs.c(col, col + 128), qgu_c(0, 0, kc), bks[0], kc == 0,
               kc == KC - 1)
            mm(wgs.c(col, col + 128), qgu_c(0, 1, kc), bks[1], kc == 0,
               kc == KC - 1)
            mm(wus.c(col, col + 128), qgu_c(1, 0, kc), bks[2], kc == 0,
               kc == KC - 1)
            mm(wus.c(col, col + 128), qgu_c(1, 1, kc), bks[3], kc == 0,
               kc == KC - 1)
        for tb, mxi in ((0, mq4[0]), (1, mq4[1])):
            stt(g_t, bks[tb], swg_sb.c(im, im + 1), scb_g[tb],
                OP.mult, OP.mult)
            act(s_t, g_t, AF.Silu)
            stt(u_t, bks[2 + tb], swu_sb.c(im, im + 1), scb_u[tb],
                OP.mult, OP.mult)
            tt(u_t, s_t, u_t, OP.mult)
            ib = ib_t[tb]
            ts(ib, u_t, 10.0, -10.0, OP.min, OP.max)
            xb = xsd[tb]
            act(xb, ib, AF.Abs, scale=ivi_sb.c(im, im + 1))
            if im == 0:
                cp(mxi, xb)
            else:
                tt(mxi, mxi, xb, OP.max)
            dma("scalar", idchunk(tb, im), ib)

    scb_i = scb_g
    for tb, mxi in ((0, mq4[0]), (1, mq4[1])):
        mkscales(mxi, scb_i[tb], rcb[tb])
    for tb in range(2):
        for kc in range(IKC):
            rb = rb_c(tb, kc)
            dma("sync", rb, idchunk(tb, kc))
            qb = qint[kc % 3]
            stt(qb, rb, ivi_sb.c(kc, kc + 1), rcb[tb], OP.mult, OP.mult)
            dma("gpsimd", qi_c(tb, kc), qb)

    ens_d(3)
    for m0 in range(2):
        for tb in range(2):
            lo = (m0 * 128 * 1024 + 512 * tb) * 4
            dma("sync", x2r2[(2 * m0 + tb) % 4],
                BV(x2_dram.ap[m0, :, 512 * tb:512 * tb + 512],
                   "x2_dram", lo, lo + 512 * 4, F32))
    for m in range(KC):
        bd = [pb[2 * (m % 2)], pb[2 * (m % 2) + 1]]
        for kci in range(IKC):
            ds = m * 2 + kci // 8
            if kci % 8 == 0:
                ens_d(ds + 3)
            wds = wdr[ds % 4]
            col = 128 * (kci % 8)
            mm(wds.c(col, col + 128), qi_c(0, kci), bd[0], kci == 0,
               kci == IKC - 1)
            mm(wds.c(col, col + 128), qi_c(1, kci), bd[1], kci == 0,
               kci == IKC - 1)
        for tb in range(2):
            xb = x2r2[(2 * m + tb) % 4]
            lo = (m * 128 * 1024 + 512 * tb) * 4
            stt(d_t, bd[tb], gbswd.c(m, m + 1), scb_i[tb], OP.mult, OP.mult)
            yb = y_tb[tb]
            tt(yb, d_t, xb, OP.add)
            dma("sync", BV(yT_my[128 * m:128 * m + 128,
                                 512 * tb:512 * tb + 512],
                           "y_out", lo, lo + 512 * 4, F32), yb)
            if m + 2 < KC:
                lo2 = ((m + 2) * 128 * 1024 + 512 * tb) * 4
                dma("sync", x2r2[(2 * m + tb) % 4],
                    BV(x2_dram.ap[m + 2, :, 512 * tb:512 * tb + 512],
                       "x2_dram", lo2, lo2 + 512 * 4, F32))
    arena.pop()
    tr.wait_all("sync", [y_out])

    tr.finalize()
    with nc.Block() as block:
        @block.sync
        def _(e):
            for fn in tr.q["sync"]:
                fn(e)

        @block.gpsimd
        def _(e):
            for fn in tr.q["gpsimd"]:
                fn(e)

        @block.scalar
        def _(e):
            for fn in tr.q["scalar"]:
                fn(e)

        @block.vector
        def _(e):
            for fn in tr.q["vector"]:
                fn(e)

        @block.tensor
        def _(e):
            for fn in tr.q["tensor"]:
                fn(e)

    st.close()
    return nc


def make_inputs(core, inp):
    b, half = core // 2, core % 2
    lo = 1024 * half
    f32 = np.float32
    x = np.asarray(inp["L0_attn_hidden_states"], f32)
    qkv_w = np.asarray(inp["L0_attn_qkv_w"], f32)
    qkv_b = np.asarray(inp["L0_attn_qkv_b"], f32)
    out_w = np.asarray(inp["L0_attn_out_w"], f32)
    out_b = np.asarray(inp["L0_attn_out_b"], f32)
    wmod = np.asarray(inp["L0_wmod"], f32)
    cc = np.asarray(inp["L0_c"], f32)

    d = {}
    d["xT_my"] = np.ascontiguousarray(x[b].T)
    d["xT_own"] = np.ascontiguousarray(x[b, lo:lo + 1024].T)
    heads = []
    for hl in range(NHC):
        hg = 16 * half + hl
        for t in range(3):
            heads.append(qkv_w[:, E * t + 128 * hg:E * t + 128 * hg + 128]
                         .reshape(KC, 128, 128))
    d["wqkv"] = np.ascontiguousarray(
        np.stack(heads).transpose(0, 2, 1, 3)).reshape(NHC, 3, 128, KC * 128)
    bq = np.zeros((128, 48), f32)
    for hl in range(NHC):
        hg = 16 * half + hl
        bq[:, hl] = qkv_b[128 * hg:128 * hg + 128]
        bq[:, 16 + hl] = qkv_b[E + 128 * hg:E + 128 * hg + 128]
        bq[:, 32 + hl] = qkv_b[2 * E + 128 * hg:2 * E + 128 * hg + 128]
    d["bqkv"] = bq
    d["wmod_my"] = np.ascontiguousarray(
        wmod[512 * core:512 * core + 512, :].reshape(4, 128, 24, 1024)
        .transpose(1, 2, 0, 3)).reshape(128, 24 * 4096)
    d["c_my"] = np.ascontiguousarray(cc[:, 512 * core:512 * core + 512])
    e = np.zeros((4, 1), f32)
    e[b, 0] = 1.0
    d["e_b"] = e
    d["sel0"] = np.full((128, 1), 1.0 if half == 0 else 0.0, f32)
    d["sel1"] = np.full((128, 1), 1.0 if half == 1 else 0.0, f32)
    d["ow"] = np.ascontiguousarray(
        out_w.reshape(KC, 128, KC, 128).transpose(2, 1, 0, 3)).reshape(
        KC, 128, KC * 128)
    d["obT"] = np.ascontiguousarray(out_b.reshape(32, 128).T)
    wgm = np.asarray(inp["L0_ffn_w_gate"], f32)
    wum = np.asarray(inp["L0_ffn_w_up"], f32)
    wdm = np.asarray(inp["L0_ffn_w_down"], f32)
    d["wg"] = np.ascontiguousarray(
        wgm.T.reshape(KC, 128, IKC, 128).transpose(2, 1, 0, 3)).reshape(
        IKC, 128, KC * 128)
    d["wu"] = np.ascontiguousarray(
        wum.T.reshape(KC, 128, IKC, 128).transpose(2, 1, 0, 3)).reshape(
        IKC, 128, KC * 128)
    d["wd"] = np.ascontiguousarray(
        wdm.T.reshape(IKC, 128, KC, 128).transpose(2, 1, 0, 3)).reshape(
        KC, 128, IKC * 128)
    d["invgT"] = np.ascontiguousarray(
        np.asarray(inp["L0_ffn_inv_gate"], f32).reshape(KC, 128).T)
    d["invuT"] = np.ascontiguousarray(
        np.asarray(inp["L0_ffn_inv_up"], f32).reshape(KC, 128).T)
    d["inviT"] = np.ascontiguousarray(
        np.asarray(inp["L0_ffn_inv_inter"], f32).reshape(IKC, 128).T)
    d["swgT"] = np.ascontiguousarray(
        np.asarray(inp["L0_ffn_s_wgate"], f32).reshape(IKC, 128).T)
    d["swuT"] = np.ascontiguousarray(
        np.asarray(inp["L0_ffn_s_wup"], f32).reshape(IKC, 128).T)
    d["swdT"] = np.ascontiguousarray(
        np.asarray(inp["L0_ffn_s_wdown"], f32).reshape(KC, 128).T)
    return d


def make_consts():
    f32 = np.float32
    j = np.arange(64, dtype=np.float64)
    invf = 1.0 / (10000.0 ** (j / 64.0))
    t = np.arange(S, dtype=np.float64)
    ang = np.outer(invf, t)
    cos = np.cos(ang)
    sin = np.sin(ang)
    cos2 = np.concatenate([cos, cos], 0).astype(f32)
    sin2s = np.concatenate([-sin, sin], 0).astype(f32)
    tri = (np.arange(128)[:, None] <= np.arange(128)[None, :])
    psw = np.zeros((128, 128), np.float32)
    for i in range(128):
        psw[(i + 64) % 128, i] = 1.0
    return {"cos2": cos2, "sin2s": sin2s,
            "trimask": tri.astype(ml_dtypes.bfloat16),
            "pswap": psw.astype(ml_dtypes.bfloat16),
            "ident_bf": np.eye(128).astype(ml_dtypes.bfloat16),
            "ident_f": np.eye(128, dtype=f32),
            "ones_bf": np.ones((128, 128), ml_dtypes.bfloat16),
            "ones_f": np.ones((1, 128), f32)}


_CACHE = {}


def kernel(**inputs):
    if "nc" not in _CACHE:
        _CACHE["nc"] = build_program()
    nc = _CACHE["nc"]
    consts = make_consts()
    in_maps = []
    for c in range(8):
        m = make_inputs(c, inputs)
        m.update(consts)
        in_maps.append(m)
    res = run_bass_kernel_spmd(nc, in_maps, core_ids=list(range(8)))
    y = np.empty((B, S, E), np.float32)
    for c in range(8):
        b, half = c // 2, c % 2
        lo = 1024 * half
        y[b, lo:lo + 1024] = res.results[c]["yT_my"].T
    return y
